# version 02 (final)
# baseline (speedup 1.0000x reference)
import contextlib
import os
import numpy as np
import ml_dtypes
import concourse.bass as bass
import concourse.mybir as mybir
from concourse.bass_utils import run_bass_kernel_spmd

F32 = mybir.dt.float32
BF16 = mybir.dt.bfloat16
U8 = mybir.dt.uint8
AF = mybir.ActivationFunctionType
ALU = mybir.AluOpType
AX = mybir.AxisListType.X
BF = ml_dtypes.bfloat16

W = 8
T = 1024
D = 4096
DI = 8192
NH = 128
PD = 64
NG = 8
NS = 128
CH = 256
NCH = T // CH
CONV_DIM = DI + 2 * NG * NS
DIN = 2 * DI + 2 * NG * NS + NH
EPS = 1e-5

ENGINES = ("sync", "scalar", "vector", "tensor", "gpsimd")


class PB:
    def __init__(self, nc):
        self.nc = nc
        self.q = {e: [] for e in ENGINES}
        self.sem = {}
        self.cnt = {}
        self._maxwait = {}

    def new_sem(self, handle, name):
        self.sem[name] = handle
        self.cnt[name] = 0
        self._maxwait[name] = 0

    def emit(self, eng, fn, wait=(), inc=None):
        wait = tuple((s, v) for (s, v) in wait if v and v > 0)
        for s, v in wait:
            self._maxwait[s] = max(self._maxwait[s], v)
        tgt = None
        if inc is not None:
            s, d = inc
            self.cnt[s] += d
            tgt = self.cnt[s]

        def closure(e, wait=wait, fn=fn, inc=inc):
            for s, v in wait:
                e.wait_ge(self.sem[s], v)
            inst = fn(e) if fn is not None else None
            if inc is not None:
                assert inst is not None
                inst.then_inc(self.sem[inc[0]], inc[1])
        self.q[eng].append(closure)
        return tgt

    def wait(self, eng, wait):
        self.emit(eng, None, wait=wait)

    def dma(self, eng, out, in_, sem="ld", wait=()):
        return self.emit(eng, lambda e, o=out, i=in_: e.dma_start(out=o, in_=i),
                         wait=wait, inc=(sem, 16))

    def dmat(self, eng, out, in_, sem="trd", wait=()):
        return self.emit(eng,
                         lambda e, o=out, i=in_: e.dma_start(out=o, in_=i, transpose=True),
                         wait=wait, inc=(sem, 16))

    def validate(self):
        for s, v in self._maxwait.items():
            assert v <= self.cnt[s], f"wait {s}>={v} but final {self.cnt[s]}"

    def finish(self):
        self.wait("sync", [("st", self.cnt["st"]), ("stg", self.cnt["stg"]),
                           ("cast", self.cnt["cast"])])
        self.validate()

    def run(self, block):
        for eng in ENGINES:
            ops = self.q[eng]

            def body(e, ops=ops):
                for f in ops:
                    f(e)
            getattr(block, eng)(body)


def ts(i, n):
    return slice(i * n, (i + 1) * n)


def make_consts():
    m = np.zeros((256, 256), dtype=BF)
    sg, lg = np.meshgrid(np.arange(256), np.arange(256), indexing="ij")
    m[sg <= lg] = 1
    return {
        "c_ident_bf": np.eye(128, dtype=BF),
        "c_ident_f32": np.eye(128, dtype=np.float32),
        "c_ones_f32": np.ones((1, 128), dtype=np.float32),
        "c_tril": m,
        "c_zeros": np.zeros((1, 8192), dtype=np.float32),
        "c_eps": np.full((1, 1), 1e-5, dtype=np.float32),
        "c_ones_bf": np.ones((1, 128), dtype=BF),
        "c_mask128": np.where(np.arange(128)[:, None] >= np.arange(128)[None, :],
                              0.0, -1e9).astype(np.float32),
        "c_ninf": np.full((1, 64), -1e9, dtype=np.float32),
        "c_eexp": np.kron(np.eye(64), np.ones((1, 8))).astype(BF),
    }


CONST_SPECS = [
    ("c_ident_bf", [128, 128], BF16),
    ("c_ident_f32", [128, 128], F32),
    ("c_ones_f32", [1, 128], F32),
    ("c_tril", [256, 256], BF16),
    ("c_zeros", [1, 8192], F32),
    ("c_eps", [1, 1], F32),
    ("c_ones_bf", [1, 128], BF16),
    ("c_mask128", [128, 128], F32),
    ("c_ninf", [1, 64], F32),
    ("c_eexp", [64, 512], BF16),
]


def build_launch1(upto="all"):
    nc = bass.Bass("TRN2", target_bir_lowering=False)
    dI = lambda n, s, d=F32: nc.dram_tensor(n, s, d, kind="ExternalInput").ap()
    dO = lambda n, s, d=F32: nc.dram_tensor(n, s, d, kind="ExternalOutput").ap()
    dT = lambda n, s, d=BF16: nc.dram_tensor(n, s, d, kind="Internal").ap()

    x_main = dI("x_main", [T, D])
    x_halo3 = dI("x_halo3", [3, D])
    w_in = dI("w_in", [DIN, D])
    conv_w0 = dI("conv_w0", [CONV_DIM, 4])
    conv_b0 = dI("conv_b0", [128, 80])
    dt_bias = dI("dt_bias", [NH])
    a_log = dI("a_log", [NH])
    d_vec = dI("d_vec", [NH])
    ffn_w = dI("ffn_w", [D, D])
    ffn_b = dI("ffn_b", [D])
    ffn_a = dI("ffn_a", [D, 32])
    limes = dI("limes", [8, 32])
    cst = {n: dI(n, s, d) for n, s, d in CONST_SPECS}

    o_xn = dO("o_xn", [T, D], BF16)
    o_xbc = dO("o_xbc", [CONV_DIM, T], BF16)
    o_z = dO("o_z", [T, DI], BF16)
    o_dt = dO("o_dt", [NH, T])
    o_acs = dO("o_acs", [NH, T])
    o_acst = dO("o_acst", [T, NH])
    o_y1 = dO("o_y1", [T, DI], BF16)
    o_s = dO("o_s", [NCH * NS, DI], BF16)
    o_f = dO("o_f", [NS, DI])
    o_asum = dO("o_asum", [NH, 1])
    o_locmax = dO("o_locmax", [1, 2])
    o_h = dO("o_h", [T, D], BF16)
    o_hst = dO("o_hst", [8, 512])
    o_dst = dO("o_dst", [8, 512])
    o_zffn = dO("o_zffn", [32, T])

    o_dbg1 = dO("o_dbg1", [128, 16384], BF16)
    o_dbg2 = dO("o_dbg2", [128, 2048], BF16)
    o_dbg3 = dO("o_dbg3", [128, 256])
    w_in_bf = dT("w_in_bf", [DIN, D])
    ffnw_bf = dT("ffnw_bf", [D, D])
    pre_dram = dT("pre_dram", [CONV_DIM, T])
    halo_dram = dT("halo_dram", [4, CONV_DIM])

    with contextlib.ExitStack() as xs:
        E = xs.enter_context
        xnt_t = E(nc.sbuf_tensor("xnt_sb", [128, 32 * 1024], BF16))
        wt_t = E(nc.sbuf_tensor("wt_sb", [128, 32 * 1024], BF16))
        mb_t = E(nc.sbuf_tensor("miscb_sb", [128, 16 * 1024], BF16))
        fa_t = E(nc.sbuf_tensor("f32_sb", [128, 12160], F32))
        pA = E(nc.psum_tensor("pA", [128, 512], F32))
        pB = E(nc.psum_tensor("pB", [128, 512], F32))
        pT = E(nc.psum_tensor("pT", [128, 256], BF16))
        pR = E(nc.psum_tensor("pR", [128, 512], F32))
        pY0 = E(nc.psum_tensor("pY0", [128, 512], F32))
        pY1 = E(nc.psum_tensor("pY1", [128, 512], F32))
        pS = E(nc.psum_tensor("pS", [128, 512], F32))
        pH = E(nc.psum_tensor("pH", [128, 512], F32))
        block = E(nc.Block())
        pb = PB(nc)
        for s in ["cast", "trd", "tr2", "ld", "pe", "ve", "ac", "gp", "st", "ts", "stg", "xa", "xb"]:
            pb.new_sem(E(nc.semaphore("sem_" + s)), s)

        xnT = xnt_t[:]
        wt0 = wt_t[:, 0:16384]
        wt1 = wt_t[:, 16384:32768]
        wts = [wt0, wt1]
        mb = mb_t[:]
        fa = fa_t[:]

        ident_bf = mb[:, 0:128]
        tril = mb[:, 128:640]
        xnT_halo = mb[:, 640:768]
        xntok = mb[:, 768:4864]
        evb = mb[:, 4864:6912]
        halo_sm = mb[:, 6912:7424]
        pre_t = mb[:, 7424:9536]
        hls = mb[:, 9536:9792]
        zffn_bf = mb[:, 9792:10816]
        ffna_sb = mb[:, 10816:11840]
        limes_bf = mb[:, 11840:11872]
        limesT_bf = mb[:, 11872:11880]
        bexp_bf = mb[:, 11904:16000]
        ident_f = fa[:, 0:128]
        ones_f = fa[:, 128:256]
        zrow = fa[:, 256:512]
        smv = fa[:, 512:544]
        convw = fa[:, 544:864]
        convb = fa[:, 864:944]
        dbc = fa[:, 944:1072]
        stat = fa[:, 1072:1136]
        dtf = fa[:, 1136:2160]
        acs = fa[:, 2160:3184]
        acst = fa[:, 3184:4208]
        htile = fa[:, 4208:4720]
        hst = fa[:, 4720:5232]
        dst_sl = fa[:, 5232:5744]
        zffn = fa[:, 5744:6768]
        dsT = fa[:, 6768:7024]
        dtT = fa[:, 7024:7280]
        nacst = fa[:, 7280:7536]
        rowf = fa[:, 7536:7792]
        eacol = fa[:, 7792:7920]
        earow = fa[:, 7920:8048]
        bcol8 = fa[:, 8048:8056]
        xld = fa[:, 8064:10112]
        sqs = fa[:, 10112:12160]
        dtraw = xld[:, 0:1024]
        daf = xld[:, 1024:2048]
        cacc = sqs[:, 0:1024]


        def SC(i, rows=128):
            return stat[0:rows, 4 * i:4 * i + 1]

        def SM(i):
            return smv[:, 4 * i:4 * i + 1]

        ld0 = [
            pb.dma("sync", ident_bf, cst["c_ident_bf"]),
            pb.dma("sync", tril[:, 0:256], cst["c_tril"][0:128, :]),
            pb.dma("sync", tril[:, 256:512], cst["c_tril"][128:256, :]),
            pb.dma("sync", ident_f, cst["c_ident_f32"]),
            pb.dma("sync", ones_f[0:1, :], cst["c_ones_f32"]),
            pb.dma("sync", zrow, cst["c_zeros"][:, 0:256].broadcast_to([128, 256])),
            pb.dma("sync", SM(0), dt_bias[:, None]),
            pb.dma("sync", SM(1), a_log[:, None]),
            pb.dma("sync", convw.rearrange("p (n k) -> p n k", n=80),
                   conv_w0.rearrange("(n p) k -> p n k", p=128)),
            pb.dma("sync", convb, conv_b0[:, :]),
            pb.dma("gpsimd", bexp_bf, ffn_b[None, :].broadcast_to([128, 4096]), sem="cast"),
            pb.dma("sync", dbc, d_vec[None, :].broadcast_to([128, 128])),
            pb.dma("sync", stat[:, 56:57], cst["c_eps"][0:1, 0:1].broadcast_to([128, 1])),
        ]
        lds = max(ld0)

        blk_rows = [(18432, 18560)] \
            + [(8192 + 512 * i, 8704 + 512 * i) for i in range(20)] \
            + [(512 * i, 512 * i + 512) for i in range(16)]
        cast_at = []
        ffn_cast_at = []
        if upto not in ("ln0", "ln"):
            cast_at = [pb.dma("gpsimd", w_in_bf[r0:r1, :], w_in[r0:r1, :], sem="cast")
                       for (r0, r1) in blk_rows]
            ffn_cast_at = [pb.dma("gpsimd", ffnw_bf[ts(b, 512), :], ffn_w[ts(b, 512), :],
                                  sem="cast") for b in range(8)]

        def emit_ln(rows, src, tile_idx, halo):
            a1 = v1 = None
            for hh in range(2):
                ldx = pb.dma("sync", xld[0:rows, :], src[:, ts(hh, 2048)],
                             wait=[("ac", a1)] if a1 else [("ve", ln_prev_v4[0])])
                v1 = pb.emit("vector", lambda e, hh=hh: e.tensor_reduce(
                    SC(8 + hh, rows), xld[0:rows, :], AX, ALU.add),
                    wait=[("ld", ldx)], inc=("ve", 1))
                a1 = pb.emit("scalar", lambda e, hh=hh: e.activation(
                    sqs[0:rows, :], xld[0:rows, :], AF.Square,
                    accum_out=SC(10 + hh, rows)),
                    wait=[("ld", ldx), ("ve", v1)], inc=("ac", 1))

            s_sum = pb.emit("scalar", lambda e: e.add(SC(13, rows), SC(8, rows), SC(9, rows)),
                            wait=[("ve", v1)], inc=("ac", 1))
            p_s2 = pb.emit("gpsimd", lambda e: e.tensor_tensor(SC(12, rows), SC(10, rows),
                                                               SC(11, rows), ALU.add),
                           wait=[("ac", a1)], inc=("gp", 1))
            v_mean = pb.emit("vector", lambda e: e.tensor_scalar(SC(2, rows), SC(13, rows),
                                                                 1.0 / D, None, ALU.mult),
                             wait=[("ac", s_sum)], inc=("ve", 1))
            a_m2 = pb.emit("scalar", lambda e: e.activation(SC(3, rows), SC(2, rows), AF.Square),
                           wait=[("ve", v_mean)], inc=("ac", 1))
            v2 = pb.emit("vector", lambda e: e.tensor_scalar(SC(5, rows), SC(12, rows),
                                                             1.0 / D, SC(3, rows),
                                                             ALU.mult, ALU.subtract),
                         wait=[("gp", p_s2), ("ac", a_m2)], inc=("ve", 1))
            a2 = pb.emit("scalar", lambda e: (
                e.activation(SC(6, rows), SC(5, rows), AF.Ln, bias=stat[0:rows, 56:57]),
                e.activation(SC(7, rows), SC(6, rows), AF.Exp, scale=-0.5))[-1],
                wait=[("ve", v2)], inc=("ac", 1))
            v3 = pb.emit("vector",
                         lambda e: e.tensor_scalar(xntok[0:rows, 2048:4096], xld[0:rows, :],
                                                   SC(2, rows), SC(7, rows),
                                                   ALU.subtract, ALU.mult),
                         wait=[("ac", a2), ("st", xn_st[0])], inc=("ve", 1))
            ldx0 = pb.dma("sync", xld[0:rows, :], src[:, 0:2048], wait=[("ve", v3)])
            v4 = pb.emit("vector",
                         lambda e: e.tensor_scalar(xntok[0:rows, 0:2048], xld[0:rows, :],
                                                   SC(2, rows), SC(7, rows),
                                                   ALU.subtract, ALU.mult),
                         wait=[("ld", ldx0)], inc=("ve", 1))
            ln_prev_v4[0] = v4
            if not halo:
                xn_st[0] = pb.dma("sync", o_xn[ts(tile_idx, 128), :], xntok[:], sem="st",
                                  wait=[("ve", v4)])
            for c2 in range(16):
                def tr(e, c2=c2):
                    e.transpose(pT[:, 0:rows], xntok[0:rows, ts(2 * c2, 128)],
                                ident_bf[0:rows, 0:rows])
                    return e.transpose(pT[:, 128:128 + rows], xntok[0:rows, ts(2 * c2 + 1, 128)],
                                       ident_bf[0:rows, 0:rows])
                pv = pb.emit("tensor", tr, wait=[("ve", v4)], inc=("pe", 1))
                if not halo:
                    def ev(e, c2=c2, ti=tile_idx):
                        e.tensor_copy(
                            xnT[:, (2 * c2) * 1024 + ti * 128:(2 * c2) * 1024 + ti * 128 + 128],
                            pT[:, 0:128])
                        return e.tensor_copy(
                            xnT[:, (2 * c2 + 1) * 1024 + ti * 128:(2 * c2 + 1) * 1024 + ti * 128 + 128],
                            pT[:, 128:256])
                else:
                    def ev(e, c2=c2):
                        e.tensor_copy(xnT_halo[:, (2 * c2) * 4:(2 * c2) * 4 + 3], pT[:, 0:3])
                        return e.tensor_copy(xnT_halo[:, (2 * c2 + 1) * 4:(2 * c2 + 1) * 4 + 3],
                                             pT[:, 128:131])
                evv = pb.emit("vector", ev, wait=[("pe", pv)], inc=("ve", 1))
                pb.wait("tensor", [("ve", evv)])

        pb.wait("tensor", [("ld", lds)])
        pb.wait("vector", [("ld", lds)])
        xn_st = {0: 0}
        ln_prev_v4 = {0: 0}
        for t_i in range(1 if upto == "ln0" else 8):
            emit_ln(128, x_main[ts(t_i, 128), :], t_i, False)
        if upto != "ln0":
            emit_ln(3, x_halo3[:], 0, True)
        xnT_rdy = pb.cnt["ve"]
        st_after_ln = pb.cnt["st"]

        if upto in ("ln", "ln0"):
            pb.finish()
            pb.run(block)
            return nc

        pe_done = {}
        ev_last = {0: 0, 1: 0}
        halo_last_ev = 0
        halo_last_pe = 0
        dt_ev = None

        def inproj_block(bi):
            nonlocal halo_last_ev, halo_last_pe, dt_ev
            r0, r1 = blk_rows[bi]
            osz = r1 - r0
            wt = wts[bi % 2]
            wait_tr = [("cast", cast_at[bi])]
            if bi >= 2:
                wait_tr.append(("pe", pe_done[bi - 2]))
            tsem = "trd" if bi % 2 == 0 else "tr2"
            trd_v = None
            for k in range(32):
                trd_v = pb.dmat("sync", wt[:, k * 512:k * 512 + osz],
                                w_in_bf[r0:r1, ts(k, 128)], sem=tsem,
                                wait=wait_tr if k == 0 else ())
            pb.wait("tensor", [(tsem, trd_v), ("ve", xnT_rdy)])

            if bi == 0:
                for k in range(32):
                    pb.emit("tensor", lambda e, k=k: e.matmul(
                        pA[:], wt[:, k * 512:k * 512 + 128], xnT[:, k * 1024:k * 1024 + 512],
                        start=(k == 0), stop=(k == 31)))
                    pb.emit("tensor", lambda e, k=k: e.matmul(
                        pB[:], wt[:, k * 512:k * 512 + 128],
                        xnT[:, k * 1024 + 512:k * 1024 + 1024],
                        start=(k == 0), stop=(k == 31)),
                        inc=("pe", 1) if k == 31 else None)
                pe_done[bi] = pb.cnt["pe"]
                dt_ev = pb.emit("vector", lambda e: (
                    e.tensor_copy(dtraw[:, 0:512], pA[:]),
                    e.tensor_copy(dtraw[:, 512:1024], pB[:]))[-1],
                    wait=[("pe", pe_done[bi])], inc=("ve", 1))
                return

            if bi <= 20:
                for j in range(4):
                    buf = j % 2
                    pb.wait("tensor", [("ve", max(ev_last[0], ev_last[1]))])
                    for k in range(32):
                        pb.emit("tensor", lambda e, k=k, j=j: e.matmul(
                            pA[:], wt[:, k * 512 + j * 128:k * 512 + j * 128 + 128],
                            xnT[:, k * 1024:k * 1024 + 512], start=(k == 0), stop=(k == 31)))
                        pb.emit("tensor", lambda e, k=k, j=j: e.matmul(
                            pB[:], wt[:, k * 512 + j * 128:k * 512 + j * 128 + 128],
                            xnT[:, k * 1024 + 512:k * 1024 + 1024],
                            start=(k == 0), stop=(k == 31)),
                            inc=("pe", 1) if k == 31 else None)
                    mmd = pb.cnt["pe"]
                    dstb = evb[:, ts(buf, 1024)]
                    ev = pb.emit("vector", lambda e, dstb=dstb: (
                        e.tensor_copy(dstb[:, 0:512], pA[:]),
                        e.tensor_copy(dstb[:, 512:1024], pB[:]))[-1],
                        wait=[("pe", mmd), ("st", ev_st.get(buf, 0))], inc=("ve", 1))
                    ev_last[buf] = ev
                    row = r0 - 8192 + j * 128
                    ev_st[buf] = pb.dma("sync", pre_dram[row:row + 128, :], dstb, sem="st",
                                        wait=[("ve", ev)])
                pb.wait("tensor", [("ve", halo_last_ev)])
                for k in range(32):
                    pb.emit("tensor", lambda e, k=k: e.matmul(
                        pH[0:3, :], xnT_halo[:, k * 4:k * 4 + 3], wt[:, ts(k, 512)],
                        start=(k == 0), stop=(k == 31)),
                        inc=("pe", 1) if k == 31 else None)
                hv = pb.emit("vector", lambda e: e.tensor_copy(halo_sm[0:3, :], pH[0:3, :]),
                             wait=[("pe", pb.cnt["pe"]), ("st", halo_st.get(0, 0))], inc=("ve", 1))
                halo_last_ev = hv
                halo_st[0] = pb.dma("sync", halo_dram[0:3, r0 - 8192:r0 - 8192 + 512],
                                    halo_sm[0:3, :], sem="st", wait=[("ve", hv)])
            else:
                zi = bi - 21
                for t_i in range(8):
                    ps = pA if t_i % 2 == 0 else pB
                    buf = t_i % 2
                    pb.wait("tensor", [("ve", ev_last[buf])])
                    for k in range(32):
                        pb.emit("tensor", lambda e, k=k, t_i=t_i, ps=ps: e.matmul(
                            ps[:], xnT[:, k * 1024 + t_i * 128:k * 1024 + t_i * 128 + 128],
                            wt[:, ts(k, 512)], start=(k == 0), stop=(k == 31)),
                            inc=("pe", 1) if k == 31 else None)
                    mmd = pb.cnt["pe"]
                    dstb = evb[:, buf * 1024:buf * 1024 + 512]
                    ev = pb.emit("vector", lambda e, dstb=dstb, ps=ps: e.tensor_copy(dstb, ps[:]),
                                 wait=[("pe", mmd), ("st", ev_st.get(buf, 0))], inc=("ve", 1))
                    ev_last[buf] = ev
                    ev_st[buf] = pb.dma("sync", o_z[ts(t_i, 128), ts(zi, 512)], dstb, sem="st",
                                        wait=[("ve", ev)])
            pe_done[bi] = pb.cnt["pe"]

        ev_st = {}
        halo_st = {}
        for bi in range(37):
            inproj_block(bi)
        inproj_pe = pb.cnt["pe"]

        a_a = pb.emit("scalar", lambda e: e.activation(SM(2), SM(1), AF.Exp),
                      wait=[("ld", lds)], inc=("ac", 1))
        v_a = pb.emit("vector", lambda e: e.tensor_scalar(SM(3), SM(2), -1.0, None,
                                                          ALU.mult),
                      wait=[("ac", a_a)], inc=("ve", 1))
        u_ = sqs[:, 1024:2048]
        v_u = pb.emit("vector", lambda e: e.tensor_scalar(u_, dtraw[:, :], SM(0),
                                                          None, ALU.add),
                      wait=[("ld", lds), ("ve", dt_ev)], inc=("ve", 1))
        ab1 = daf
        a_ab = pb.emit("scalar", lambda e: e.activation(ab1[:, :], u_, AF.Abs),
                       wait=[("ve", v_u)], inc=("ac", 1))
        a_ex = pb.emit("scalar", lambda e: e.activation(ab1[:, :], ab1[:, :], AF.Exp, scale=-1.0),
                       inc=("ac", 1))
        a_ln = pb.emit("scalar", lambda e: e.activation(ab1[:, :], ab1[:, :], AF.Ln, bias=1.0),
                       inc=("ac", 1))
        a_re = pb.emit("scalar", lambda e: e.activation(u_, u_, AF.Relu), inc=("ac", 1))
        v_dt = pb.emit("vector", lambda e: e.tensor_tensor(dtf[:, :], ab1[:, :], u_, ALU.add),
                       wait=[("ac", a_re)], inc=("ve", 1))
        pb.dma("sync", o_dt[:, :], dtf[:, :], sem="st", wait=[("ve", v_dt)])
        v_da = pb.emit("vector", lambda e: e.tensor_scalar(daf[:, :], dtf[:, :], SM(3), None,
                                                           ALU.mult),
                       wait=[("ve", v_dt), ("ac", a_ln)], inc=("ve", 1))
        v_sc = None
        for c in range(NCH):
            v_sc = pb.emit("vector", lambda e, c=c: e.tensor_tensor_scan(
                acs[:, ts(c, 256)], daf[:, ts(c, 256)], zrow[:, 0:256], 0.0, ALU.add, ALU.add),
                wait=[("ve", v_da)], inc=("ve", 1))
        pb.dma("sync", o_acs[:, :], acs[:, :], sem="st", wait=[("ve", v_sc)])

        def asum_fn(e):
            e.tensor_tensor(SM(4), acs[:, 255:256], acs[:, 511:512], ALU.add)
            return e.tensor_tensor(SM(5), acs[:, 767:768], acs[:, 1023:1024], ALU.add)
        v_as = pb.emit("vector", asum_fn, wait=[("ve", v_sc)], inc=("ve", 1))
        a_as = pb.emit("scalar", lambda e: e.add(SM(6), SM(4), SM(5)),
                       wait=[("ve", v_as)], inc=("ac", 1))
        pb.dma("sync", o_asum[:, :], SM(6), sem="st", wait=[("ac", a_as)])
        ac_tr = None
        for c in range(NCH):
            for lt in range(2):
                tr = pb.emit("tensor", lambda e, c=c, lt=lt: e.transpose(
                    pR[:, 0:128], acs[:, c * 256 + lt * 128:c * 256 + lt * 128 + 128], ident_f[:]),
                    wait=[("ve", v_sc), ("ve", ac_tr)], inc=("pe", 1))
                ac_tr = pb.emit("vector", lambda e, c=c, lt=lt: e.tensor_copy(
                    acst[:, c * 256 + lt * 128:c * 256 + lt * 128 + 128], pR[:, 0:128]),
                    wait=[("pe", tr)], inc=("ve", 1))
        pb.dma("sync", o_acst.rearrange("(a p) h -> p a h", p=128),
               acst.rearrange("p (a h) -> p a h", a=8), sem="st",
               wait=[("ve", ac_tr)])

        if upto == "inproj":
            pb.finish()
            pb.run(block)
            return nc

        conv_ve = {0: 0, 1: 0}
        xc_st = 0
        for ot in range(80):
            buf = ot % 2
            pv = pre_t[:, buf * 1056:buf * 1056 + 1027]
            hb = hls[0:3, ts(buf, 128)]
            w1 = [("ve", conv_ve[buf])] if conv_ve[buf] else []
            l1 = pb.dma("sync", pv[:, 3:1027], pre_dram[ts(ot, 128), :], wait=w1)
            l2 = pb.dma("sync", hb, halo_dram[0:3, ts(ot, 128)])
            trh = pb.emit("tensor", lambda e, hb=hb: e.transpose(
                pT[:, 0:3], hb, ident_bf[0:3, 0:3]),
                wait=[("ld", l2), ("ve", conv_ve[buf])], inc=("pe", 1))
            hv = pb.emit("vector", lambda e, pv=pv: e.tensor_copy(pv[:, 0:3], pT[:, 0:3]),
                         wait=[("pe", trh), ("ld", l1)], inc=("ve", 1))
            pb.wait("tensor", [("ve", hv)])
            cv = pb.emit("vector", lambda e, pv=pv, ot=ot: (
                e.tensor_scalar(cacc[:, :], pv[:, 0:1024], convw[:, ot * 4:ot * 4 + 1], None,
                                ALU.mult),
                e.scalar_tensor_tensor(cacc[:, :], pv[:, 1:1025], convw[:, ot * 4 + 1:ot * 4 + 2],
                                       cacc[:, :], ALU.mult, ALU.add),
                e.scalar_tensor_tensor(cacc[:, :], pv[:, 2:1026], convw[:, ot * 4 + 2:ot * 4 + 3],
                                       cacc[:, :], ALU.mult, ALU.add),
                e.scalar_tensor_tensor(cacc[:, :], pv[:, 3:1027], convw[:, ot * 4 + 3:ot * 4 + 4],
                                       cacc[:, :], ALU.mult, ALU.add))[-1],
                wait=[("ve", hv)], inc=("ve", 1))
            conv_ve[buf] = cv
            av = pb.emit("scalar", lambda e, ot=ot: e.activation(
                xntok[:, 0:1024], cacc[:, :], AF.Silu, bias=convb[:, ot:ot + 1]),
                wait=[("ve", cv), ("st", max(st_after_ln, xc_st)), ("ld", lds)], inc=("ac", 1))
            xc_st = pb.dma("sync", o_xbc[ts(ot, 128), :], xntok[:, 0:1024], sem="st",
                           wait=[("ac", av)])
            conv_ve[buf] = cv
            pb.wait("vector", [("ac", av)])

        if upto == "conv":
            pb.finish()
            pb.run(block)
            return nc

        l_fa = pb.dma("gpsimd", ffna_sb.rearrange("p (k r) -> p k r", k=32),
              ffn_a.rearrange("(k p) r -> p k r", p=128), sem="cast")
        l_lm = pb.dma("gpsimd", limes_bf[0:8, 0:32], limes[:, :], sem="cast")
        l_b8 = pb.dma("sync", bcol8[0:8, 0:1], ffn_b[0:8, None])

        h_ev = {0: 0, 1: 0}
        h_st = {0: 0, 1: 0}
        for bi in range(8):
            wtb = wts[bi % 2]
            wait_tr = [("cast", ffn_cast_at[bi]),
                       ("pe", pe_done[("h", bi - 2)] if bi >= 2 else inproj_pe)]
            tsem = "trd" if bi % 2 == 0 else "tr2"
            trd_v = None
            for k in range(32):
                trd_v = pb.dmat("sync", wtb[:, ts(k, 512)], ffnw_bf[ts(bi, 512), ts(k, 128)],
                                sem=tsem, wait=wait_tr if k == 0 else ())
            pb.wait("tensor", [(tsem, trd_v)])
            for t_i in range(8):
                ps = pA if t_i % 2 == 0 else pB
                buf = t_i % 2
                pb.wait("tensor", [("ve", h_ev[buf])])
                for k in range(32):
                    pb.emit("tensor", lambda e, k=k, t_i=t_i, ps=ps, wtb=wtb: e.matmul(
                        ps[:], xnT[:, k * 1024 + t_i * 128:k * 1024 + t_i * 128 + 128],
                        wtb[:, ts(k, 512)], start=(k == 0), stop=(k == 31)),
                        inc=("pe", 1) if k == 31 else None)
                mmd = pb.cnt["pe"]
                hdst = evb[:, buf * 1024:buf * 1024 + 512]
                ev = pb.emit("vector", lambda e, ps=ps, bi=bi, hdst=hdst: e.tensor_tensor(
                    hdst, ps[:], bexp_bf[:, ts(bi, 512)], ALU.add),
                    wait=[("pe", mmd), ("st", h_st[buf]),
                          ("cast", ld0[-3] if (bi == 0 and t_i == 0) else 0)], inc=("ve", 1))
                h_ev[buf] = ev
                h_st[buf] = pb.dma("sync", o_h[ts(t_i, 128), ts(bi, 512)], hdst, sem="st",
                                   wait=[("ve", ev)])
            pe_done[("h", bi)] = pb.cnt["pe"]
            if bi == 0:
                for k in range(32):
                    pb.emit("tensor", lambda e, k=k, wtb=wtb: e.matmul(
                        pH[0:8, :], wtb[:, k * 512:k * 512 + 8],
                        xnT[:, k * 1024 + 1:k * 1024 + 1024:2],
                        start=(k == 0), stop=(k == 31)),
                        inc=("pe", 1) if k == 31 else None)
                hsv = pb.emit("vector", lambda e: e.tensor_scalar(
                    hst[0:8, :], pH[0:8, :], bcol8[0:8, 0:1], None, ALU.add),
                    wait=[("pe", pb.cnt["pe"]), ("ld", l_b8)], inc=("ve", 1))
                pb.dma("sync", o_hst[:, :], hst[0:8, :], sem="st", wait=[("ve", hsv)])

        zf_ev = {0: 0, 1: 0}
        for half in range(2):
            ps = pA if half == 0 else pB
            pb.wait("tensor", [("ve", h_ev[half]), ("cast", l_fa)])
            for k in range(32):
                pb.emit("tensor", lambda e, k=k, half=half, ps=ps: e.matmul(
                    ps[0:32, :], ffna_sb[:, ts(k, 32)],
                    xnT[:, k * 1024 + half * 512:k * 1024 + half * 512 + 512],
                    start=(k == 0), stop=(k == 31)),
                    inc=("pe", 1) if k == 31 else None)
            zf_ev[half] = pb.emit("vector", lambda e, half=half, ps=ps: e.tensor_copy(
                zffn[0:32, ts(half, 512)], ps[0:32, :]),
                wait=[("pe", pb.cnt["pe"])], inc=("ve", 1))
        ffn0_pe = pb.cnt["pe"]
        pb.dma("sync", o_zffn[:, :], zffn[0:32, :], sem="st", wait=[("ve", zf_ev[1])])
        zb_ev = pb.emit("vector", lambda e: e.tensor_copy(zffn_bf[0:32, :], zffn[0:32, :]),
                        wait=[("ve", zf_ev[1])], inc=("ve", 1))
        lt_tr = pb.emit("tensor", lambda e: e.transpose(pT[0:32, 0:8], limes_bf[0:8, 0:32],
                                                        ident_bf[0:8, 0:8]),
                        wait=[("ld", l_lm)], inc=("pe", 1))
        lt_ev = pb.emit("vector", lambda e: e.tensor_copy(limesT_bf[0:32, 0:8], pT[0:32, 0:8]),
                        wait=[("pe", lt_tr)], inc=("ve", 1))
        dmm = pb.emit("tensor", lambda e: e.matmul(
            pH[0:8, :], limesT_bf[0:32, 0:8], zffn_bf[0:32, 1:1024:2], start=True, stop=True),
            wait=[("ve", max(zb_ev, lt_ev))], inc=("pe", 1))
        dsv = pb.emit("vector", lambda e: e.tensor_copy(dst_sl[0:8, :], pH[0:8, :]),
                      wait=[("pe", dmm)], inc=("ve", 1))
        pb.dma("sync", o_dst[:, :], dst_sl[0:8, :], sem="st", wait=[("ve", dsv)])

        lmx = stat[0:8, 32:34]
        def lmax_fn(e):
            e.tensor_reduce(lmx[0:8, 0:1], hst[0:8, :], AX, ALU.max, apply_absolute_value=True)
            return e.tensor_reduce(lmx[0:8, 1:2], dst_sl[0:8, :], AX, ALU.max,
                                   apply_absolute_value=True)
        lv = pb.emit("vector", lmax_fn, wait=[("ve", dsv)], inc=("ve", 1))
        ltr = pb.emit("tensor", lambda e: e.transpose(pR[0:2, 0:8], lmx,
                                                      ident_f[0:8, 0:8]),
                      wait=[("ve", lv), ("ve", ac_tr)], inc=("pe", 1))
        lm2 = pb.emit("vector", lambda e: e.tensor_reduce(stat[0:2, 40:41], pR[0:2, 0:8], AX,
                                                          ALU.max),
                      wait=[("pe", ltr)], inc=("ve", 1))
        ltr2 = pb.emit("tensor", lambda e: e.transpose(pR[0:1, 128:130], stat[0:2, 40:41],
                                                       ident_f[0:2, 0:2]),
                       wait=[("ve", lm2)], inc=("pe", 1))
        lm3 = pb.emit("vector", lambda e: e.tensor_copy(stat[0:1, 48:50], pR[0:1, 128:130]),
                      wait=[("pe", ltr2)], inc=("ve", 1))
        pb.dma("sync", o_locmax[:, :], stat[0:1, 48:50], sem="st", wait=[("ve", lm3)])

        if upto == "ffn0":
            pb.finish()
            pb.run(block)
            return nc

        xdt = wt_t[:, 0:16384]
        xdd = wt_t[:, 16384:32768]
        bct = xnt_t[:, 0:4096]
        fbf = xnt_t[:, 4096:12288]
        s_sb = xnt_t[:, 12288:20480]
        btok = xnt_t[:, 20480:22528]
        gmask = xnt_t[:, 22528:26624]
        ltb = xnt_t[:, 26624:27136]
        glb = xnt_t[:, 27136:27648]
        bdb = xnt_t[:, 27648:27904]
        xdtr = xnt_t[:, 27904:28160]
        ybuf = xnt_t[:, 28160:29184]

        n_chunks = NCH
        n_heads = NH
        ssm_parts = "ABCDE"
        if upto.startswith("ssm"):
            tag = upto[3:]
            if tag and tag[0] in "ABCD":
                ssm_parts = "ABCD"[: "ABCD".index(tag[0]) + 1]
                tag = tag[1:]
            if tag:
                parts = tag.split("h")
                n_chunks = int(parts[0]) if parts[0] else 1
                n_heads = int(parts[1]) if len(parts) > 1 else NH
            else:
                n_chunks, n_heads = 1, 0
        o_xbc_r = o_xbc.rearrange("(k p) t -> p k t", p=128)
        y_st = {}
        y_ev_g = 0
        prev_dst_ev = 0
        prev_fv = 0
        prev_bct_pe = 0
        fv = None
        SKIP = os.environ.get("SSM_SKIP", "")
        for c in range(n_chunks):
            cs = slice(c * 256, c * 256 + 256)
            pb.wait("tensor", [("ve", lm3 if c == 0 else prev_fv)])
            ldx2 = 0
            if "l" not in SKIP:
                ldx2 = pb.dma("sync", bct, o_xbc_r[:, 64:80, cs], sem="ld",
                              wait=[("pe", ffn0_pe if c == 0 else prev_bct_pe),
                                    ("st", xc_st), ("ve", prev_fv)])
            vna = 0
            if "v" not in SKIP:
                vna = pb.emit("vector", lambda e, cs=cs: e.tensor_scalar(
                    nacst[:, 0:256], acst[:, cs], -1.0, None, ALU.mult),
                    wait=[("ve", max(ac_tr, prev_dst_ev))], inc=("ve", 1))
            a_ds = 0
            if "a" not in SKIP:
                a_ds = pb.emit("scalar", lambda e, c=c, cs=cs: e.activation(
                    sqs[:, 1024:1280], acs[:, cs], AF.Exp,
                    bias=acs[:, c * 256 + 255:c * 256 + 256], scale=-1.0),
                    wait=[("ve", max(v_sc, prev_dst_ev))], inc=("ac", 1))
            dclamp = sqs[:, 1280:1792]
            tr_ev = 0
            for st_i in range(0 if "t" in SKIP else 2):
                t1 = pb.emit("tensor", lambda e, st_i=st_i: e.transpose(
                    pR[:, 0:128], sqs[:, 1024 + st_i * 128:1024 + st_i * 128 + 128], ident_f[:]),
                    wait=[("ac", a_ds), ("ve", tr_ev)], inc=("pe", 1))
                e1 = pb.emit("vector", lambda e, st_i=st_i: e.tensor_copy(
                    dsT[:, ts(st_i, 128)], pR[:, 0:128]), wait=[("pe", t1)], inc=("ve", 1))
                t2 = pb.emit("tensor", lambda e, c=c, st_i=st_i: e.transpose(
                    pR[:, 128:256], dtf[:, c * 256 + st_i * 128:c * 256 + st_i * 128 + 128],
                    ident_f[:]), wait=[("ve", v_dt), ("ve", e1)], inc=("pe", 1))
                e2 = pb.emit("vector", lambda e, st_i=st_i: e.tensor_copy(
                    dtT[:, ts(st_i, 128)], pR[:, 128:256]), wait=[("pe", t2)], inc=("ve", 1))
                tr_ev = e2
            prev_dst_ev = tr_ev
            xv = 0
            xl_prev = 0
            for st_i in range(2 if "B" in ssm_parts else 0):
                for kb in range(64):
                    bufb = kb % 2
                    xsem = "xa" if bufb == 0 else "xb"
                    xl = pb.dmat("sync", xdtr[:, ts(bufb, 128)],
                                 o_xbc[ts(kb, 128), c * 256 + st_i * 128:c * 256 + st_i * 128 + 128],
                                 sem=xsem,
                                 wait=([("st", xc_st), ("pe", ffn0_pe), ("ve", xv)]
                                       if (c == 0 and st_i == 0 and kb < 2) else
                                       ([("ve", xv)] if kb >= 2 or st_i or c else ())))
                    h2 = 2 * kb

                    def evx(e, kb=kb, st_i=st_i, h2=h2, bufb=bufb):
                        e.tensor_tensor(
                            xdt[:, st_i * 8192 + kb * 128:st_i * 8192 + kb * 128 + 128],
                            xdtr[:, ts(bufb, 128)],
                            dtT[:, st_i * 128 + h2:st_i * 128 + h2 + 2][:, :, None]
                            .broadcast_to([128, 2, PD]), ALU.mult)
                        return e.tensor_tensor(
                            xdd[:, st_i * 8192 + kb * 128:st_i * 8192 + kb * 128 + 128],
                            xdtr[:, ts(bufb, 128)],
                            dbc[:, h2:h2 + 2][:, :, None].broadcast_to([128, 2, PD]), ALU.mult)
                    xv = pb.emit("vector", evx,
                                 wait=[(xsem, xl), ("ve", tr_ev)] +
                                 ([("pe", ffn0_pe)] if (c == 0 and st_i == 0 and kb == 0) else []),
                                 inc=("ve", 1))
            xdt_rdy = xv
            if c == 0 and os.environ.get("DBG_XDT"):
                pb.dma("sync", o_dbg1[:, :], wt_t[:, 0:16384], sem="st",
                       wait=[("ve", xv)])
                pb.dma("sync", o_dbg3[:, :], dsT[:, :], sem="st", wait=[("ve", tr_ev)])
            gvv = {}
            btv = {}
            g_ev = 0
            for gi in range(8 if "C" in ssm_parts else 0):
                pb.wait("tensor", [("ve", g_ev)])
                for st_i in range(2):
                    pb.emit("tensor", lambda e, gi=gi, st_i=st_i: e.matmul(
                        pA[:, ts(st_i, 256)],
                        bct[:, gi * 256 + st_i * 128:gi * 256 + st_i * 128 + 128],
                        bct[:, (8 + gi) * 256:(8 + gi) * 256 + 256],
                        start=(st_i == 0), stop=(st_i == 1), skip_group_check=(st_i == 1)),
                        wait=([("ld", ldx2)] + ([("ve", lm3)] if c == 0 else []))
                        if (gi == 0 and st_i == 0) else (),
                        inc=("pe", 1) if st_i == 1 else None)
                gmm = pb.cnt["pe"]
                g_ev = pb.emit("vector", lambda e, gi=gi: (
                    e.tensor_tensor(gmask[:, gi * 512:gi * 512 + 256], pA[:, 0:256],
                                    tril[:, 0:256], ALU.mult),
                    e.tensor_tensor(gmask[:, gi * 512 + 256:gi * 512 + 512], pA[:, 256:512],
                                    tril[:, 256:512], ALU.mult))[-1],
                    wait=[("pe", gmm)], inc=("ve", 1))
                gvv[gi] = g_ev
                for st_i in range(2):
                    btv[(gi, st_i)] = pb.dmat(
                        "sync", btok[:, (gi * 2 + st_i) * 128:(gi * 2 + st_i) * 128 + 128],
                        o_xbc[8192 + gi * 128:8192 + gi * 128 + 128,
                              c * 256 + st_i * 128:c * 256 + st_i * 128 + 128],
                        sem="ts",
                        wait=[("st", xc_st), ("pe", ffn0_pe)] if (c == 0 and gi == 0 and st_i == 0)
                        else ())
                btv[gi] = btv[(gi, 1)]
            btok_rdy = btv.get(7, 0)
            if c == 0 and os.environ.get("DBG_XDT"):
                pb.dma("sync", o_dbg2[:, :], btok[:, :], sem="st", wait=[("ts", btok_rdy)])
            prev_bct_pe = pb.cnt["pe"]

            y_ev = y_ev_g
            prev_bc_pe = 0
            prev_lt_ac = 0
            prev_gl_ve = 0
            prev_dcl = 0
            prev_mm = 0
            for h in range(n_heads if "D" in ssm_parts else 0):
                gi = h // 16
                hm = h % 8
                fresh = hm == 0
                rx = pb.emit("tensor", lambda e, h=h, cs=cs: e.matmul(
                    pR[0:1, 256:512], ident_f[:, h:h + 1], acs[:, cs], start=True, stop=True,
                    skip_group_check=True),
                    wait=[("ve", prev_dcl), ("ve", tr_ev if h == 0 else 0)], inc=("pe", 1))
                re_ = pb.emit("vector", lambda e: e.tensor_copy(rowf[0:1, 0:256], pR[0:1, 256:512]),
                              wait=[("pe", rx)], inc=("ve", 1))
                bc = pb.emit("tensor", lambda e: e.matmul(
                    pR[:, 256:512], ones_f[0:1, :], rowf[0:1, 0:256], start=True, stop=True,
                    skip_group_check=True),
                    wait=[("ve", re_)], inc=("pe", 1))
                v_dcl = pb.emit("vector", lambda e, h=h: (
                    e.tensor_scalar(dclamp[:, 0:256], pR[:, 256:512],
                                    nacst[:, h:h + 1], 0.0, ALU.add, ALU.min),
                    e.tensor_scalar(dclamp[:, 256:512], pR[:, 256:512],
                                    nacst[:, 128 + h:128 + h + 1], 0.0, ALU.add, ALU.min))[-1],
                    wait=[("pe", bc), ("ve", vna), ("ac", prev_lt_ac)], inc=("ve", 1))
                a_lt = pb.emit("scalar", lambda e: (
                    e.activation(ltb[:, 0:256], dclamp[:, 0:256], AF.Exp),
                    e.activation(ltb[:, 256:512], dclamp[:, 256:512], AF.Exp))[-1],
                    wait=[("ve", v_dcl), ("ve", prev_gl_ve)], inc=("ac", 1))
                prev_lt_ac = a_lt
                prev_dcl = v_dcl
                v_gl = pb.emit("vector", lambda e, gi=gi: (
                    e.tensor_tensor(glb[:, 0:256], ltb[:, 0:256],
                                    gmask[:, gi * 512:gi * 512 + 256], ALU.mult),
                    e.tensor_tensor(glb[:, 256:512], ltb[:, 256:512],
                                    gmask[:, gi * 512 + 256:gi * 512 + 512], ALU.mult))[-1],
                    wait=[("ac", a_lt), ("ve", gvv[gi]), ("pe", prev_mm)], inc=("ve", 1))
                prev_gl_ve = v_gl
                v_bd = pb.emit("vector", lambda e, gi=gi, h=h: (
                    e.tensor_scalar(bdb[:, 0:128], btok[:, (gi * 2) * 128:(gi * 2) * 128 + 128],
                                    dsT[:, h:h + 1], None, ALU.mult),
                    e.tensor_scalar(bdb[:, 128:256],
                                    btok[:, (gi * 2 + 1) * 128:(gi * 2 + 1) * 128 + 128],
                                    dsT[:, 128 + h:128 + h + 1], None, ALU.mult))[-1],
                    wait=[("ts", btok_rdy), ("ve", xdt_rdy), ("pe", prev_mm)], inc=("ve", 1))
                if fresh:
                    pb.wait("tensor", [("ve", y_ev)])
                pb.wait("tensor", [("ve", max(v_gl, v_bd))])
                for lt_i in range(2):
                    pYx = pY0 if lt_i == 0 else pY1
                    for st_i in range(2):
                        pb.emit("tensor", lambda e, lt_i=lt_i, st_i=st_i, h=h, hm=hm, pYx=pYx,
                                fresh=fresh: e.matmul(
                            pYx[:, hm * 64:hm * 64 + 64],
                            glb[:, st_i * 256 + lt_i * 128:st_i * 256 + lt_i * 128 + 128],
                            xdt[:, st_i * 8192 + h * 64:st_i * 8192 + h * 64 + 64],
                            start=(fresh and st_i == 0), stop=False,
                            skip_group_check=not (fresh and st_i == 0)))
                for st_i in range(2):
                    pb.emit("tensor", lambda e, st_i=st_i, h=h, hm=hm, fresh=fresh: e.matmul(
                        pS[:, hm * 64:hm * 64 + 64], bdb[:, ts(st_i, 128)],
                        xdt[:, st_i * 8192 + h * 64:st_i * 8192 + h * 64 + 64],
                        start=(fresh and st_i == 0), stop=False,
                        skip_group_check=not (fresh and st_i == 0)),
                        inc=("pe", 1) if st_i == 1 else None)
                prev_mm = pb.cnt["pe"]
                if hm == 7:
                    hg = h // 8

                    def yev(e, hg=hg):
                        e.tensor_tensor(ybuf[:, 0:512], pY0[:],
                                        xdd[:, 0 * 8192 + hg * 512:0 * 8192 + hg * 512 + 512],
                                        ALU.add)
                        e.tensor_tensor(ybuf[:, 512:1024], pY1[:],
                                        xdd[:, 1 * 8192 + hg * 512:1 * 8192 + hg * 512 + 512],
                                        ALU.add)
                        return e.tensor_copy(s_sb[:, hg * 512:hg * 512 + 512], pS[:])
                    y_ev = pb.emit("vector", yev,
                                   wait=[("pe", prev_mm), ("st", y_st.get(0, 0))], inc=("ve", 1))
                    y_st[0] = pb.dma("sync", o_y1[c * 256:c * 256 + 128,
                                                  hg * 512:hg * 512 + 512],
                                     ybuf[:, 0:512], sem="st", wait=[("ve", y_ev)])
                    y_st[0] = pb.dma("sync", o_y1[c * 256 + 128:c * 256 + 256,
                                                  hg * 512:hg * 512 + 512],
                                     ybuf[:, 512:1024], sem="st", wait=[("ve", y_ev)])
            s_done = y_ev
            y_ev_g = y_ev
            prev_bct_pe = pb.cnt["pe"]
            sd = 0
            if "s" not in SKIP:
                sd = pb.dma("sync", o_s[ts(c, 128), :], s_sb[:, :], sem="st",
                            wait=[("ve", s_done)])
            if "f" in SKIP:
                fv = 0
            elif c == 0:
                fv = pb.emit("vector", lambda e: e.tensor_copy(fbf[:, :], s_sb[:, :]),
                             wait=[("ve", s_done)], inc=("ve", 1))
            else:
                a_ea = pb.emit("scalar", lambda e, c=c: e.activation(
                    SM(7), acs[:, c * 256 + 255:c * 256 + 256], AF.Exp),
                    wait=[("ve", v_sc)], inc=("ac", 1))
                t_ea = pb.emit("tensor", lambda e: e.transpose(pR[0:1, 0:128], SM(7),
                                                               ident_f[:]),
                               wait=[("ac", a_ea), ("ve", prev_dcl)], inc=("pe", 1))
                e_ea = pb.emit("vector", lambda e: e.tensor_copy(earow[0:1, 0:128],
                                                                 pR[0:1, 0:128]),
                               wait=[("pe", t_ea)], inc=("ve", 1))
                b_ea = pb.emit("tensor", lambda e: e.matmul(
                    pR[:, 128:256], ones_f[0:1, :], earow[0:1, 0:128], start=True, stop=True,
                    skip_group_check=True),
                    wait=[("ve", e_ea)], inc=("pe", 1))
                e2_ea = pb.emit("vector", lambda e: e.tensor_copy(eacol[:, :], pR[:, 128:256]),
                                wait=[("pe", b_ea)], inc=("ve", 1))
                fv = pb.emit("vector", lambda e: (
                    e.tensor_tensor(fbf[:, :], fbf[:, :],
                                    eacol[:, :, None].broadcast_to([128, 128, PD]), ALU.mult),
                    e.tensor_tensor(fbf[:, :], fbf[:, :], s_sb[:, :], ALU.add))[-1],
                    wait=[("ve", max(e2_ea, s_done))], inc=("ve", 1))
            pb.wait("vector", [("st", sd)])
            prev_fv = fv or 0
        pb.dma("gpsimd", o_f[:, :], fbf[:, :], sem="stg", wait=[("ve", fv)])

        pb.finish()
        pb.run(block)
    return nc



def build_launch2():
    nc = bass.Bass("TRN2", target_bir_lowering=False)
    dI = lambda n, s, d=F32: nc.dram_tensor(n, s, d, kind="ExternalInput").ap()
    dO = lambda n, s, d=F32: nc.dram_tensor(n, s, d, kind="ExternalOutput").ap()
    dT = lambda n, s, d=BF16: nc.dram_tensor(n, s, d, kind="Internal").ap()

    x_main = dI("x_main", [T, D])
    i_xn = dI("i_xn", [T, D], BF16)
    i_xbc = dI("i_xbc", [CONV_DIM, T], BF16)
    i_z = dI("i_z", [T, DI], BF16)
    i_acs = dI("i_acs", [NH, T])
    i_acst = dI("i_acst", [T, NH])
    i_y1 = dI("i_y1", [T, DI], BF16)
    i_s = dI("i_s", [NCH * NS, DI], BF16)
    i_h = dI("i_h", [T, D], BF16)
    i_hst = dI("i_hst", [8, 512])
    i_dst = dI("i_dst", [8, 512])
    i_zffn = dI("i_zffn", [32, T])
    g_f = dI("g_f", [W * NS, DI])
    g_a = dI("g_a", [W * NH, 1])
    g_max = dI("g_max", [W, 2])
    cmask = dI("cmask", [W])
    usuf = dI("usuf", [W, W])
    wo_w = dI("wo_w", [D, DI])
    bmat = dI("bmat", [32, D])
    limes = dI("limes", [8, 32])
    norm_l = dI("norm_l", [128, 64])
    cst = {n: dI(n, s, d) for n, s, d in CONST_SPECS}
    c_eps6 = dI("c_eps6", [1, 1])
    c_eps9 = dI("c_eps9", [1, 1])

    b_x1 = dO("b_x1", [T, D])
    b_dbgA = dO("b_dbgA", [128, 64])
    b_dbgB = dO("b_dbgB", [32, 1024])
    b_dbgC = dO("b_dbgC", [8, 512])
    b_dbgD = dO("b_dbgD", [128, 32])
    b_dbgE = dO("b_dbgE", [8, 64])
    b_dbgF = dO("b_dbgF", [128, 48])

    wo_bf = dT("wo_bf2", [D, DI])
    bmat_b = dT("bmat_b2", [32, D])
    g_dram = nc.dram_tensor("b_g", [T, D], BF16, kind="ExternalOutput").ap()
    rt_dram = dT("rt_dram2", [16, 512])
    wt_dram = dT("wt_dram2", [4, 128, 128])

    with contextlib.ExitStack() as xs:
        E = xs.enter_context
        b1 = E(nc.sbuf_tensor("L2b1", [128, 32 * 1024], BF16))
        b2 = E(nc.sbuf_tensor("L2b2", [128, 32 * 1024], BF16))
        m2 = E(nc.sbuf_tensor("L2m2", [128, 14 * 1024], BF16))
        fa = E(nc.sbuf_tensor("L2f32", [128, 12160], F32))
        pA = E(nc.psum_tensor("qA", [128, 512], F32))
        pB = E(nc.psum_tensor("qB", [128, 512], F32))
        pC = E(nc.psum_tensor("qC", [128, 512], F32))
        pD = E(nc.psum_tensor("qD", [128, 512], F32))
        pR = E(nc.psum_tensor("qR", [128, 512], F32))
        pT = E(nc.psum_tensor("qT", [128, 256], BF16))
        block = E(nc.Block())
        pb = PB(nc)
        for s in ["cast", "trd", "tr2", "ld", "ld2", "ld3", "pe", "ve", "ac", "gp",
                  "st", "st2", "stg"]:
            pb.new_sem(E(nc.semaphore("q_" + s)), s)

        b1v, b2v, m2v, fav = b1[:], b2[:], m2[:], fa[:]
        y_sl = b1v[:, 0:16384]
        z_sl = b1v[:, 16384:32768]
        fm_f = b1v[:, 0:16384].bitcast(F32)
        yT = b2v[:, 0:16384]
        sq_bf = b2v[:, 0:16384]
        h_sl = b2v[:, 0:8192]
        tmp_cmb = b2v[:, 0:16384].bitcast(F32)
        s_sl = b2v[:, 16384:24576]
        bm_sb = b2v[0:32, 16384:20480]
        r_bf = b2v[:, 24576:32768]
        cT = m2v[:, 0:2048]
        wob = m2v[:, 2048:3072]
        ident_bf = m2v[:, 3072:3200]
        limes_bf2 = m2v[0:8, 3200:3232]
        mskq = m2v[:, 3776:3808].bitcast(U8)
        zpb = m2v[0:32, 3264:4288]
        wfb2 = m2v[0:8, 4288:5312]
        t1tmp = m2v[:, 5312:6336].bitcast(F32)
        gsil = m2v[:, 6336:7360]
        slots = m2v[:, 7360:13536 - 1248]
        sl0 = 7360
        def slot(buf, what):
            base = sl0 + buf * 3072
            off = {"xn": 0, "g": 512, "xin": 1024, "x1": 2048}[what]
            seg = m2v[:, base + off: base + off + (512 if what in ("xn", "g") else 1024)]
            return seg if what in ("xn", "g") else seg.bitcast(F32)
        assert sl0 + 2 * 3072 <= 14 * 1024 - 256
        Rf = fav[:, 0:8192]
        acs_sb = fav[:, 8192:9216]
        odT = fav[:, 9216:9472]
        acst_sl = fav[:, 9472:9728]
        eacol = fav[:, 9728:9856]
        earow = fav[:, 9856:9984]
        stat = fav[:, 9984:10048]
        ident_f = fav[:, 10048:10176]
        hstb = fav[0:8, 10176:10688]
        dstb = fav[0:8, 10688:11200]
        lgA = fav[:, 11200:11232]
        prbs = fav[:, 11232:11264]
        knk = fav[:, 11264:11296]
        wn_f = fav[:, 11296:11328]
        gA8 = fav[0:8, 11328:11456]
        wnT = fav[0:8, 11456:11968]
        zeros8 = fav[:, 11968:12000]
        cm8 = fav[0:8, 12000:12004]
        us8 = fav[0:8, 12004:12012]
        whm = fav[:, 12016:12024]
        nw_l = fav[:, 12032:12096]
        gmx = fav[0:8, 12096:12098]
        zf_f = fav[32:64, 10176:11200]
        gt_f = fav[:, 12098 + 2:12160 - 48]

        def SC2(i, rows=128):
            return stat[0:rows, 4 * i:4 * i + 1]

        ld0 = [
            pb.dma("sync", ident_bf, cst["c_ident_bf"]),
            pb.dma("sync", ident_f, cst["c_ident_f32"]),
            pb.dma("sync", acs_sb, i_acs[:, :]),
            pb.dma("sync", SC2(15, 1), cst["c_eps"][0:1, 0:1]),
            pb.dma("sync", zeros8, cst["c_zeros"][:, 0:32].broadcast_to([128, 32])),
            pb.dma("sync", nw_l, norm_l[:, :]),
            pb.dma("sync", gmx, g_max[:, :]),
            pb.dma("sync", gA8, g_a.rearrange("(m h) one -> m (h one)", h=128)),
            pb.dma("sync", cm8[0:8, 0:1], cmask[:, None]),
            pb.dma("sync", us8[0:8, 0:8], usuf[:, :]),
            pb.dma("sync", SC2(14, 1), c_eps6[0:1, 0:1]),
            pb.dma("sync", SC2(13, 1), c_eps9[0:1, 0:1]),
            pb.dma("sync", hstb, i_hst[:, :]),
            pb.dma("sync", dstb, i_dst[:, :]),
            pb.dma("sync", zf_f[0:32, 0:1024], i_zffn[:, :]),
        ]
        ones_f = ident_f
        onesr = fav[0:1, 12104:12232 - 0] if False else None
        onesr = fav[:, 12100:12100]
        ones_f = fav[:, 12032 - 192:12032 - 64]
        ones_f = fav[:, 12128:12160]
        ones_f = m2v[:, 3264 - 256:3264].bitcast(F32)
        ones_f = m2v[:, 13536 - 1248:12288].bitcast(F32) if False else None
        ones_f = m2v[:, 14336 - 256:14336].bitcast(F32)
        ld0.append(pb.dma("sync", ones_f[0:1, :], cst["c_ones_f32"]))
        lds = max(ld0)

        cast_wo = [pb.dma("gpsimd", wo_bf[ts(b, 512), :], wo_w[ts(b, 512), :], sem="cast")
                   for b in range(8)]
        c_lm = pb.dma("gpsimd", limes_bf2, limes[:, :], sem="cast")
        c_bm = pb.dma("gpsimd", bmat_b[:, :], bmat[:, :], sem="cast")
        l_bm = pb.dma("sync", bm_sb, bmat_b[:, :], wait=[("cast", c_bm)])

        t_gm = pb.emit("tensor", lambda e: (
            e.transpose(pR[0:1, 0:8], gmx[0:8, 0:1], ident_f[0:8, 0:8]),
            e.transpose(pR[0:1, 8:16], gmx[0:8, 1:2], ident_f[0:8, 0:8]))[-1],
            wait=[("ld", lds)], inc=("pe", 1))
        v_g2 = pb.emit("vector", lambda e: (
            e.tensor_reduce(stat[0:1, 32:33], pR[0:1, 0:8], AX, ALU.max),
            e.tensor_reduce(stat[0:1, 33:34], pR[0:1, 8:16], AX, ALU.max))[-1],
            wait=[("pe", t_gm), ("ld", lds)], inc=("ve", 1))
        v_g4 = pb.emit("vector", lambda e: e.tensor_scalar(stat[0:1, 32:34], stat[0:1, 32:34],
                                                           SC2(14, 1), None, ALU.max),
                       wait=[("ve", v_g2)], inc=("ve", 1))
        a_g5 = pb.emit("scalar", lambda e: (
            e.activation(stat[0:1, 36:38], stat[0:1, 32:34], AF.Ln),
            e.activation(stat[0:1, 40:42], stat[0:1, 36:38], AF.Exp, scale=-1.0))[-1],
            wait=[("ve", v_g4)], inc=("ac", 1))
        v_g6 = pb.emit("vector", lambda e: e.tensor_scalar(stat[0:1, 44:46], stat[0:1, 40:42],
                                                           0.5, None, ALU.mult),
                       wait=[("ac", a_g5)], inc=("ve", 1))
        t_h = pb.emit("tensor", lambda e: e.matmul(pR[0:8, 16:17], ones_f[0:1, 0:8],
                                                   stat[0:1, 44:45], start=True, stop=True,
                                                   skip_group_check=True),
                      wait=[("ve", v_g6)], inc=("pe", 1))
        t_d = pb.emit("tensor", lambda e: e.matmul(pR[0:8, 20:21], ones_f[0:1, 0:8],
                                                   stat[0:1, 45:46], start=True, stop=True,
                                                   skip_group_check=True),
                      inc=("pe", 1))
        t_e9 = pb.emit("tensor", lambda e: (
            e.matmul(pR[0:128, 24:25], ones_f[0:1, 0:128], SC2(13, 1),
                     start=True, stop=True, skip_group_check=True),
            e.matmul(pR[0:128, 28:29], ones_f[0:1, 0:128], SC2(15, 1),
                     start=True, stop=True, skip_group_check=True))[-1],
            inc=("pe", 1))
        v_hd = pb.emit("vector", lambda e: (
            e.tensor_copy(SC2(1, 8), pR[0:8, 16:17]),
            e.tensor_copy(SC2(2, 8), pR[0:8, 20:21]),
            e.tensor_copy(SC2(13), pR[0:128, 24:25]),
            e.tensor_copy(SC2(15), pR[0:128, 28:29]))[-1],
            wait=[("pe", t_e9)], inc=("ve", 1))
        v_snap = pb.emit("vector", lambda e: (
            e.tensor_copy(stat[0:2, 34:42], pR[0:2, 0:8]),
            e.tensor_copy(stat[0:2, 42:43], stat[0:2, 0:1]),
            e.tensor_copy(stat[0:1, 16:18], pR[0:1, 8:10]),
            e.tensor_copy(stat[0:1, 18:20], stat[0:1, 32:34]),
            e.tensor_copy(stat[0:1, 20:22], stat[0:1, 36:38]),
            e.tensor_copy(stat[0:1, 22:24], stat[0:1, 40:42]),
            e.tensor_copy(stat[0:1, 24:26], stat[0:1, 44:46]))[-1],
            wait=[("ve", v_hd)], inc=("ve", 1))
        dmp = pb.dma("sync", b_dbgE[:, :], stat[0:8, :], sem="st", wait=[("ve", v_snap)])

        v_lg = pb.emit("vector", lambda e: (
            e.tensor_scalar(hstb, hstb, SC2(1, 8), None, ALU.mult),
            e.scalar_tensor_tensor(hstb, dstb, SC2(2, 8), hstb, ALU.mult, ALU.add))[-1],
            wait=[("ve", v_hd), ("st", dmp)], inc=("ve", 1))
        lgb = m2v[0:8, 3232:3744]
        v_lgb = pb.emit("vector", lambda e: e.tensor_copy(lgb, hstb),
                        wait=[("ve", v_lg)], inc=("ve", 1))
        s_lgb = pb.dma("sync", rt_dram[0:8, :], lgb, sem="st2", wait=[("ve", v_lgb)])
        tr_done = 0
        lgA_bf16x = m2v[:, 3744:3744 + 64]
        tdl = []
        for a4 in range(4):
            tdl.append(pb.dmat("sync", lgA_bf16x[:, a4 * 16:a4 * 16 + 16],
                               rt_dram[0:16, ts(a4, 128)],
                               sem="trd" if a4 % 2 == 0 else "tr2",
                               wait=[("st2", s_lgb)] if a4 == 0 else ()))
        tr_done = pb.emit("vector", lambda e: e.tensor_copy(
            lgA[:, 0:32].rearrange("p (a e) -> p a e", a=4),
            lgA_bf16x.rearrange("p (a e) -> p a e", a=4)[:, :, 0:8]),
            wait=[("trd", tdl[2]), ("tr2", tdl[3])], inc=("ve", 1))
        wn_done = 0
        for a4 in range(4):
            lg = lgA[:, ts(a4, 8)]
            v_m1 = pb.emit("vector", lambda e, lg=lg: e.tensor_reduce(SC2(3), lg, AX, ALU.max),
                           wait=[("ve", tr_done), ("ve", wn_done)], inc=("ve", 1))
            a_nm = pb.emit("scalar", lambda e: e.mul(SC2(4), SC2(3), -1.0),
                           wait=[("ve", v_m1)], inc=("ac", 1))
            p_sb = pb.emit("vector", lambda e, lg=lg: e.tensor_scalar(
                knk[:, 0:8], lg, SC2(4), None, ALU.add),
                wait=[("ac", a_nm)], inc=("ve", 1))
            a_ex = pb.emit("scalar", lambda e: e.activation(prbs[:, 0:8], knk[:, 0:8], AF.Exp),
                           wait=[("ve", p_sb)], inc=("ac", 1))
            v_sm = pb.emit("vector", lambda e: e.tensor_reduce(SC2(5), prbs[:, 0:8], AX,
                                                               ALU.add),
                           wait=[("ac", a_ex)], inc=("ve", 1))
            a_ri = pb.emit("scalar", lambda e: (
                e.activation(SC2(6), SC2(5), AF.Ln),
                e.activation(SC2(7), SC2(6), AF.Exp, scale=-1.0))[-1],
                wait=[("ve", v_sm)], inc=("ac", 1))
            v_pr = pb.emit("vector", lambda e: e.tensor_scalar(prbs[:, 0:8], prbs[:, 0:8],
                                                               SC2(7), None, ALU.mult),
                           wait=[("ac", a_ri)], inc=("ve", 1))
            prb2 = knk[:, 8:16]
            knk2 = knk[:, 16:24]
            a_cp = pb.emit("scalar", lambda e: e.copy(prb2, prbs[:, 0:8]),
                           wait=[("ve", v_pr)], inc=("ac", 1))
            v_t1 = pb.emit("vector", lambda e: e.tensor_reduce(SC2(8), prb2, AX, ALU.max),
                           wait=[("ac", a_cp)], inc=("ve", 1))
            a_t1 = pb.emit("scalar", lambda e: e.copy(SC2(9), SC2(8)),
                           wait=[("ve", v_t1)], inc=("ac", 1))
            v_kk = pb.emit("vector", lambda e: (
                e.tensor_scalar(mskq[:, 0:8], prb2, SC2(9), None, ALU.is_equal),
                e.select(knk[:, 0:8], mskq[:, 0:8], zeros8[:, 0:8], prb2))[-1],
                wait=[("ac", a_t1)], inc=("ve", 1))
            a_cp2 = pb.emit("scalar", lambda e: e.copy(knk2, knk[:, 0:8]),
                            wait=[("ve", v_kk)], inc=("ac", 1))
            v_t2 = pb.emit("vector", lambda e: e.tensor_reduce(SC2(10), knk2, AX, ALU.max),
                           wait=[("ac", a_cp2)], inc=("ve", 1))
            a_ng = pb.emit("scalar", lambda e: e.mul(SC2(11), SC2(10), -2.0),
                           wait=[("ve", v_t2)], inc=("ac", 1))
            a_sg = pb.emit("scalar", lambda e: e.activation(knk[:, 0:8], prb2,
                                                            AF.Sigmoid, bias=SC2(11),
                                                            scale=2.0), inc=("ac", 1))
            v_w = pb.emit("vector", lambda e: e.tensor_tensor(wn_f[:, 0:8], prb2,
                                                              knk[:, 0:8], ALU.mult),
                          wait=[("ac", a_sg)], inc=("ve", 1))
            w2b = knk[:, 24:32]
            a_ws = pb.emit("scalar", lambda e: e.activation(w2b, wn_f[:, 0:8], AF.Identity),
                           wait=[("ve", v_w)], inc=("ac", 1))
            v_ws2 = pb.emit("vector", lambda e: e.tensor_reduce(SC2(0), w2b, AX, ALU.add),
                            wait=[("ac", a_ws)], inc=("ve", 1))
            a_wi = pb.emit("scalar", lambda e: (
                e.activation(SC2(6), SC2(0), AF.Ln, bias=SC2(13)),
                e.activation(SC2(7), SC2(6), AF.Exp, scale=-1.0))[-1],
                wait=[("ve", v_ws2)], inc=("ac", 1))
            v_wn = pb.emit("vector", lambda e: e.tensor_scalar(wn_f[:, 0:8], w2b,
                                                               SC2(7), None, ALU.mult),
                           wait=[("ac", a_wi)], inc=("ve", 1))
            if a4 == 3:
                v_snp2 = pb.emit("vector", lambda e: (
                    e.tensor_copy(fav[:, 12104:12112], prbs[:, 0:8]),
                    e.tensor_copy(fav[:, 12112:12120], knk[:, 0:8]),
                    e.tensor_copy(fav[:, 12120:12128], w2b),
                    e.tensor_copy(fav[:, 12128:12136], knk2),
                    e.tensor_copy(fav[:, 12136:12137], SC2(5)),
                    e.tensor_copy(fav[:, 12137:12138], SC2(7)),
                    e.tensor_copy(fav[:, 12138:12139], SC2(8)),
                    e.tensor_copy(fav[:, 12139:12140], SC2(10)),
                    e.tensor_copy(fav[:, 12140:12141], SC2(12)),
                    e.tensor_copy(fav[:, 12141:12149], lgA[:, 24:32]),
                    e.tensor_copy(fav[:, 12149:12150], SC2(13)),
                    e.tensor_copy(fav[:, 12150:12151], SC2(0)),
                    e.tensor_copy(fav[:, 12151:12152], SC2(6)))[-1],
                    wait=[("ve", v_wn)], inc=("ve", 1))
                pb.dma("sync", b_dbgF[:, :], fav[:, 12104:12152], sem="st2",
                       wait=[("ve", v_snp2)])
            wnb = m2v[:, 3808:3816]
            v_wb = pb.emit("vector", lambda e: e.tensor_copy(wnb, wn_f[:, 0:8]),
                           wait=[("ve", v_wn)], inc=("ve", 1))
            s_wb = pb.dma("sync", wt_dram[a4, :, 0:8], wnb, sem="st2", wait=[("ve", v_wb)])
            wnT_bf = m2v[0:128, 3840 + a4 * 128:3840 + a4 * 128 + 128]
            t_wb = pb.dmat("sync", wnT_bf, wt_dram[a4, :, :],
                           sem="trd" if a4 % 2 == 0 else "tr2",
                           wait=[("st2", s_wb)])
            wn_done = pb.emit("vector", lambda e, a4=a4, wnT_bf=wnT_bf: e.tensor_copy(
                wnT[0:8, ts(a4, 128)], wnT_bf[0:8, :]),
                wait=[("trd" if a4 % 2 == 0 else "tr2", t_wb)], inc=("ve", 1))
        v_wf = pb.emit("vector", lambda e: e.tensor_copy(
            wfb2, wnT[0:8, :, None].broadcast_to([8, 512, 2])),
            wait=[("ve", wn_done)], inc=("ve", 1))
        pm_ev = 0
        for half in range(2):
            t_p = pb.emit("tensor", lambda e, half=half: e.matmul(
                pA[0:32, :], limes_bf2[0:8, 0:32], wfb2[0:8, 512 * half:512 * half + 512],
                start=True, stop=True),
                wait=[("ve", v_wf), ("cast", c_lm), ("ve", pm_ev)], inc=("pe", 1))
            pm_ev = pb.emit("vector", lambda e, half=half: e.tensor_tensor(
                zpb[0:32, ts(half, 512)], zf_f[0:32, ts(half, 512)], pA[0:32, 0:512],
                ALU.mult),
                wait=[("pe", t_p)], inc=("ve", 1))
        v_zp = pm_ev

        pb.dma("sync", b_dbgA[:, :], stat[:, :], sem="st", wait=[("ve", v_zp)])
        pb.dma("sync", b_dbgB[:, :], zf_f[0:32, 0:1024], sem="st", wait=[("ve", v_zp)])
        pb.dma("sync", b_dbgC[:, :], wnT[0:8, :], sem="st", wait=[("ve", v_zp)])
        pb.dma("sync", b_dbgD[:, :], lgA[:, :], sem="st", wait=[("ve", v_zp)])
        gst = {0: 0, 1: 0, "a": 0}
        hld = {0: 0, 1: 0}
        for t_i in range(8):
            buf = t_i % 2
            lh = pb.dma("sync", h_sl[:, ts(buf, 4096)], i_h[ts(t_i, 128), :],
                        sem="ld2" if buf == 0 else "ld3",
                        wait=[("ve", hld[buf])] if hld[buf] else ())
            last_gt = 0
            for oc in range(8):
                ps = pC if oc % 2 == 0 else pD
                t_g = pb.emit("tensor", lambda e, t_i=t_i, oc=oc, ps=ps: e.matmul(
                    ps[:], zpb[0:32, ts(t_i, 128)], bm_sb[0:32, ts(oc, 512)],
                    start=True, stop=True),
                    wait=[("ve", v_zp), ("ld", l_bm),
                          ("ld2" if buf == 0 else "ld3", lh),
                          ("ac", gst["a"])], inc=("pe", 1))
                v_gt = pb.emit("vector", lambda e, oc=oc, buf=buf, ps=ps: e.scalar_tensor_tensor(
                    t1tmp[:, 0:512], ps[:], 0.5,
                    h_sl[:, buf * 4096 + oc * 512:buf * 4096 + oc * 512 + 512],
                    ALU.mult, ALU.add),
                    wait=[("pe", t_g), ("ac", gst["a"])], inc=("ve", 1))
                a_gs = pb.emit("scalar", lambda e: e.activation(gsil[:, 0:512],
                                                                t1tmp[:, 0:512], AF.Silu),
                               wait=[("ve", v_gt), ("st", gst[buf])], inc=("ac", 1))
                gst["a"] = a_gs
                gst[buf] = pb.dma("sync", g_dram[ts(t_i, 128), ts(oc, 512)], gsil[:, 0:512],
                                  sem="st", wait=[("ac", a_gs)])
                last_gt = v_gt
            hld[buf] = last_gt
        g_pe = pb.cnt["pe"]
        g_done_st = pb.cnt["st"]
        g_done_ve = pb.cnt["ve"]
        g_done_ac = pb.cnt["ac"]

        v_am = pb.emit("vector", lambda e: e.tensor_scalar(gA8, gA8, cm8[0:8, 0:1], None,
                                                           ALU.mult),
                       wait=[("ld", lds), ("ve", g_done_ve)], inc=("ve", 1))
        t_sf = pb.emit("tensor", lambda e: e.matmul(pR[0:8, 384:512], us8[0:8, 0:8], gA8,
                                                    start=True, stop=True,
                                                    skip_group_check=True),
                       wait=[("ve", v_am)], inc=("pe", 1))
        a_we = pb.emit("scalar", lambda e: e.activation(gA8, pR[0:8, 384:512], AF.Exp),
                       wait=[("pe", t_sf), ("ac", g_done_ac)], inc=("ac", 1))
        v_wm = pb.emit("vector", lambda e: e.tensor_scalar(gA8, gA8, cm8[0:8, 0:1], None,
                                                           ALU.mult),
                       wait=[("ac", a_we)], inc=("ve", 1))
        rc_ev = 0
        for m in range(W):
            l_fm = pb.dma("sync", fm_f, g_f[ts(m, 128), :], sem="ld2",
                          wait=[("ve", rc_ev if rc_ev else g_done_ve)])
            tc = pb.emit("tensor", lambda e, m=m: e.matmul(
                pR[0:1, 384:512], ident_f[0:8, m:m + 1], gA8, start=True, stop=True,
                skip_group_check=True),
                wait=[("ve", v_wm), ("ve", rc_ev)], inc=("pe", 1))
            ec = pb.emit("vector", lambda e: e.tensor_copy(earow[0:1, 0:128],
                                                           pR[0:1, 384:512]),
                         wait=[("pe", tc)], inc=("ve", 1))
            bc = pb.emit("tensor", lambda e: e.matmul(pR[:, 256:384], ones_f[0:1, :],
                                                      earow[0:1, 0:128], start=True,
                                                      stop=True, skip_group_check=True),
                         wait=[("ve", ec)], inc=("pe", 1))
            e2 = pb.emit("vector", lambda e: e.tensor_copy(eacol[:, :], pR[:, 256:384]),
                         wait=[("pe", bc)], inc=("ve", 1))
            if m == 0:
                rc_ev = pb.emit("vector", lambda e: e.tensor_tensor(
                    Rf[:, :], fm_f[:, :], eacol[:, :, None].broadcast_to([128, 128, PD]),
                    ALU.mult), wait=[("ve", e2), ("ld2", l_fm)], inc=("ve", 1))
            else:
                rc_ev = pb.emit("vector", lambda e: (
                    e.tensor_tensor(tmp_cmb[:, 0:8192], fm_f[:, :],
                                    eacol[:, :, None].broadcast_to([128, 128, PD]), ALU.mult),
                    e.tensor_tensor(Rf[:, :], Rf[:, :], tmp_cmb[:, 0:8192], ALU.add))[-1],
                    wait=[("ve", e2), ("ld2", l_fm), ("pe", g_pe)], inc=("ve", 1))

        y_pe_prev = 0
        evslot_ev = {0: 0, 1: 0}
        x1_st = {0: 0, 1: 0}
        t1_ev = 0
        yt_ev = 0
        for c in range(NCH):
            cs = slice(c * 256, c * 256 + 256)
            l_y = pb.dma("sync", y_sl.rearrange("p (a d) -> p a d", a=2),
                         i_y1[cs, :].rearrange("(a p) d -> p a d", p=128), sem="ld",
                         wait=[("pe", y_pe_prev), ("ve", rc_ev)])
            l_c = pb.dma("sync", cT.rearrange("p (k t) -> p k t", k=8),
                         i_xbc.rearrange("(k p) t -> p k t", p=128)[:, 72:80, cs], sem="ld",
                         wait=[("pe", y_pe_prev)])
            l_s = pb.dma("sync", s_sl, i_s[ts(c, 128), :], sem="ld",
                         wait=[("ve", rc_ev), ("pe", g_pe)])
            l_a = pb.dma("sync", acst_sl.rearrange("p (a h) -> p a h", a=2),
                         i_acst[cs, :].rearrange("(a p) h -> p a h", p=128), sem="ld",
                         wait=[("ac", g_done_ac if c == 0 else 0)])
            l_z2 = pb.dma("sync", z_sl.rearrange("p (a d) -> p a d", a=2),
                          i_z[cs, :].rearrange("(a p) d -> p a d", p=128), sem="ld",
                          wait=[("ve", rc_ev)])
            ld_all = l_z2
            a_od = pb.emit("scalar", lambda e: e.activation(odT[:, 0:256], acst_sl[:, 0:256],
                                                            AF.Exp),
                           wait=[("ld", ld_all), ("ve", t1_ev)], inc=("ac", 1))
            v_rb = pb.emit("vector", lambda e: e.tensor_copy(r_bf[:, :], Rf[:, :]),
                           wait=[("ve", rc_ev), ("pe", y_pe_prev)], inc=("ve", 1))
            for hg in range(16):
                gi = hg // 2
                for hm in range(8):
                    h = hg * 8 + hm
                    for lt in range(2):
                        psx = pA if lt == 0 else pB
                        pb.emit("tensor", lambda e, gi=gi, lt=lt, h=h, hm=hm, psx=psx: e.matmul(
                            psx[:, hm * 64:hm * 64 + 64],
                            cT[:, gi * 256 + lt * 128:gi * 256 + lt * 128 + 128],
                            r_bf[:, h * 64:h * 64 + 64],
                            start=(hm == 0), stop=False, skip_group_check=(hm != 0)),
                            wait=([("ld", ld_all), ("ve", v_rb), ("ve", t1_ev)]
                                  if hm == 0 else ()),
                            inc=("pe", 1) if (hm == 7 and lt == 1) else None)
                mmd = pb.cnt["pe"]

                def t1add(e, hg=hg):
                    e.tensor_tensor(t1tmp[:, 0:512], pA[:],
                                    odT[:, hg * 8:hg * 8 + 8][:, :, None]
                                    .broadcast_to([128, 8, PD]), ALU.mult)
                    e.tensor_tensor(y_sl[:, hg * 512:hg * 512 + 512], t1tmp[:, 0:512],
                                    y_sl[:, hg * 512:hg * 512 + 512], ALU.add)
                    e.tensor_tensor(t1tmp[:, 0:512], pB[:],
                                    odT[:, 128 + hg * 8:128 + hg * 8 + 8][:, :, None]
                                    .broadcast_to([128, 8, PD]), ALU.mult)
                    return e.tensor_tensor(y_sl[:, 8192 + hg * 512:8192 + hg * 512 + 512],
                                           t1tmp[:, 0:512],
                                           y_sl[:, 8192 + hg * 512:8192 + hg * 512 + 512],
                                           ALU.add)
                t1_ev = pb.emit("vector", t1add,
                                wait=[("pe", mmd), ("ld", ld_all), ("ac", a_od)],
                                inc=("ve", 1))
            g_ev2 = t1_ev
            for st_i in range(2):
                ysl = y_sl[:, st_i * 8192:st_i * 8192 + 8192]
                zsl = z_sl[:, st_i * 8192:st_i * 8192 + 8192]
                a_sz = pb.emit("scalar", lambda e, zsl=zsl: e.activation(zsl, zsl, AF.Silu),
                               wait=[("ld", ld_all), ("ve", g_ev2)], inc=("ac", 1))
                v_g = pb.emit("vector", lambda e, ysl=ysl, zsl=zsl: e.tensor_tensor(
                    ysl, ysl, zsl, ALU.mult), wait=[("ac", a_sz)], inc=("ve", 1))
                a_sq = pb.emit("scalar", lambda e, ysl=ysl, st_i=st_i: e.activation(
                    sq_bf[:, st_i * 8192:st_i * 8192 + 8192], ysl, AF.Square),
                    wait=[("ve", v_g), ("pe", y_pe_prev), ("ve", yt_ev)], inc=("ac", 1))
                v_v = pb.emit("vector", lambda e, st_i=st_i: e.tensor_reduce(
                    SC2(5 + st_i), sq_bf[:, st_i * 8192:st_i * 8192 + 8192], AX, ALU.add),
                    wait=[("ac", a_sq)], inc=("ve", 1))
                a_rs = pb.emit("scalar", lambda e, st_i=st_i: (
                    e.activation(SC2(7 + st_i), SC2(5 + st_i), AF.Ln, bias=SC2(15),
                                 scale=1.0 / DI),
                    e.activation(SC2(9 + st_i), SC2(7 + st_i), AF.Exp, scale=-0.5))[-1],
                    wait=[("ve", v_v)], inc=("ac", 1))
                g_ev2 = pb.emit("vector", lambda e, ysl=ysl, st_i=st_i: e.tensor_scalar(
                    ysl, ysl, SC2(9 + st_i), None, ALU.mult),
                    wait=[("ac", a_rs)], inc=("ve", 1))
            yt_ev = 0
            for kb in range(64):
                t_y = pb.emit("tensor", lambda e, kb=kb: (
                    e.transpose(pT[:, 0:128], y_sl[:, kb * 128:kb * 128 + 128], ident_bf[:]),
                    e.transpose(pT[:, 128:256],
                                y_sl[:, 8192 + kb * 128:8192 + kb * 128 + 128],
                                ident_bf[:]))[-1],
                    wait=[("ve", g_ev2), ("ve", yt_ev)], inc=("pe", 1))
                yt_ev = pb.emit("vector", lambda e, kb=kb: e.tensor_scalar(
                    yT[:, kb * 256:kb * 256 + 256], pT[:, 0:256], nw_l[:, kb:kb + 1],
                    None, ALU.mult),
                    wait=[("pe", t_y)], inc=("ve", 1))
            wo_pe = {0: 0, 1: 0}
            for oc in range(8):
                for k in range(64):
                    kp = k % 2
                    tsem = "trd" if kp == 0 else "tr2"
                    wbuf = wob[:, ts(kp, 512)]
                    wo_tr = pb.dmat("sync", wbuf, wo_bf[ts(oc, 512), ts(k, 128)], sem=tsem,
                                    wait=([("pe", wo_pe[kp])] if wo_pe[kp] else
                                          [("cast", cast_wo[oc]), ("ve", yt_ev)]))
                    for tt2 in range(2):
                        psx = pC if tt2 == 0 else pD
                        pb.emit("tensor", lambda e, k=k, tt2=tt2, psx=psx, wbuf=wbuf: e.matmul(
                            psx[:], yT[:, k * 256 + tt2 * 128:k * 256 + tt2 * 128 + 128],
                            wbuf, start=(k == 0), stop=(k == 63)),
                            wait=([(tsem, wo_tr)] + ([("ve", evslot_ev[0]),
                                                      ("ve", evslot_ev[1])]
                                                     if k == 0 else [])) if tt2 == 0 else (),
                            inc=("pe", 1) if (tt2 == 1) else None)
                    wo_pe[kp] = pb.cnt["pe"]
                for tt2 in range(2):
                    psx = pC if tt2 == 0 else pD
                    buf = tt2
                    row = slice(c * 256 + tt2 * 128, c * 256 + tt2 * 128 + 128)
                    lsem = "ld2" if buf == 0 else "ld3"
                    l_xn = pb.dma("sync", slot(buf, "xn"), i_xn[row, ts(oc, 512)], sem=lsem,
                                  wait=[("ve", evslot_ev[buf])] if evslot_ev[buf] else ())
                    l_g = pb.dma("sync", slot(buf, "g"), g_dram[row, ts(oc, 512)], sem=lsem,
                                 wait=[("st", g_done_st)] if (c == 0 and oc == 0) else ())
                    l_x = pb.dma("sync", slot(buf, "xin"), x_main[row, ts(oc, 512)],
                                 sem=lsem)
                    mm_d = pb.cnt["pe"]

                    def sevac(e, psx=psx, buf=buf):
                        e.tensor_tensor(psx[:], psx[:], slot(buf, "xn"), ALU.add)
                        e.tensor_tensor(psx[:], psx[:], slot(buf, "g"), ALU.mult)
                        return e.tensor_tensor(slot(buf, "x1"), psx[:], slot(buf, "xin"),
                                               ALU.add)
                    ev = pb.emit("vector", sevac,
                                 wait=[("pe", mm_d), (lsem, l_x),
                                       ("st", x1_st[buf])], inc=("ve", 1))
                    evslot_ev[buf] = ev
                    x1_st[buf] = pb.dma("sync", b_x1[row, ts(oc, 512)], slot(buf, "x1"),
                                        sem="st", wait=[("ve", ev)])
            if c < NCH - 1:
                a_ea = pb.emit("scalar", lambda e, c=c: e.activation(
                    SC2(11), acs_sb[:, c * 256 + 255:c * 256 + 256], AF.Exp),
                    wait=[("ld", lds)], inc=("ac", 1))
                t_e = pb.emit("tensor", lambda e: e.transpose(pR[0:1, 384:512], SC2(11),
                                                              ident_f[:]),
                              wait=[("ac", a_ea)], inc=("pe", 1))
                e_e = pb.emit("vector", lambda e: e.tensor_copy(earow[0:1, 0:128],
                                                                pR[0:1, 384:512]),
                              wait=[("pe", t_e)], inc=("ve", 1))
                b_e = pb.emit("tensor", lambda e: e.matmul(pR[:, 256:384], ones_f[0:1, :],
                                                           earow[0:1, 0:128], start=True,
                                                           stop=True, skip_group_check=True),
                              wait=[("ve", e_e)], inc=("pe", 1))
                e2_e = pb.emit("vector", lambda e: e.tensor_copy(eacol[:, :], pR[:, 256:384]),
                               wait=[("pe", b_e)], inc=("ve", 1))
                rc_ev = pb.emit("vector", lambda e: (
                    e.tensor_tensor(Rf[:, :], Rf[:, :],
                                    eacol[:, :, None].broadcast_to([128, 128, PD]), ALU.mult),
                    e.tensor_tensor(Rf[:, :], Rf[:, :], s_sl[:, :], ALU.add))[-1],
                    wait=[("ve", e2_e), ("ld", ld_all), ("ve", v_rb)], inc=("ve", 1))
            y_pe_prev = pb.cnt["pe"]

        pb.finish()
        pb.run(block)
    return nc

def build_launch3():
    nc = bass.Bass("TRN2", target_bir_lowering=False)
    dI = lambda n, s, d=F32: nc.dram_tensor(n, s, d, kind="ExternalInput").ap()
    dO = lambda n, s, d=F32: nc.dram_tensor(n, s, d, kind="ExternalOutput").ap()
    dT = lambda n, s, d=BF16: nc.dram_tensor(n, s, d, kind="Internal").ap()

    i_x1 = dI("i_x1", [T, D])
    x1_halo2 = dI("x1_halo2", [2, D])
    cw3 = dI("cw3", [3, D, 3072])
    cb3 = dI("cb3", [3072])
    wq3 = dI("wq3", [1024, 1024])
    bq3c = dI("bq3c", [128, 8])
    wk3 = dI("wk3", [1024, 1024])
    bk3c = dI("bk3c", [128, 8])
    wv3 = dI("wv3", [1024, 1024])
    bv3 = dI("bv3", [1024])
    wo3 = dI("wo3", [1024, 1024])
    bo3c = dI("bo3c", [128, 8])
    pw3 = dI("pw3", [1024, D])
    pb3 = dI("pb3", [D])
    aw3 = dI("aw3", [512, D])
    bw3 = dI("bw3", [D, 512])
    wqr3 = dI("wqr3", [16, D])
    keys3 = dI("keys3", [64, 16])
    cst = {n: dI(n, s, d) for n, s, d in CONST_SPECS}

    o_x2 = dO("o_x2", [T, D])
    d3C = dO("d3C", [128, 192])

    cw_bf = [dT(f"cw_bf{w}", [D, 3072]) for w in range(3)]
    wq_bfd = dT("wq_bfd3", [1024, 1024])
    wk_bfd = dT("wk_bfd3", [1024, 1024])
    wv_bfd = dT("wv_bfd3", [1024, 1024])
    wo_bfd = dT("wo_bfd3", [1024, 1024])
    pw_bfd = dT("pw_bfd3", [1024, D])
    aw_bfd = dT("aw_bfd3", [512, D])
    bw_bfd = dT("bw_bfd3", [D, 512])
    wqr_bfd = dT("wqr_bfd3", [16, D])
    keys_bfd = dT("keys_bfd3", [64, 128])
    cb_bfd = dT("cb_bfd3", [1, 3072])
    bv_bfd = dT("bv_bfd3", [1, 1024])
    pb_bfd = dT("pb_bfd3", [1, D])
    hd_dram = dT("hd_dram3", [16, D])
    g1_dram = dT("g1_dram3", [T, D])

    with contextlib.ExitStack() as xs:
        E = xs.enter_context
        a3 = E(nc.sbuf_tensor("L3a", [128, 32832], BF16))
        b3 = E(nc.sbuf_tensor("L3b", [128, 32768], BF16))
        c3 = E(nc.sbuf_tensor("L3c", [128, 12288], BF16))
        d3 = E(nc.sbuf_tensor("L3d", [128, 22784], BF16))
        f3 = E(nc.sbuf_tensor("L3f", [128, 1536], F32))
        pA = E(nc.psum_tensor("r3A", [128, 512], F32))
        pB = E(nc.psum_tensor("r3B", [128, 512], F32))
        pC = E(nc.psum_tensor("r3C", [128, 512], F32))
        pD = E(nc.psum_tensor("r3D", [128, 512], F32))
        pE2 = E(nc.psum_tensor("r3E", [128, 512], F32))
        pF2 = E(nc.psum_tensor("r3F", [128, 512], F32))
        pG = E(nc.psum_tensor("r3G", [128, 512], F32))
        pTb = E(nc.psum_tensor("r3T", [128, 256], BF16))
        block = E(nc.Block())
        pb = PB(nc)
        for s in ["cast", "trd", "tr2", "ld", "ld2", "ld3", "pe", "ve", "ac", "gp",
                  "st", "st2", "stg"]:
            pb.new_sem(E(nc.semaphore("w_" + s)), s)

        av, bv_, cv_, dv, fv = a3[:], b3[:], c3[:], d3[:], f3[:]
        xn1T = av[:, 0:32832]
        wbuf = [bv_[:, 0:4096], bv_[:, 4096:8192]]
        wq_sb = bv_[:, 0:8192]
        wk_sb = bv_[:, 8192:16384]
        wv_sb = bv_[:, 16384:24576]
        wo_sb = bv_[:, 24576:32768]
        awT_sb = bv_[:, 0:16384]
        bwT_sb = bv_[:, 16384:32768]
        pw_sb = bv_[:, 0:32768]
        xld3 = cv_[:, 0:4096].bitcast(F32)
        sqs3 = cv_[:, 4096:8192].bitcast(F32)
        xn_tok = cv_[:, 8192:12288]
        cvt = [cv_[:, i * 3072:(i + 1) * 3072] for i in range(4)]
        zT_f = cv_[:, 0:1024].bitcast(F32)
        x1c_f = [cv_[:, 1024 + j * 1024:1024 + j * 1024 + 1024].bitcast(F32)
                 for j in range(2)]
        x2_f = [cv_[:, 3072 + j * 1024:3072 + j * 1024 + 1024].bitcast(F32)
                for j in range(2)]
        g1c = [cv_[:, 5120 + j * 512:5120 + j * 512 + 512] for j in range(2)]
        ident_bf = dv[:, 0:128]
        ones_bf = dv[0:1, 128:256]
        o2T_all = dv[:, 256:8448]
        qhT_sb = dv[:, 8448:9472]
        khT_sb = dv[:, 9472:10496]
        vh_sb = dv[:, 10496:11520]
        p_sb = dv[:, 11520:11648]
        pT_sb = dv[:, 11648:11776]
        brows = dv
        cbrow = dv[0:1, 11776:14848]
        bvrow = dv[0:1, 14848:15872]
        pbrow = dv[0:1, 18688:22784]
        keysT_sb = dv[:, 15872:15936]
        wqT_sb = dv[:, 15936:16448]
        gateT_bf = dv[:, 16448:16576]
        zgT_sb = dv[:, 16576:17088]
        oT_tmp = dv[:, 17088:18112]
        qT_bf = dv[:, 18112:18240]
        hview = dv[:, 18240:18256]
        msk3 = dv[:, 18256:18288].bitcast(U8)
        eexp_sb = dv[:, 18288:18800 + 0] if False else dv[:, 18176:18176]
        eexp_sb = dv[:, 18176:18112] if False else None
        eexp_sb = dv[:, 18112:18624]
        qT_bf = dv[:, 18624:18688 - 64 + 64] if False else None
        qT_bf = dv[:, 18640:18688]
        oT_tmp = dv[:, 17088:18112]
        eexp_sb = dv[:, 18112:18624]
        qT_bf = dv[:, 18624:18688 - 48 + 88] if False else None
        qT_bf = dv[:, 15936 - 192:15936 - 64] if False else None
        qT_bf = dv[:, 18624:18688]
        qT_bf = dv[:, 11520:11648]
        mask_f = fv[:, 0:128]
        att_sc = fv[:, 128:256]
        ex_f = fv[:, 256:384]
        stat3 = fv[:, 384:448]
        ffn_sc = fv[:, 448:512]
        gate_f = fv[:, 512:576]
        ninf_f = fv[:, 576:640]
        zro64 = fv[:, 640:704]
        bq_sc = fv[:, 704:712]
        bk_cc = fv[:, 712:720]
        bo_cc = fv[:, 720:728]
        wk_f = fv[:, 728:792]
        ident_f = fv[:, 768 + 32:1024 - 64] if False else None
        ident_f = fv[:, 832:960]
        ones_f = fv[:, 960:1088]
        bq_raw = fv[:, 1088:1096]
        wk2_f = fv[:, 1096:1160]
        spare = fv[:, 1160:1536]

        def SC3(i, rows=128):
            return stat3[0:rows, 4 * i:4 * i + 1]

        cast_cw = [pb.dma("gpsimd", cw_bf[w][:, :], cw3[w, :, :], sem="cast")
                   for w in range(3)]
        c_small = [
            pb.dma("gpsimd", wq_bfd[:, :], wq3[:, :], sem="cast"),
            pb.dma("gpsimd", wk_bfd[:, :], wk3[:, :], sem="cast"),
            pb.dma("gpsimd", wv_bfd[:, :], wv3[:, :], sem="cast"),
            pb.dma("gpsimd", wo_bfd[:, :], wo3[:, :], sem="cast"),
            pb.dma("gpsimd", pw_bfd[:, :], pw3[:, :], sem="cast"),
            pb.dma("gpsimd", aw_bfd[:, :], aw3[:, :], sem="cast"),
            pb.dma("gpsimd", bw_bfd[:, :], bw3[:, :], sem="cast"),
            pb.dma("gpsimd", wqr_bfd[:, :], wqr3[:, :], sem="cast"),
            pb.dma("gpsimd", keys_bfd[0:64, 0:16], keys3[:, :], sem="cast"),
            pb.dma("gpsimd", cb_bfd[:, :], cb3[None, :], sem="cast"),
            pb.dma("gpsimd", bv_bfd[:, :], bv3[None, :], sem="cast"),
            pb.dma("gpsimd", pb_bfd[:, :], pb3[None, :], sem="cast"),
        ]
        cast_all = max(cast_cw + c_small)

        ld0 = [
            pb.dma("sync", ident_bf, cst["c_ident_bf"]),
            pb.dma("sync", ones_bf, cst["c_ones_bf"]),
            pb.dma("sync", mask_f, cst["c_mask128"]),
            pb.dma("sync", ninf_f, cst["c_ninf"][0:1, :].broadcast_to([128, 64])),
            pb.dma("sync", zro64, cst["c_zeros"][:, 0:64].broadcast_to([128, 64])),
            pb.dma("sync", eexp_sb[0:64, :], cst["c_eexp"]),
            pb.dma("sync", ident_f, cst["c_ident_f32"]),
            pb.dma("sync", ones_f[0:1, :], cst["c_ones_f32"]),
            pb.dma("sync", SC3(15, 1), cst["c_eps"][0:1, 0:1]),
            pb.dma("sync", bq_raw, bq3c[:, :]),
            pb.dma("sync", bk_cc, bk3c[:, :]),
            pb.dma("sync", bo_cc, bo3c[:, :]),
        ]
        lds = max(ld0)
        t_eps = pb.emit("tensor", lambda e: e.matmul(
            pG[0:128, 24:25], ones_f[0:1, 0:128], SC3(15, 1),
            start=True, stop=True, skip_group_check=True),
            wait=[("ld", lds)], inc=("pe", 1))
        v_eps = pb.emit("vector", lambda e: e.tensor_copy(SC3(14), pG[0:128, 24:25]),
                        wait=[("pe", t_eps), ("ld", lds)], inc=("ve", 1))
        v_bq = pb.emit("vector", lambda e: e.tensor_scalar(bq_sc, bq_raw, 0.125, None,
                                                           ALU.mult),
                       wait=[("ve", v_eps)], inc=("ve", 1))

        ld1 = [
            pb.dma("sync", cbrow, cb_bfd[:, :], wait=[("cast", cast_all)]),
            pb.dma("sync", bvrow, bv_bfd[:, :]),
            pb.dma("sync", pbrow, pb_bfd[:, :]),
        ]

        ln_prev = {0: 0}

        def emit_ln3(rows, src, dst_fn):
            a1 = v1 = None
            for hh in range(2):
                ldx = pb.dma("sync", xld3[0:rows, :], src[:, ts(hh, 2048)],
                             wait=[("ac", a1)] if a1 else
                             ([("ve", ln_prev[0])] if ln_prev[0] else [("ve", v_bq)]))
                v1 = pb.emit("vector", lambda e, rows=rows, hh=hh: e.tensor_reduce(
                    SC3(8 + hh, rows), xld3[0:rows, :], AX, ALU.add),
                    wait=[("ld", ldx)], inc=("ve", 1))
                a1 = pb.emit("scalar", lambda e, rows=rows: e.activation(
                    sqs3[0:rows, :], xld3[0:rows, :], AF.Square),
                    wait=[("ld", ldx), ("ve", v1)], inc=("ac", 1))
                v1 = pb.emit("vector", lambda e, rows=rows, hh=hh: e.tensor_reduce(
                    SC3(10 + hh, rows), sqs3[0:rows, :], AX, ALU.add),
                    wait=[("ac", a1)], inc=("ve", 1))
            s_sum = pb.emit("scalar", lambda e, rows=rows: e.add(
                SC3(13, rows), SC3(8, rows), SC3(9, rows)),
                wait=[("ve", v1)], inc=("ac", 1))
            p_s2 = pb.emit("gpsimd", lambda e, rows=rows: e.tensor_tensor(
                SC3(12, rows), SC3(10, rows), SC3(11, rows), ALU.add),
                wait=[("ve", v1)], inc=("gp", 1))
            v_mean = pb.emit("vector", lambda e, rows=rows: e.tensor_scalar(
                SC3(2, rows), SC3(13, rows), 1.0 / D, None, ALU.mult),
                wait=[("ac", s_sum)], inc=("ve", 1))
            a_m2 = pb.emit("scalar", lambda e, rows=rows: e.activation(
                SC3(3, rows), SC3(2, rows), AF.Square),
                wait=[("ve", v_mean)], inc=("ac", 1))
            v2 = pb.emit("vector", lambda e, rows=rows: e.tensor_scalar(
                SC3(5, rows), SC3(12, rows), 1.0 / D, SC3(3, rows),
                ALU.mult, ALU.subtract),
                wait=[("gp", p_s2), ("ac", a_m2)], inc=("ve", 1))
            a2 = pb.emit("scalar", lambda e, rows=rows: (
                e.activation(SC3(6, rows), SC3(5, rows), AF.Ln, bias=SC3(14, rows)),
                e.activation(SC3(7, rows), SC3(6, rows), AF.Exp, scale=-0.5))[-1],
                wait=[("ve", v2)], inc=("ac", 1))
            v3 = pb.emit("vector", lambda e, rows=rows: e.tensor_scalar(
                xn_tok[0:rows, 2048:4096], xld3[0:rows, :], SC3(2, rows), SC3(7, rows),
                ALU.subtract, ALU.mult),
                wait=[("ac", a2)], inc=("ve", 1))
            ldx0 = pb.dma("sync", xld3[0:rows, :], src[:, 0:2048], wait=[("ve", v3)])
            v4 = pb.emit("vector", lambda e, rows=rows: e.tensor_scalar(
                xn_tok[0:rows, 0:2048], xld3[0:rows, :], SC3(2, rows), SC3(7, rows),
                ALU.subtract, ALU.mult),
                wait=[("ld", ldx0)], inc=("ve", 1))
            ln_prev[0] = dst_fn(v4)

        def main_dst(t_i):
            def fn(v4):
                last = v4
                for c2 in range(16):
                    pv = pb.emit("tensor", lambda e, c2=c2: (
                        e.transpose(pTb[:, 0:128], xn_tok[:, ts(2 * c2, 128)], ident_bf),
                        e.transpose(pTb[:, 128:256], xn_tok[:, ts(2 * c2 + 1, 128)],
                                    ident_bf))[-1],
                        wait=[("ve", last)], inc=("pe", 1))
                    last = pb.emit("vector", lambda e, c2=c2, t_i=t_i: (
                        e.tensor_copy(xn1T[:, (2 * c2) * 1026 + 1 + t_i * 128:
                                           (2 * c2) * 1026 + 1 + t_i * 128 + 128],
                                      pTb[:, 0:128]),
                        e.tensor_copy(xn1T[:, (2 * c2 + 1) * 1026 + 1 + t_i * 128:
                                           (2 * c2 + 1) * 1026 + 1 + t_i * 128 + 128],
                                      pTb[:, 128:256]))[-1],
                        wait=[("pe", pv)], inc=("ve", 1))
                    pb.wait("tensor", [("ve", last)])
                return last
            return fn

        def halo_dst(v4):
            sh = pb.dma("sync", hd_dram[0:2, :], xn_tok[0:2, :], sem="st2",
                        wait=[("ve", v4)])
            last = 0
            for kt in range(32):
                tdh = pb.dmat("sync", hview, hd_dram[:, ts(kt, 128)],
                              sem="trd" if kt % 2 == 0 else "tr2",
                              wait=([("st2", sh)] if kt == 0 else []) +
                                   ([("ve", last)] if last else []))
                last = pb.emit("vector", lambda e, kt=kt: (
                    e.tensor_copy(xn1T[:, kt * 1026:kt * 1026 + 1], hview[:, 0:1]),
                    e.tensor_copy(xn1T[:, kt * 1026 + 1025:kt * 1026 + 1026],
                                  hview[:, 1:2]))[-1],
                    wait=[("trd" if kt % 2 == 0 else "tr2", tdh)], inc=("ve", 1))
            return last

        for t_i in range(8):
            emit_ln3(128, i_x1[ts(t_i, 128), :], main_dst(t_i))
        emit_ln3(2, x1_halo2[:, :], halo_dst)
        xn1T_rdy = pb.cnt["ve"]

        tkq = pb.dmat("sync", keysT_sb, keys_bfd[:, :], sem="trd",
                      wait=[("cast", cast_all), ("ve", xn1T_rdy)])
        last_t = tkq
        for kt in range(32):
            last_t = pb.dmat("sync", wqT_sb[:, ts(kt, 16)], wqr_bfd[:, ts(kt, 128)],
                             sem="tr2" if kt % 2 == 0 else "trd",
                             wait=[("trd", tkq)] if kt == 0 else ())
        wqT_done_trd = pb.cnt["trd"]
        wqT_done_tr2 = pb.cnt["tr2"]

        banks = [pA, pB, pC, pD]
        bank_ev = [0, 0, 0, 0]
        wb_pe = [0, 0]
        qkv_loaded = {0: 0}
        att_done = {0: 0}
        o2_done = [0] * 8

        def conv_group(G):
            cvdone = [0, 0, 0, 0]
            for c in range(6):
                first_mm = {}
                for w in range(3):
                    for kg in range(4):
                        idx = c * 12 + w * 4 + kg
                        buf = idx % 2
                        sem = "ld2" if buf == 0 else "ld3"
                        wld = pb.dma("sync", wbuf[buf].rearrange("p (a n) -> p a n", a=8),
                                     cw_bf[w].rearrange("(a p) n -> p a n", p=128)
                                     [:, ts2(kg * 8, 8), ts(c, 512)],
                                     sem=sem,
                                     wait=([("pe", wb_pe[buf])] if wb_pe[buf] else
                                           [("cast", cast_all)]))
                        mmlast = 0
                        for i4 in range(4):
                            i = G * 4 + i4
                            def mms(e, i4=i4, i=i, kg=kg, w=w, c=c, buf=buf):
                                r = None
                                for k8 in range(8):
                                    kt = kg * 8 + k8
                                    st = (w == 0 and kg == 0 and k8 == 0)
                                    r = e.matmul(
                                        banks[i4][:, 0:512],
                                        xn1T[:, kt * 1026 + i * 128 + w:
                                             kt * 1026 + i * 128 + w + 128],
                                        wbuf[buf][:, ts(k8, 512)],
                                        start=st, stop=False,
                                        skip_group_check=True)
                                return r
                            wl = [(sem, wld)]
                            if w == 0 and kg == 0:
                                wl.append(("ve", bank_ev[i4]))
                                if G == 0 and c == 0:
                                    wl.append(("ve", xn1T_rdy))
                                if G == 1 and c == 0:
                                    wl.append(("ve", att_done[0]))
                            mmlast = pb.emit("tensor", mms, wait=wl, inc=("pe", 1))
                        wb_pe[buf] = mmlast
                for i4 in range(4):
                    bmm = pb.emit("tensor", lambda e, i4=i4, c=c: e.matmul(
                        banks[i4][:, 0:512], ones_bf[0:1, 0:128],
                        cbrow[0:1, ts(c, 512)], start=False, stop=True,
                        skip_group_check=True),
                        wait=[("ld", max(ld1))], inc=("pe", 1))
                    bank_ev[i4] = pb.emit("vector", lambda e, i4=i4, c=c: e.tensor_copy(
                        cvt[i4][:, ts(c, 512)], banks[i4][:, 0:512]),
                        wait=[("pe", bmm)], inc=("ve", 1))
                    cvdone[i4] = bank_ev[i4]
            return cvdone

        def ts2(a, n):
            return slice(a, a + n)

        def attn_tile(G, i4, cvdone):
            i = G * 4 + i4
            cv_i = cvt[i4]
            last = cvdone
            for j in range(24):
                pv = pb.emit("tensor", lambda e, j=j: e.transpose(
                    pTb[:, 0:128], cv_i[:, ts(j, 128)], ident_bf),
                    wait=[("ve", last)], inc=("pe", 1))
                last = pb.emit("vector", lambda e, j=j: e.tensor_copy(
                    cv_i[:, ts(j, 128)], pTb[:, 0:128]),
                    wait=[("pe", pv)], inc=("ve", 1))
                pb.wait("tensor", [("ve", last)])
            cvT_done = last
            qk_ev = {0: 0}
            for pr in range(8):
                for (wsb, dst, bias_ap, scl, ps) in (
                        (wq_sb, qhT_sb, bq_sc, 0.125, pE2),
                        (wk_sb, khT_sb, bk_cc, 1.0, pF2)):
                    def qmm(e, pr=pr, wsb=wsb, ps=ps):
                        r = None
                        for kt in range(8):
                            r = e.matmul(ps[:, 0:128],
                                         wsb[:, kt * 1024 + pr * 128:
                                             kt * 1024 + pr * 128 + 128],
                                         cv_i[:, ts(kt, 128)],
                                         start=(kt == 0), stop=(kt == 7),
                                         skip_group_check=True)
                        return r
                    tq = pb.emit("tensor", qmm,
                                 wait=[("ve", cvT_done), ("ld", qkv_loaded[0]),
                                       ("ac", qk_ev[0])], inc=("pe", 1))
                    qk_ev[0] = pb.emit("scalar",
                                       lambda e, pr=pr, dst=dst, bias_ap=bias_ap,
                                       scl=scl, ps=ps: e.activation(
                                           dst[:, ts(pr, 128)], ps[:, 0:128],
                                           AF.Identity,
                                           bias=bias_ap[:, pr:pr + 1], scale=scl),
                                       wait=[("pe", tq)], inc=("ac", 1))
            for nh in range(2):
                def vmm(e, nh=nh):
                    for kt in range(8):
                        e.matmul(pG[:, 0:512],
                                 cv_i[:, ts(kt, 128)],
                                 wv_sb[:, kt * 1024 + nh * 512:
                                       kt * 1024 + nh * 512 + 512],
                                 start=(kt == 0), stop=False,
                                 skip_group_check=True)
                    return e.matmul(pG[:, 0:512], ones_bf[0:1, 0:128],
                                    bvrow[0:1, ts(nh, 512)], start=False, stop=True,
                                    skip_group_check=True)
                tv = pb.emit("tensor", vmm,
                             wait=[("ve", cvT_done), ("ld", qkv_loaded[0]),
                                   ("ve", qk_ev.get(1, 0))], inc=("pe", 1))
                qk_ev[1] = pb.emit("vector", lambda e, nh=nh: e.tensor_copy(
                    vh_sb[:, ts(nh, 512)], pG[:, 0:512]),
                    wait=[("pe", tv)], inc=("ve", 1))
            qkv_ready_ac = qk_ev[0]
            qkv_ready_ve = qk_ev[1]
            h_ev = {0: 0, 1: 0, 2: 0}
            for h in range(16):
                pr, half = h // 2, h % 2
                t_sc = pb.emit("tensor", lambda e, pr=pr, half=half: e.matmul(
                    pE2[:, 128:256],
                    qhT_sb[half * 64:half * 64 + 64, ts(pr, 128)],
                    khT_sb[half * 64:half * 64 + 64, ts(pr, 128)],
                    start=True, stop=True, skip_group_check=True),
                    wait=[("ac", qkv_ready_ac), ("ve", h_ev[0])], inc=("pe", 1))
                v_msk = pb.emit("vector", lambda e: e.tensor_tensor(
                    att_sc, pE2[:, 128:256], mask_f, ALU.add),
                    wait=[("pe", t_sc)], inc=("ve", 1))
                v_mx = pb.emit("vector", lambda e: e.tensor_reduce(
                    SC3(0), att_sc, AX, ALU.max),
                    wait=[("ve", v_msk)], inc=("ve", 1))
                a_ng = pb.emit("scalar", lambda e: e.mul(SC3(1), SC3(0), -1.0),
                               wait=[("ve", v_mx)], inc=("ac", 1))
                a_ex = pb.emit("scalar", lambda e: e.activation(
                    ex_f, att_sc, AF.Exp, bias=SC3(1)),
                    wait=[("ac", a_ng)], inc=("ac", 1))
                v_sm = pb.emit("vector", lambda e: e.tensor_reduce(
                    SC3(2), ex_f, AX, ALU.add),
                    wait=[("ac", a_ex)], inc=("ve", 1))
                a_rc = pb.emit("scalar", lambda e: (
                    e.activation(SC3(3), SC3(2), AF.Ln),
                    e.activation(SC3(4), SC3(3), AF.Exp, scale=-1.0))[-1],
                    wait=[("ve", v_sm)], inc=("ac", 1))
                v_p = pb.emit("vector", lambda e: e.tensor_scalar(
                    p_sb, ex_f, SC3(4), None, ALU.mult),
                    wait=[("ac", a_rc)], inc=("ve", 1))
                t_pT = pb.emit("tensor", lambda e: e.transpose(
                    pTb[:, 128:256], p_sb, ident_bf),
                    wait=[("ve", v_p), ("ve", h_ev[1])], inc=("pe", 1))
                h_ev[1] = pb.emit("vector", lambda e: e.tensor_copy(pT_sb, pTb[:, 128:256]),
                                  wait=[("pe", t_pT)], inc=("ve", 1))
                t_oT = pb.emit("tensor", lambda e, h=h: e.matmul(
                    pF2[:, 128:256][0:64, :],
                    vh_sb[:, h * 64:h * 64 + 64], pT_sb,
                    start=True, stop=True, skip_group_check=True),
                    wait=[("ve", h_ev[1]), ("ve", qkv_ready_ve), ("ve", h_ev[2])],
                    inc=("pe", 1))
                h_ev[2] = pb.emit("vector", lambda e, pr=pr, half=half: e.tensor_copy(
                    oT_tmp[half * 64:half * 64 + 64, ts(pr, 128)],
                    pF2[0:64, 128:256]),
                    wait=[("pe", t_oT)], inc=("ve", 1))
                h_ev[0] = h_ev[2]
            o2_ev = 0
            for mt in range(8):
                def omm(e, mt=mt):
                    r = None
                    for kt in range(8):
                        r = e.matmul(pE2[:, 384:512],
                                     wo_sb[:, kt * 1024 + mt * 128:
                                           kt * 1024 + mt * 128 + 128],
                                     oT_tmp[:, ts(kt, 128)],
                                     start=(kt == 0), stop=(kt == 7),
                                     skip_group_check=True)
                    return r
                to2 = pb.emit("tensor", omm,
                              wait=[("ve", h_ev[2]), ("ac", o2_ev)], inc=("pe", 1))
                o2_ev = pb.emit("scalar", lambda e, mt=mt, i=i: e.activation(
                    o2T_all[:, i * 1024 + mt * 128:i * 1024 + mt * 128 + 128],
                    pE2[:, 384:512], AF.Identity, bias=bo_cc[:, mt:mt + 1]),
                    wait=[("pe", to2)], inc=("ac", 1))
            o2_done[i] = o2_ev
            return o2_ev

        for G in range(2):
            cvdone = conv_group(G)
            pe_after_conv = pb.cnt["pe"]
            if G == 0:
                lq = [pb.dma("sync", wq_sb.rearrange("p (a n) -> p a n", a=8),
                             wq_bfd.rearrange("(a p) n -> p a n", p=128),
                             wait=[("pe", pe_after_conv)]),
                      pb.dma("sync", wk_sb.rearrange("p (a n) -> p a n", a=8),
                             wk_bfd.rearrange("(a p) n -> p a n", p=128)),
                      pb.dma("sync", wv_sb.rearrange("p (a n) -> p a n", a=8),
                             wv_bfd.rearrange("(a p) n -> p a n", p=128)),
                      pb.dma("sync", wo_sb.rearrange("p (a n) -> p a n", a=8),
                             wo_bfd.rearrange("(a p) n -> p a n", p=128))]
                qkv_loaded[0] = max(lq)
            else:
                lq = [pb.dma("sync", wq_sb.rearrange("p (a n) -> p a n", a=8),
                             wq_bfd.rearrange("(a p) n -> p a n", p=128),
                             wait=[("pe", pe_after_conv)])]
                qkv_loaded[0] = max(lq)
            for i4 in range(4):
                attn_tile(G, i4, cvdone[i4])
            att_done[0] = pb.cnt["ve"]
            wb_pe[0] = wb_pe[1] = pb.cnt["pe"]

        pe_after_att = pb.cnt["pe"]
        lawt = []
        for kt in range(32):
            lawt.append(pb.dmat("sync", awT_sb[:, ts(kt, 512)],
                                aw_bfd[0:512, ts(kt, 128)],
                                sem="trd" if kt % 2 == 0 else "tr2",
                                wait=[("pe", pe_after_att)] if kt == 0 else ()))
        for j in range(4):
            for oc in range(8):
                lawt.append(pb.dmat("sync",
                                    bwT_sb[:, j * 4096 + oc * 512:
                                           j * 4096 + oc * 512 + 512],
                                    bw_bfd[ts(oc, 512), ts(j, 128)],
                                    sem="trd" if (j * 8 + oc) % 2 == 0 else "tr2"))
        awt_trd = pb.cnt["trd"]
        awt_tr2 = pb.cnt["tr2"]
        g1_prev = {0: 0, 1: 0}
        ffn_prev = {"pe": 0, "ve": 0, "ac": 0}
        for i in range(8):
            t_qT = pb.emit("tensor", lambda e, i=i: [
                e.matmul(pG[0:16, 0:128], wqT_sb[:, ts(kt, 16)],
                         xn1T[:, kt * 1026 + 1 + i * 128:kt * 1026 + 1 + i * 128 + 128],
                         start=(kt == 0), stop=(kt == 31), skip_group_check=True)
                for kt in range(32)][-1],
                wait=[("trd", wqT_done_trd), ("tr2", wqT_done_tr2),
                      ("ve", ffn_prev["ve"]), ("ac", ffn_prev["ac"])],
                inc=("pe", 1))
            v_qT = pb.emit("vector", lambda e: e.tensor_copy(qT_bf[0:16, :],
                                                             pG[0:16, 0:128]),
                           wait=[("pe", t_qT)], inc=("ve", 1))
            t_sc2 = pb.emit("tensor", lambda e: e.matmul(
                pG[:, 128:192], qT_bf[0:16, :], keysT_sb[0:16, 0:64],
                start=True, stop=True, skip_group_check=True),
                wait=[("ve", v_qT)], inc=("pe", 1))
            v_sc2 = pb.emit("vector", lambda e: (
                e.tensor_copy(ffn_sc, pG[:, 128:192]),
                e.tensor_copy(wk_f, pG[:, 128:192]))[-1],
                wait=[("pe", t_sc2)], inc=("ve", 1))
            vprev = v_sc2
            for it in range(15):
                srcb = wk_f if it % 2 == 0 else wk2_f
                dstb = wk2_f if it % 2 == 0 else wk_f
                v_m = pb.emit("vector", lambda e, srcb=srcb: e.tensor_reduce(
                    SC3(5), srcb, AX, ALU.max),
                    wait=[("ve", vprev)], inc=("ve", 1))
                if it == 0:
                    a_c = pb.emit("scalar", lambda e: (
                        e.copy(SC3(6), SC3(5)), e.copy(SC3(8), SC3(5)))[-1],
                        wait=[("ve", v_m)], inc=("ac", 1))
                else:
                    a_c = pb.emit("scalar", lambda e: e.copy(SC3(6), SC3(5)),
                                  wait=[("ve", v_m)], inc=("ac", 1))
                vprev = pb.emit("vector", lambda e, srcb=srcb, dstb=dstb: (
                    e.tensor_scalar(msk3, srcb, SC3(6), None, ALU.is_equal),
                    e.select(dstb, msk3, ninf_f, srcb))[-1],
                    wait=[("ac", a_c)], inc=("ve", 1))
            v_thr = pb.emit("vector", lambda e: e.tensor_reduce(
                SC3(5), wk2_f, AX, ALU.max),
                wait=[("ve", vprev)], inc=("ve", 1))
            a_thr = pb.emit("scalar", lambda e: (
                e.copy(SC3(7), SC3(5)), e.mul(SC3(9), SC3(8), -1.0))[-1],
                wait=[("ve", v_thr)], inc=("ac", 1))
            v_ge = pb.emit("vector", lambda e: e.tensor_scalar(
                msk3, ffn_sc, SC3(7), None, ALU.is_ge),
                wait=[("ac", a_thr)], inc=("ve", 1))
            a_ex2 = pb.emit("scalar", lambda e: e.activation(
                ex_f[:, 0:64], ffn_sc, AF.Exp, bias=SC3(9)),
                wait=[("ve", v_ge)], inc=("ac", 1))
            v_sel = pb.emit("vector", lambda e: (
                e.select(gate_f, msk3, ex_f[:, 0:64], zro64),
                e.tensor_reduce(SC3(10), gate_f, AX, ALU.add))[-1],
                wait=[("ac", a_ex2)], inc=("ve", 1))
            a_rc2 = pb.emit("scalar", lambda e: (
                e.activation(SC3(11), SC3(10), AF.Ln),
                e.activation(SC3(12), SC3(11), AF.Exp, scale=-1.0))[-1],
                wait=[("ve", v_sel)], inc=("ac", 1))
            v_gt = pb.emit("vector", lambda e: e.tensor_scalar(
                gate_f, gate_f, SC3(12), None, ALU.mult),
                wait=[("ac", a_rc2)], inc=("ve", 1))
            if i == 0:
                pb.dma("sync", d3C[:, 0:64], ffn_sc, sem="st2", wait=[("ve", v_gt)])
                pb.dma("sync", d3C[:, 64:128], gate_f, sem="st2")
                pb.dma("sync", d3C[:, 128:192], stat3, sem="st2")
            t_gT = pb.emit("tensor", lambda e: e.transpose(
                pG[0:64, 192:320], gate_f, ident_f),
                wait=[("ve", v_gt)], inc=("pe", 1))
            v_gTe = pb.emit("vector", lambda e: e.tensor_copy(
                gateT_bf[0:64, :], pG[0:64, 192:320]),
                wait=[("pe", t_gT)], inc=("ve", 1))
            t_ge = pb.emit("tensor", lambda e: [
                e.matmul(pE2[:, ts(j, 128)], eexp_sb[0:64, ts(j, 128)],
                         gateT_bf[0:64, :], start=True, stop=True,
                         skip_group_check=True) for j in range(4)][-1],
                wait=[("ve", v_gTe), ("ve", ffn_prev["ve"])], inc=("pe", 1))
            t_zT = pb.emit("tensor", lambda e, i=i: [
                e.matmul(pF2[:, ts(j, 128)], awT_sb[:, kt * 512 + j * 128:
                                                    kt * 512 + j * 128 + 128],
                         xn1T[:, kt * 1026 + 1 + i * 128:
                              kt * 1026 + 1 + i * 128 + 128],
                         start=(kt == 0), stop=(kt == 31), skip_group_check=True)
                for j in range(4) for kt in range(32)][-1],
                wait=[("trd", awt_trd), ("tr2", awt_tr2), ("ve", ffn_prev["ve"]),
                      ("ve", att_done[0])], inc=("pe", 1))
            v_ze = pb.emit("vector", lambda e: e.tensor_copy(zT_f, pF2[:, 0:512]),
                           wait=[("pe", t_zT), ("st", g1_prev[1])], inc=("ve", 1))
            v_zg = pb.emit("vector", lambda e: e.tensor_tensor(
                zgT_sb, zT_f, pE2[:, 0:512], ALU.mult),
                wait=[("ve", v_ze), ("pe", t_ge)], inc=("ve", 1))
            ffn_prev["ve"] = v_zg
            for oc in range(8):
                buf = oc % 2
                t_fo = pb.emit("tensor", lambda e, oc=oc: [
                    e.matmul(pG[:, 0:512],
                             zgT_sb[:, ts(j, 128)],
                             bwT_sb[:, j * 4096 + oc * 512:j * 4096 + oc * 512 + 512],
                             start=(j == 0), stop=(j == 3), skip_group_check=True)
                    for j in range(4)][-1],
                    wait=[("ve", v_zg), ("ac", ffn_prev["ac"])], inc=("pe", 1))
                a_g1 = pb.emit("scalar", lambda e, buf=buf: e.activation(
                    g1c[buf], pG[:, 0:512], AF.Silu, scale=2.0),
                    wait=[("pe", t_fo), ("st", g1_prev[buf])], inc=("ac", 1))
                ffn_prev["ac"] = a_g1
                g1_prev[buf] = pb.dma("sync", g1_dram[ts(i, 128), ts(oc, 512)],
                                      g1c[buf], sem="st", wait=[("ac", a_g1)])

        pe_after_ffn = pb.cnt["pe"]
        st_after_ffn = pb.cnt["st"]
        lpw = pb.dma("sync", pw_sb.rearrange("p (a n) -> p a n", a=8),
                     pw_bfd.rearrange("(a p) n -> p a n", p=128),
                     wait=[("pe", pe_after_ffn)])
        cmb_prev = {0: 0, 1: 0, "st": 0}
        for i in range(8):
            for oc in range(8):
                buf = oc % 2
                lg1 = pb.dma("sync", g1c[buf], g1_dram[ts(i, 128), ts(oc, 512)],
                             sem="ld2" if buf == 0 else "ld3",
                             wait=[("ve", cmb_prev[buf]), ("st", st_after_ffn)])
                lx1 = pb.dma("sync", x1c_f[buf], i_x1[ts(i, 128), ts(oc, 512)],
                             sem="trd" if buf == 0 else "tr2")
                ps = pE2 if buf == 0 else pF2
                t_pr = pb.emit("tensor", lambda e, i=i, oc=oc, ps=ps: [
                    e.matmul(ps[:, 0:512],
                             o2T_all[:, i * 1024 + kt * 128:i * 1024 + kt * 128 + 128],
                             pw_sb[:, kt * 4096 + oc * 512:kt * 4096 + oc * 512 + 512],
                             start=(kt == 0), stop=False, skip_group_check=True)
                    for kt in range(8)][-1],
                    wait=[("ld", lpw), ("ve", cmb_prev[buf]),
                          ("ac", o2_done[i])], inc=("pe", 1))
                t_prb = pb.emit("tensor", lambda e, oc=oc, ps=ps: e.matmul(
                    ps[:, 0:512], ones_bf[0:1, 0:128], pbrow[0:1, ts(oc, 512)],
                    start=False, stop=True, skip_group_check=True), inc=("pe", 1))
                v_c1 = pb.emit("vector", lambda e, buf=buf, ps=ps: e.tensor_tensor(
                    ps[:, 0:512], ps[:, 0:512], g1c[buf], ALU.mult),
                    wait=[("pe", t_prb), ("ld2" if buf == 0 else "ld3", lg1)],
                    inc=("ve", 1))
                v_c2 = pb.emit("vector", lambda e, buf=buf, ps=ps: e.tensor_tensor(
                    x2_f[buf], ps[:, 0:512], x1c_f[buf], ALU.add),
                    wait=[("trd" if buf == 0 else "tr2", lx1), ("ve", v_c1)],
                    inc=("ve", 1))
                stx = pb.dma("sync", o_x2[ts(i, 128), ts(oc, 512)], x2_f[buf],
                             sem="st", wait=[("ve", v_c2)])
                cmb_prev[buf] = v_c2
                cmb_prev["st"] = stx
        pb.finish()
        pb.run(block)
    return nc

def kernel(**inputs):
    x = np.asarray(inputs["L0_attn_hidden_states"], np.float32)[0]
    csts = make_consts()

    def rl(a):
        return np.ascontiguousarray(a)

    conv_b0 = rl(np.asarray(inputs["L0_attn_conv_b"], np.float32).reshape(80, 128).T)
    norm_l = rl(np.asarray(inputs["L0_attn_norm_w"], np.float32).reshape(64, 128).T)
    usuf = (np.arange(8)[:, None] > np.arange(8)[None, :]).astype(np.float32)

    im1s = []
    for k in range(8):
        x_main = rl(x[k * T:(k + 1) * T])
        halo3 = np.zeros((3, D), np.float32) if k == 0 else rl(x[k * T - 3:k * T])
        im = {
            "x_main": x_main, "x_halo3": halo3,
            "w_in": np.asarray(inputs["L0_attn_in_proj_w"], np.float32),
            "conv_w0": np.asarray(inputs["L0_attn_conv_w"], np.float32),
            "conv_b0": conv_b0,
            "dt_bias": np.asarray(inputs["L0_attn_dt_bias"], np.float32),
            "a_log": np.asarray(inputs["L0_attn_A_log"], np.float32),
            "d_vec": np.asarray(inputs["L0_attn_D"], np.float32),
            "ffn_w": np.asarray(inputs["L0_ffn_W"], np.float32),
            "ffn_b": np.asarray(inputs["L0_ffn_b"], np.float32),
            "ffn_a": np.asarray(inputs["L0_ffn_A"], np.float32),
            "limes": np.asarray(inputs["L0_ffn_LiMEs"], np.float32),
        }
        im.update(csts)
        im1s.append(im)
    nc1 = build_launch1(upto="all")
    rs1 = run_bass_kernel_spmd(nc1, im1s, core_ids=list(range(8))).results

    gF = np.concatenate([np.asarray(rs1[k]["o_f"], np.float32) for k in range(8)], 0)
    gA = np.concatenate([np.asarray(rs1[k]["o_asum"], np.float32) for k in range(8)], 0)
    gmax = np.concatenate([np.asarray(rs1[k]["o_locmax"], np.float32)
                           for k in range(8)], 0)
    im2s = []
    for k in range(8):
        cmask = (np.arange(8) < k).astype(np.float32)
        im = {
            "x_main": rl(x[k * T:(k + 1) * T]),
            "i_xn": np.asarray(rs1[k]["o_xn"]),
            "i_xbc": np.asarray(rs1[k]["o_xbc"]),
            "i_z": np.asarray(rs1[k]["o_z"]),
            "i_acs": np.asarray(rs1[k]["o_acs"]),
            "i_acst": np.asarray(rs1[k]["o_acst"]),
            "i_y1": np.asarray(rs1[k]["o_y1"]),
            "i_s": np.asarray(rs1[k]["o_s"]),
            "i_h": np.asarray(rs1[k]["o_h"]),
            "i_hst": np.asarray(rs1[k]["o_hst"]),
            "i_dst": np.asarray(rs1[k]["o_dst"]),
            "i_zffn": np.asarray(rs1[k]["o_zffn"]),
            "g_f": gF, "g_a": gA, "g_max": gmax, "cmask": cmask, "usuf": usuf,
            "wo_w": np.asarray(inputs["L0_attn_out_proj_w"], np.float32),
            "bmat": np.asarray(inputs["L0_ffn_Bmat"], np.float32),
            "limes": np.asarray(inputs["L0_ffn_LiMEs"], np.float32),
            "norm_l": norm_l,
            "c_eps6": np.full((1, 1), 1e-6, np.float32),
            "c_eps9": np.full((1, 1), 1e-9, np.float32),
        }
        im.update(csts)
        im2s.append(im)
    nc2 = build_launch2()
    rs2 = run_bass_kernel_spmd(nc2, im2s, core_ids=list(range(8))).results
    x1_full = np.concatenate([np.asarray(rs2[k]["b_x1"], np.float32)
                              for k in range(8)], 0)

    wq3 = rl(np.asarray(inputs["L1_attn_wq"], np.float32).reshape(1024, 1024))
    wk3 = rl(np.asarray(inputs["L1_attn_wk"], np.float32).reshape(1024, 1024))
    wv3 = rl(np.asarray(inputs["L1_attn_wv"], np.float32).reshape(1024, 1024))
    wo3 = rl(np.asarray(inputs["L1_attn_wo"], np.float32).reshape(1024, 1024))
    bq3c = rl(np.asarray(inputs["L1_attn_bq"], np.float32).reshape(1024)
              .reshape(8, 128).T)
    bk3c = rl(np.asarray(inputs["L1_attn_bk"], np.float32).reshape(1024)
              .reshape(8, 128).T)
    bv3 = rl(np.asarray(inputs["L1_attn_bv"], np.float32).reshape(1024))
    bo3c = rl(np.asarray(inputs["L1_attn_bo"], np.float32).reshape(8, 128).T)
    im3s = []
    for k in range(8):
        halo2 = np.zeros((2, D), np.float32)
        if k > 0:
            halo2[0] = x1_full[k * T - 1]
        if k < 7:
            halo2[1] = x1_full[(k + 1) * T]
        im = {
            "i_x1": rl(x1_full[k * T:(k + 1) * T]),
            "x1_halo2": halo2,
            "cw3": np.asarray(inputs["L1_attn_conv_w"], np.float32),
            "cb3": np.asarray(inputs["L1_attn_conv_b"], np.float32),
            "wq3": wq3, "bq3c": bq3c, "wk3": wk3, "bk3c": bk3c,
            "wv3": wv3, "bv3": bv3, "wo3": wo3, "bo3c": bo3c,
            "pw3": np.asarray(inputs["L1_attn_proj_w"], np.float32),
            "pb3": np.asarray(inputs["L1_attn_proj_b"], np.float32),
            "aw3": np.asarray(inputs["L1_ffn_A_w"], np.float32),
            "bw3": np.asarray(inputs["L1_ffn_B_w"], np.float32),
            "wqr3": np.asarray(inputs["L1_ffn_Wq_w"], np.float32),
            "keys3": np.asarray(inputs["L1_ffn_keys"], np.float32),
        }
        im.update(csts)
        im3s.append(im)
    nc3 = build_launch3()
    rs3 = run_bass_kernel_spmd(nc3, im3s, core_ids=list(range(8))).results
    out = np.concatenate([np.asarray(rs3[k]["o_x2"], np.float32) for k in range(8)], 0)
    return out[None, :, :]


# version 04
# speedup vs baseline: 1.0011x; 1.0011x over previous
import contextlib
import os
import numpy as np
import ml_dtypes
import concourse.bass as bass
import concourse.mybir as mybir
from concourse.bass_utils import run_bass_kernel_spmd

F32 = mybir.dt.float32
BF16 = mybir.dt.bfloat16
U8 = mybir.dt.uint8
AF = mybir.ActivationFunctionType
ALU = mybir.AluOpType
AX = mybir.AxisListType.X
BF = ml_dtypes.bfloat16

W = 8
T = 1024
D = 4096
DI = 8192
NH = 128
PD = 64
NG = 8
NS = 128
CH = 256
NCH = T // CH
CONV_DIM = DI + 2 * NG * NS
DIN = 2 * DI + 2 * NG * NS + NH
EPS = 1e-5

ENGINES = ("sync", "scalar", "vector", "tensor", "gpsimd")


class PB:
    def __init__(self, nc):
        self.nc = nc
        self.q = {e: [] for e in ENGINES}
        self.sem = {}
        self.cnt = {}
        self._maxwait = {}

    def new_sem(self, handle, name):
        self.sem[name] = handle
        self.cnt[name] = 0
        self._maxwait[name] = 0

    def emit(self, eng, fn, wait=(), inc=None):
        wait = tuple((s, v) for (s, v) in wait if v and v > 0)
        for s, v in wait:
            self._maxwait[s] = max(self._maxwait[s], v)
        tgt = None
        if inc is not None:
            s, d = inc
            self.cnt[s] += d
            tgt = self.cnt[s]

        def closure(e, wait=wait, fn=fn, inc=inc):
            for s, v in wait:
                e.wait_ge(self.sem[s], v)
            inst = fn(e) if fn is not None else None
            if inc is not None:
                assert inst is not None
                inst.then_inc(self.sem[inc[0]], inc[1])
        self.q[eng].append(closure)
        return tgt

    def wait(self, eng, wait):
        self.emit(eng, None, wait=wait)

    def dma(self, eng, out, in_, sem="ld", wait=()):
        return self.emit(eng, lambda e, o=out, i=in_: e.dma_start(out=o, in_=i),
                         wait=wait, inc=(sem, 16))

    def dmat(self, eng, out, in_, sem="trd", wait=()):
        return self.emit(eng,
                         lambda e, o=out, i=in_: e.dma_start(out=o, in_=i, transpose=True),
                         wait=wait, inc=(sem, 16))

    def validate(self):
        for s, v in self._maxwait.items():
            assert v <= self.cnt[s], f"wait {s}>={v} but final {self.cnt[s]}"

    def finish(self):
        self.wait("sync", [("st", self.cnt["st"]), ("stg", self.cnt["stg"]),
                           ("cast", self.cnt["cast"])])
        self.validate()

    def run(self, block):
        for eng in ENGINES:
            ops = self.q[eng]

            def body(e, ops=ops):
                for f in ops:
                    f(e)
            getattr(block, eng)(body)


def ts(i, n):
    return slice(i * n, (i + 1) * n)


def make_consts():
    m = np.zeros((256, 256), dtype=BF)
    sg, lg = np.meshgrid(np.arange(256), np.arange(256), indexing="ij")
    m[sg <= lg] = 1
    return {
        "c_ident_bf": np.eye(128, dtype=BF),
        "c_ident_f32": np.eye(128, dtype=np.float32),
        "c_ones_f32": np.ones((1, 128), dtype=np.float32),
        "c_tril": m,
        "c_zeros": np.zeros((1, 8192), dtype=np.float32),
        "c_eps": np.full((1, 1), 1e-5, dtype=np.float32),
        "c_ones_bf": np.ones((1, 128), dtype=BF),
        "c_mask128": np.where(np.arange(128)[:, None] >= np.arange(128)[None, :],
                              0.0, -1e9).astype(np.float32),
        "c_ninf": np.full((1, 64), -1e9, dtype=np.float32),
        "c_eexp": np.kron(np.eye(64), np.ones((1, 8))).astype(BF),
    }


CONST_SPECS = [
    ("c_ident_bf", [128, 128], BF16),
    ("c_ident_f32", [128, 128], F32),
    ("c_ones_f32", [1, 128], F32),
    ("c_tril", [256, 256], BF16),
    ("c_zeros", [1, 8192], F32),
    ("c_eps", [1, 1], F32),
    ("c_ones_bf", [1, 128], BF16),
    ("c_mask128", [128, 128], F32),
    ("c_ninf", [1, 64], F32),
    ("c_eexp", [64, 512], BF16),
]


def build_launch1(upto="all"):
    nc = bass.Bass("TRN2", target_bir_lowering=False)
    dI = lambda n, s, d=F32: nc.dram_tensor(n, s, d, kind="ExternalInput").ap()
    dO = lambda n, s, d=F32: nc.dram_tensor(n, s, d, kind="ExternalOutput").ap()
    dT = lambda n, s, d=BF16: nc.dram_tensor(n, s, d, kind="Internal").ap()

    x_main = dI("x_main", [T, D])
    x_halo3 = dI("x_halo3", [3, D])
    w_in = dI("w_in", [DIN, D])
    conv_w0 = dI("conv_w0", [CONV_DIM, 4])
    conv_b0 = dI("conv_b0", [128, 80])
    dt_bias = dI("dt_bias", [NH])
    a_log = dI("a_log", [NH])
    d_vec = dI("d_vec", [NH])
    ffn_w = dI("ffn_w", [D, D])
    ffn_b = dI("ffn_b", [D])
    ffn_a = dI("ffn_a", [D, 32])
    limes = dI("limes", [8, 32])
    cst = {n: dI(n, s, d) for n, s, d in CONST_SPECS}

    o_xn = dO("o_xn", [T, D], BF16)
    o_xbc = dO("o_xbc", [CONV_DIM, T], BF16)
    o_z = dO("o_z", [T, DI], BF16)
    o_dt = dO("o_dt", [NH, T])
    o_acs = dO("o_acs", [NH, T])
    o_acst = dO("o_acst", [T, NH])
    o_y1 = dO("o_y1", [T, DI], BF16)
    o_s = dO("o_s", [NCH * NS, DI], BF16)
    o_f = dO("o_f", [NS, DI])
    o_asum = dO("o_asum", [NH, 1])
    o_locmax = dO("o_locmax", [1, 2])
    o_h = dO("o_h", [T, D], BF16)
    o_hst = dO("o_hst", [8, 512])
    o_dst = dO("o_dst", [8, 512])
    o_zffn = dO("o_zffn", [32, T])

    o_dbg1 = dO("o_dbg1", [128, 16384], BF16)
    o_dbg2 = dO("o_dbg2", [128, 2048], BF16)
    o_dbg3 = dO("o_dbg3", [128, 256])
    w_in_bf = dT("w_in_bf", [DIN, D])
    ffnw_bf = dT("ffnw_bf", [D, D])
    pre_dram = dT("pre_dram", [CONV_DIM, T])
    halo_dram = dT("halo_dram", [4, CONV_DIM])

    with contextlib.ExitStack() as xs:
        E = xs.enter_context
        xnt_t = E(nc.sbuf_tensor("xnt_sb", [128, 32 * 1024], BF16))
        wt_t = E(nc.sbuf_tensor("wt_sb", [128, 32 * 1024], BF16))
        mb_t = E(nc.sbuf_tensor("miscb_sb", [128, 16 * 1024], BF16))
        fa_t = E(nc.sbuf_tensor("f32_sb", [128, 12160], F32))
        pA = E(nc.psum_tensor("pA", [128, 512], F32))
        pB = E(nc.psum_tensor("pB", [128, 512], F32))
        pT = E(nc.psum_tensor("pT", [128, 256], BF16))
        pR = E(nc.psum_tensor("pR", [128, 512], F32))
        pY0 = E(nc.psum_tensor("pY0", [128, 512], F32))
        pY1 = E(nc.psum_tensor("pY1", [128, 512], F32))
        pS = E(nc.psum_tensor("pS", [128, 512], F32))
        pH = E(nc.psum_tensor("pH", [128, 512], F32))
        block = E(nc.Block())
        pb = PB(nc)
        for s in ["cast", "trd", "tr2", "ld", "pe", "ve", "ac", "gp", "st", "ts", "stg", "xa", "xb"]:
            pb.new_sem(E(nc.semaphore("sem_" + s)), s)

        xnT = xnt_t[:]
        wt0 = wt_t[:, 0:16384]
        wt1 = wt_t[:, 16384:32768]
        wts = [wt0, wt1]
        mb = mb_t[:]
        fa = fa_t[:]

        ident_bf = mb[:, 0:128]
        tril = mb[:, 128:640]
        xnT_halo = mb[:, 640:768]
        xntok = mb[:, 768:4864]
        evb = mb[:, 4864:6912]
        halo_sm = mb[:, 6912:7424]
        pre_t = mb[:, 7424:9536]
        hls = mb[:, 9536:9792]
        zffn_bf = mb[:, 9792:10816]
        ffna_sb = mb[:, 10816:11840]
        limes_bf = mb[:, 11840:11872]
        limesT_bf = mb[:, 11872:11880]
        bexp_bf = mb[:, 11904:16000]
        ident_f = fa[:, 0:128]
        ones_f = fa[:, 128:256]
        zrow = fa[:, 256:512]
        smv = fa[:, 512:544]
        convw = fa[:, 544:864]
        convb = fa[:, 864:944]
        dbc = fa[:, 944:1072]
        stat = fa[:, 1072:1136]
        dtf = fa[:, 1136:2160]
        acs = fa[:, 2160:3184]
        acst = fa[:, 3184:4208]
        htile = fa[:, 4208:4720]
        hst = fa[:, 4720:5232]
        dst_sl = fa[:, 5232:5744]
        zffn = fa[:, 5744:6768]
        dsT = fa[:, 6768:7024]
        dtT = fa[:, 7024:7280]
        nacst = fa[:, 7280:7536]
        rowf = fa[:, 7536:7792]
        eacol = fa[:, 7792:7920]
        earow = fa[:, 7920:8048]
        bcol8 = fa[:, 8048:8056]
        xld = fa[:, 8064:10112]
        sqs = fa[:, 10112:12160]
        dtraw = xld[:, 0:1024]
        daf = xld[:, 1024:2048]
        cacc = sqs[:, 0:1024]


        def SC(i, rows=128):
            return stat[0:rows, 4 * i:4 * i + 1]

        def SM(i):
            return smv[:, 4 * i:4 * i + 1]

        ld0 = [
            pb.dma("sync", ident_bf, cst["c_ident_bf"]),
            pb.dma("sync", tril[:, 0:256], cst["c_tril"][0:128, :]),
            pb.dma("sync", tril[:, 256:512], cst["c_tril"][128:256, :]),
            pb.dma("sync", ident_f, cst["c_ident_f32"]),
            pb.dma("sync", ones_f[0:1, :], cst["c_ones_f32"]),
            pb.dma("sync", zrow, cst["c_zeros"][:, 0:256].broadcast_to([128, 256])),
            pb.dma("sync", SM(0), dt_bias[:, None]),
            pb.dma("sync", SM(1), a_log[:, None]),
            pb.dma("sync", convw.rearrange("p (n k) -> p n k", n=80),
                   conv_w0.rearrange("(n p) k -> p n k", p=128)),
            pb.dma("sync", convb, conv_b0[:, :]),
            pb.dma("gpsimd", bexp_bf, ffn_b[None, :].broadcast_to([128, 4096]), sem="cast"),
            pb.dma("sync", dbc, d_vec[None, :].broadcast_to([128, 128])),
            pb.dma("sync", stat[:, 56:57], cst["c_eps"][0:1, 0:1].broadcast_to([128, 1])),
        ]
        lds = max(ld0)

        blk_rows = [(18432, 18560)] \
            + [(8192 + 512 * i, 8704 + 512 * i) for i in range(20)] \
            + [(512 * i, 512 * i + 512) for i in range(16)]
        cast_at = []
        ffn_cast_at = []

        def emit_ln(rows, src, tile_idx, halo):
            a1 = v1 = None
            for hh in range(2):
                ldx = pb.dma("sync", xld[0:rows, :], src[:, ts(hh, 2048)],
                             wait=[("ac", a1)] if a1 else [("ve", ln_prev_v4[0])])
                v1 = pb.emit("vector", lambda e, hh=hh: e.tensor_reduce(
                    SC(8 + hh, rows), xld[0:rows, :], AX, ALU.add),
                    wait=[("ld", ldx)], inc=("ve", 1))
                a1 = pb.emit("scalar", lambda e, hh=hh: e.activation(
                    sqs[0:rows, :], xld[0:rows, :], AF.Square,
                    accum_out=SC(10 + hh, rows)),
                    wait=[("ld", ldx), ("ve", v1)], inc=("ac", 1))

            s_sum = pb.emit("scalar", lambda e: e.add(SC(13, rows), SC(8, rows), SC(9, rows)),
                            wait=[("ve", v1)], inc=("ac", 1))
            p_s2 = pb.emit("gpsimd", lambda e: e.tensor_tensor(SC(12, rows), SC(10, rows),
                                                               SC(11, rows), ALU.add),
                           wait=[("ac", a1)], inc=("gp", 1))
            v_mean = pb.emit("vector", lambda e: e.tensor_scalar(SC(2, rows), SC(13, rows),
                                                                 1.0 / D, None, ALU.mult),
                             wait=[("ac", s_sum)], inc=("ve", 1))
            a_m2 = pb.emit("scalar", lambda e: e.activation(SC(3, rows), SC(2, rows), AF.Square),
                           wait=[("ve", v_mean)], inc=("ac", 1))
            v2 = pb.emit("vector", lambda e: e.tensor_scalar(SC(5, rows), SC(12, rows),
                                                             1.0 / D, SC(3, rows),
                                                             ALU.mult, ALU.subtract),
                         wait=[("gp", p_s2), ("ac", a_m2)], inc=("ve", 1))
            a2 = pb.emit("scalar", lambda e: (
                e.activation(SC(6, rows), SC(5, rows), AF.Ln, bias=stat[0:rows, 56:57]),
                e.activation(SC(7, rows), SC(6, rows), AF.Exp, scale=-0.5))[-1],
                wait=[("ve", v2)], inc=("ac", 1))
            v3 = pb.emit("vector",
                         lambda e: e.tensor_scalar(xntok[0:rows, 2048:4096], xld[0:rows, :],
                                                   SC(2, rows), SC(7, rows),
                                                   ALU.subtract, ALU.mult),
                         wait=[("ac", a2), ("st", xn_st[0])], inc=("ve", 1))
            ldx0 = pb.dma("sync", xld[0:rows, :], src[:, 0:2048], wait=[("ve", v3)])
            v4 = pb.emit("vector",
                         lambda e: e.tensor_scalar(xntok[0:rows, 0:2048], xld[0:rows, :],
                                                   SC(2, rows), SC(7, rows),
                                                   ALU.subtract, ALU.mult),
                         wait=[("ld", ldx0)], inc=("ve", 1))
            ln_prev_v4[0] = v4
            if not halo:
                xn_st[0] = pb.dma("sync", o_xn[ts(tile_idx, 128), :], xntok[:], sem="st",
                                  wait=[("ve", v4)])
            for c2 in range(16):
                def tr(e, c2=c2):
                    e.transpose(pT[:, 0:rows], xntok[0:rows, ts(2 * c2, 128)],
                                ident_bf[0:rows, 0:rows])
                    return e.transpose(pT[:, 128:128 + rows], xntok[0:rows, ts(2 * c2 + 1, 128)],
                                       ident_bf[0:rows, 0:rows])
                pv = pb.emit("tensor", tr, wait=[("ve", v4)], inc=("pe", 1))
                if not halo:
                    def ev(e, c2=c2, ti=tile_idx):
                        e.tensor_copy(
                            xnT[:, (2 * c2) * 1024 + ti * 128:(2 * c2) * 1024 + ti * 128 + 128],
                            pT[:, 0:128])
                        return e.tensor_copy(
                            xnT[:, (2 * c2 + 1) * 1024 + ti * 128:(2 * c2 + 1) * 1024 + ti * 128 + 128],
                            pT[:, 128:256])
                else:
                    def ev(e, c2=c2):
                        e.tensor_copy(xnT_halo[:, (2 * c2) * 4:(2 * c2) * 4 + 3], pT[:, 0:3])
                        return e.tensor_copy(xnT_halo[:, (2 * c2 + 1) * 4:(2 * c2 + 1) * 4 + 3],
                                             pT[:, 128:131])
                evv = pb.emit("vector", ev, wait=[("pe", pv)], inc=("ve", 1))
                pb.wait("tensor", [("ve", evv)])

        pb.wait("tensor", [("ld", lds)])
        pb.wait("vector", [("ld", lds)])
        xn_st = {0: 0}
        ln_prev_v4 = {0: 0}
        for t_i in range(1 if upto == "ln0" else 8):
            emit_ln(128, x_main[ts(t_i, 128), :], t_i, False)
        if upto != "ln0":
            emit_ln(3, x_halo3[:], 0, True)
        xnT_rdy = pb.cnt["ve"]
        st_after_ln = pb.cnt["st"]
        if upto not in ("ln0", "ln"):
            cast_at.extend(pb.dma("gpsimd", w_in_bf[r0:r1, :], w_in[r0:r1, :],
                                  sem="cast") for (r0, r1) in blk_rows)
            ffn_cast_at.extend(pb.dma("gpsimd", ffnw_bf[ts(b, 512), :],
                                      ffn_w[ts(b, 512), :], sem="cast")
                               for b in range(8))

        if upto in ("ln", "ln0"):
            pb.finish()
            pb.run(block)
            return nc

        pe_done = {}
        ev_last = {0: 0, 1: 0}
        halo_last_ev = 0
        halo_last_pe = 0
        dt_ev = None

        def inproj_block(bi):
            nonlocal halo_last_ev, halo_last_pe, dt_ev
            r0, r1 = blk_rows[bi]
            osz = r1 - r0
            wt = wts[bi % 2]
            wait_tr = [("cast", cast_at[bi])]
            if bi >= 2:
                wait_tr.append(("pe", pe_done[bi - 2]))
            tsem = "trd" if bi % 2 == 0 else "tr2"
            trd_v = None
            for k in range(32):
                trd_v = pb.dmat("sync", wt[:, k * 512:k * 512 + osz],
                                w_in_bf[r0:r1, ts(k, 128)], sem=tsem,
                                wait=wait_tr if k == 0 else ())
            pb.wait("tensor", [(tsem, trd_v), ("ve", xnT_rdy)])

            if bi == 0:
                for k in range(32):
                    pb.emit("tensor", lambda e, k=k: e.matmul(
                        pA[:], wt[:, k * 512:k * 512 + 128], xnT[:, k * 1024:k * 1024 + 512],
                        start=(k == 0), stop=(k == 31)))
                    pb.emit("tensor", lambda e, k=k: e.matmul(
                        pB[:], wt[:, k * 512:k * 512 + 128],
                        xnT[:, k * 1024 + 512:k * 1024 + 1024],
                        start=(k == 0), stop=(k == 31)),
                        inc=("pe", 1) if k == 31 else None)
                pe_done[bi] = pb.cnt["pe"]
                dt_ev = pb.emit("vector", lambda e: (
                    e.tensor_copy(dtraw[:, 0:512], pA[:]),
                    e.tensor_copy(dtraw[:, 512:1024], pB[:]))[-1],
                    wait=[("pe", pe_done[bi])], inc=("ve", 1))
                return

            if bi <= 20:
                for j in range(4):
                    buf = j % 2
                    pb.wait("tensor", [("ve", max(ev_last[0], ev_last[1]))])
                    for k in range(32):
                        pb.emit("tensor", lambda e, k=k, j=j: e.matmul(
                            pA[:], wt[:, k * 512 + j * 128:k * 512 + j * 128 + 128],
                            xnT[:, k * 1024:k * 1024 + 512], start=(k == 0), stop=(k == 31)))
                        pb.emit("tensor", lambda e, k=k, j=j: e.matmul(
                            pB[:], wt[:, k * 512 + j * 128:k * 512 + j * 128 + 128],
                            xnT[:, k * 1024 + 512:k * 1024 + 1024],
                            start=(k == 0), stop=(k == 31)),
                            inc=("pe", 1) if k == 31 else None)
                    mmd = pb.cnt["pe"]
                    dstb = evb[:, ts(buf, 1024)]
                    ev = pb.emit("vector", lambda e, dstb=dstb: (
                        e.tensor_copy(dstb[:, 0:512], pA[:]),
                        e.tensor_copy(dstb[:, 512:1024], pB[:]))[-1],
                        wait=[("pe", mmd), ("st", ev_st.get(buf, 0))], inc=("ve", 1))
                    ev_last[buf] = ev
                    row = r0 - 8192 + j * 128
                    ev_st[buf] = pb.dma("sync", pre_dram[row:row + 128, :], dstb, sem="st",
                                        wait=[("ve", ev)])
                pb.wait("tensor", [("ve", halo_last_ev)])
                for k in range(32):
                    pb.emit("tensor", lambda e, k=k: e.matmul(
                        pH[0:3, :], xnT_halo[:, k * 4:k * 4 + 3], wt[:, ts(k, 512)],
                        start=(k == 0), stop=(k == 31)),
                        inc=("pe", 1) if k == 31 else None)
                hv = pb.emit("vector", lambda e: e.tensor_copy(halo_sm[0:3, :], pH[0:3, :]),
                             wait=[("pe", pb.cnt["pe"]), ("st", halo_st.get(0, 0))], inc=("ve", 1))
                halo_last_ev = hv
                halo_st[0] = pb.dma("sync", halo_dram[0:3, r0 - 8192:r0 - 8192 + 512],
                                    halo_sm[0:3, :], sem="st", wait=[("ve", hv)])
            else:
                zi = bi - 21
                for t_i in range(8):
                    ps = pA if t_i % 2 == 0 else pB
                    buf = t_i % 2
                    pb.wait("tensor", [("ve", ev_last[buf])])
                    for k in range(32):
                        pb.emit("tensor", lambda e, k=k, t_i=t_i, ps=ps: e.matmul(
                            ps[:], xnT[:, k * 1024 + t_i * 128:k * 1024 + t_i * 128 + 128],
                            wt[:, ts(k, 512)], start=(k == 0), stop=(k == 31)),
                            inc=("pe", 1) if k == 31 else None)
                    mmd = pb.cnt["pe"]
                    dstb = evb[:, buf * 1024:buf * 1024 + 512]
                    ev = pb.emit("vector", lambda e, dstb=dstb, ps=ps: e.tensor_copy(dstb, ps[:]),
                                 wait=[("pe", mmd), ("st", ev_st.get(buf, 0))], inc=("ve", 1))
                    ev_last[buf] = ev
                    ev_st[buf] = pb.dma("sync", o_z[ts(t_i, 128), ts(zi, 512)], dstb, sem="st",
                                        wait=[("ve", ev)])
            pe_done[bi] = pb.cnt["pe"]

        ev_st = {}
        halo_st = {}
        for bi in range(37):
            inproj_block(bi)
        inproj_pe = pb.cnt["pe"]

        a_a = pb.emit("scalar", lambda e: e.activation(SM(2), SM(1), AF.Exp),
                      wait=[("ld", lds)], inc=("ac", 1))
        v_a = pb.emit("vector", lambda e: e.tensor_scalar(SM(3), SM(2), -1.0, None,
                                                          ALU.mult),
                      wait=[("ac", a_a)], inc=("ve", 1))
        u_ = sqs[:, 1024:2048]
        v_u = pb.emit("vector", lambda e: e.tensor_scalar(u_, dtraw[:, :], SM(0),
                                                          None, ALU.add),
                      wait=[("ld", lds), ("ve", dt_ev)], inc=("ve", 1))
        ab1 = daf
        a_ab = pb.emit("scalar", lambda e: e.activation(ab1[:, :], u_, AF.Abs),
                       wait=[("ve", v_u)], inc=("ac", 1))
        a_ex = pb.emit("scalar", lambda e: e.activation(ab1[:, :], ab1[:, :], AF.Exp, scale=-1.0),
                       inc=("ac", 1))
        a_ln = pb.emit("scalar", lambda e: e.activation(ab1[:, :], ab1[:, :], AF.Ln, bias=1.0),
                       inc=("ac", 1))
        a_re = pb.emit("scalar", lambda e: e.activation(u_, u_, AF.Relu), inc=("ac", 1))
        v_dt = pb.emit("vector", lambda e: e.tensor_tensor(dtf[:, :], ab1[:, :], u_, ALU.add),
                       wait=[("ac", a_re)], inc=("ve", 1))
        pb.dma("sync", o_dt[:, :], dtf[:, :], sem="st", wait=[("ve", v_dt)])
        v_da = pb.emit("vector", lambda e: e.tensor_scalar(daf[:, :], dtf[:, :], SM(3), None,
                                                           ALU.mult),
                       wait=[("ve", v_dt), ("ac", a_ln)], inc=("ve", 1))
        v_sc = None
        for c in range(NCH):
            v_sc = pb.emit("vector", lambda e, c=c: e.tensor_tensor_scan(
                acs[:, ts(c, 256)], daf[:, ts(c, 256)], zrow[:, 0:256], 0.0, ALU.add, ALU.add),
                wait=[("ve", v_da)], inc=("ve", 1))
        pb.dma("sync", o_acs[:, :], acs[:, :], sem="st", wait=[("ve", v_sc)])

        def asum_fn(e):
            e.tensor_tensor(SM(4), acs[:, 255:256], acs[:, 511:512], ALU.add)
            return e.tensor_tensor(SM(5), acs[:, 767:768], acs[:, 1023:1024], ALU.add)
        v_as = pb.emit("vector", asum_fn, wait=[("ve", v_sc)], inc=("ve", 1))
        a_as = pb.emit("scalar", lambda e: e.add(SM(6), SM(4), SM(5)),
                       wait=[("ve", v_as)], inc=("ac", 1))
        pb.dma("sync", o_asum[:, :], SM(6), sem="st", wait=[("ac", a_as)])
        ac_tr = None
        for c in range(NCH):
            for lt in range(2):
                tr = pb.emit("tensor", lambda e, c=c, lt=lt: e.transpose(
                    pR[:, 0:128], acs[:, c * 256 + lt * 128:c * 256 + lt * 128 + 128], ident_f[:]),
                    wait=[("ve", v_sc), ("ve", ac_tr)], inc=("pe", 1))
                ac_tr = pb.emit("vector", lambda e, c=c, lt=lt: e.tensor_copy(
                    acst[:, c * 256 + lt * 128:c * 256 + lt * 128 + 128], pR[:, 0:128]),
                    wait=[("pe", tr)], inc=("ve", 1))
        pb.dma("sync", o_acst.rearrange("(a p) h -> p a h", p=128),
               acst.rearrange("p (a h) -> p a h", a=8), sem="st",
               wait=[("ve", ac_tr)])

        if upto == "inproj":
            pb.finish()
            pb.run(block)
            return nc

        conv_ve = {0: 0, 1: 0}
        xc_st = 0
        for ot in range(80):
            buf = ot % 2
            pv = pre_t[:, buf * 1056:buf * 1056 + 1027]
            hb = hls[0:3, ts(buf, 128)]
            w1 = [("ve", conv_ve[buf])] if conv_ve[buf] else []
            l1 = pb.dma("sync", pv[:, 3:1027], pre_dram[ts(ot, 128), :], wait=w1)
            l2 = pb.dma("sync", hb, halo_dram[0:3, ts(ot, 128)])
            trh = pb.emit("tensor", lambda e, hb=hb: e.transpose(
                pT[:, 0:3], hb, ident_bf[0:3, 0:3]),
                wait=[("ld", l2), ("ve", conv_ve[buf])], inc=("pe", 1))
            hv = pb.emit("vector", lambda e, pv=pv: e.tensor_copy(pv[:, 0:3], pT[:, 0:3]),
                         wait=[("pe", trh), ("ld", l1)], inc=("ve", 1))
            pb.wait("tensor", [("ve", hv)])
            cv = pb.emit("vector", lambda e, pv=pv, ot=ot: (
                e.tensor_scalar(cacc[:, :], pv[:, 0:1024], convw[:, ot * 4:ot * 4 + 1], None,
                                ALU.mult),
                e.scalar_tensor_tensor(cacc[:, :], pv[:, 1:1025], convw[:, ot * 4 + 1:ot * 4 + 2],
                                       cacc[:, :], ALU.mult, ALU.add),
                e.scalar_tensor_tensor(cacc[:, :], pv[:, 2:1026], convw[:, ot * 4 + 2:ot * 4 + 3],
                                       cacc[:, :], ALU.mult, ALU.add),
                e.scalar_tensor_tensor(cacc[:, :], pv[:, 3:1027], convw[:, ot * 4 + 3:ot * 4 + 4],
                                       cacc[:, :], ALU.mult, ALU.add))[-1],
                wait=[("ve", hv)], inc=("ve", 1))
            conv_ve[buf] = cv
            av = pb.emit("scalar", lambda e, ot=ot: e.activation(
                xntok[:, 0:1024], cacc[:, :], AF.Silu, bias=convb[:, ot:ot + 1]),
                wait=[("ve", cv), ("st", max(st_after_ln, xc_st)), ("ld", lds)], inc=("ac", 1))
            xc_st = pb.dma("sync", o_xbc[ts(ot, 128), :], xntok[:, 0:1024], sem="st",
                           wait=[("ac", av)])
            conv_ve[buf] = cv
            pb.wait("vector", [("ac", av)])

        if upto == "conv":
            pb.finish()
            pb.run(block)
            return nc

        l_fa = pb.dma("gpsimd", ffna_sb.rearrange("p (k r) -> p k r", k=32),
              ffn_a.rearrange("(k p) r -> p k r", p=128), sem="cast")
        l_lm = pb.dma("gpsimd", limes_bf[0:8, 0:32], limes[:, :], sem="cast")
        l_b8 = pb.dma("sync", bcol8[0:8, 0:1], ffn_b[0:8, None])

        h_ev = {0: 0, 1: 0}
        h_st = {0: 0, 1: 0}
        for bi in range(8):
            wtb = wts[bi % 2]
            wait_tr = [("cast", ffn_cast_at[bi]),
                       ("pe", pe_done[("h", bi - 2)] if bi >= 2 else inproj_pe)]
            tsem = "trd" if bi % 2 == 0 else "tr2"
            trd_v = None
            for k in range(32):
                trd_v = pb.dmat("sync", wtb[:, ts(k, 512)], ffnw_bf[ts(bi, 512), ts(k, 128)],
                                sem=tsem, wait=wait_tr if k == 0 else ())
            pb.wait("tensor", [(tsem, trd_v)])
            for t_i in range(8):
                ps = pA if t_i % 2 == 0 else pB
                buf = t_i % 2
                pb.wait("tensor", [("ve", h_ev[buf])])
                for k in range(32):
                    pb.emit("tensor", lambda e, k=k, t_i=t_i, ps=ps, wtb=wtb: e.matmul(
                        ps[:], xnT[:, k * 1024 + t_i * 128:k * 1024 + t_i * 128 + 128],
                        wtb[:, ts(k, 512)], start=(k == 0), stop=(k == 31)),
                        inc=("pe", 1) if k == 31 else None)
                mmd = pb.cnt["pe"]
                hdst = evb[:, buf * 1024:buf * 1024 + 512]
                ev = pb.emit("vector", lambda e, ps=ps, bi=bi, hdst=hdst: e.tensor_tensor(
                    hdst, ps[:], bexp_bf[:, ts(bi, 512)], ALU.add),
                    wait=[("pe", mmd), ("st", h_st[buf]),
                          ("cast", ld0[-3] if (bi == 0 and t_i == 0) else 0)], inc=("ve", 1))
                h_ev[buf] = ev
                h_st[buf] = pb.dma("sync", o_h[ts(t_i, 128), ts(bi, 512)], hdst, sem="st",
                                   wait=[("ve", ev)])
            pe_done[("h", bi)] = pb.cnt["pe"]
            if bi == 0:
                for k in range(32):
                    pb.emit("tensor", lambda e, k=k, wtb=wtb: e.matmul(
                        pH[0:8, :], wtb[:, k * 512:k * 512 + 8],
                        xnT[:, k * 1024 + 1:k * 1024 + 1024:2],
                        start=(k == 0), stop=(k == 31)),
                        inc=("pe", 1) if k == 31 else None)
                hsv = pb.emit("vector", lambda e: e.tensor_scalar(
                    hst[0:8, :], pH[0:8, :], bcol8[0:8, 0:1], None, ALU.add),
                    wait=[("pe", pb.cnt["pe"]), ("ld", l_b8)], inc=("ve", 1))
                pb.dma("sync", o_hst[:, :], hst[0:8, :], sem="st", wait=[("ve", hsv)])

        zf_ev = {0: 0, 1: 0}
        for half in range(2):
            ps = pA if half == 0 else pB
            pb.wait("tensor", [("ve", h_ev[half]), ("cast", l_fa)])
            for k in range(32):
                pb.emit("tensor", lambda e, k=k, half=half, ps=ps: e.matmul(
                    ps[0:32, :], ffna_sb[:, ts(k, 32)],
                    xnT[:, k * 1024 + half * 512:k * 1024 + half * 512 + 512],
                    start=(k == 0), stop=(k == 31)),
                    inc=("pe", 1) if k == 31 else None)
            zf_ev[half] = pb.emit("vector", lambda e, half=half, ps=ps: e.tensor_copy(
                zffn[0:32, ts(half, 512)], ps[0:32, :]),
                wait=[("pe", pb.cnt["pe"])], inc=("ve", 1))
        ffn0_pe = pb.cnt["pe"]
        pb.dma("sync", o_zffn[:, :], zffn[0:32, :], sem="st", wait=[("ve", zf_ev[1])])
        zb_ev = pb.emit("vector", lambda e: e.tensor_copy(zffn_bf[0:32, :], zffn[0:32, :]),
                        wait=[("ve", zf_ev[1])], inc=("ve", 1))
        lt_tr = pb.emit("tensor", lambda e: e.transpose(pT[0:32, 0:8], limes_bf[0:8, 0:32],
                                                        ident_bf[0:8, 0:8]),
                        wait=[("ld", l_lm)], inc=("pe", 1))
        lt_ev = pb.emit("vector", lambda e: e.tensor_copy(limesT_bf[0:32, 0:8], pT[0:32, 0:8]),
                        wait=[("pe", lt_tr)], inc=("ve", 1))
        dmm = pb.emit("tensor", lambda e: e.matmul(
            pH[0:8, :], limesT_bf[0:32, 0:8], zffn_bf[0:32, 1:1024:2], start=True, stop=True),
            wait=[("ve", max(zb_ev, lt_ev))], inc=("pe", 1))
        dsv = pb.emit("vector", lambda e: e.tensor_copy(dst_sl[0:8, :], pH[0:8, :]),
                      wait=[("pe", dmm)], inc=("ve", 1))
        pb.dma("sync", o_dst[:, :], dst_sl[0:8, :], sem="st", wait=[("ve", dsv)])

        lmx = stat[0:8, 32:34]
        def lmax_fn(e):
            e.tensor_reduce(lmx[0:8, 0:1], hst[0:8, :], AX, ALU.max, apply_absolute_value=True)
            return e.tensor_reduce(lmx[0:8, 1:2], dst_sl[0:8, :], AX, ALU.max,
                                   apply_absolute_value=True)
        lv = pb.emit("vector", lmax_fn, wait=[("ve", dsv)], inc=("ve", 1))
        ltr = pb.emit("tensor", lambda e: e.transpose(pR[0:2, 0:8], lmx,
                                                      ident_f[0:8, 0:8]),
                      wait=[("ve", lv), ("ve", ac_tr)], inc=("pe", 1))
        lm2 = pb.emit("vector", lambda e: e.tensor_reduce(stat[0:2, 40:41], pR[0:2, 0:8], AX,
                                                          ALU.max),
                      wait=[("pe", ltr)], inc=("ve", 1))
        ltr2 = pb.emit("tensor", lambda e: e.transpose(pR[0:1, 128:130], stat[0:2, 40:41],
                                                       ident_f[0:2, 0:2]),
                       wait=[("ve", lm2)], inc=("pe", 1))
        lm3 = pb.emit("vector", lambda e: e.tensor_copy(stat[0:1, 48:50], pR[0:1, 128:130]),
                      wait=[("pe", ltr2)], inc=("ve", 1))
        pb.dma("sync", o_locmax[:, :], stat[0:1, 48:50], sem="st", wait=[("ve", lm3)])

        if upto == "ffn0":
            pb.finish()
            pb.run(block)
            return nc

        xdt = wt_t[:, 0:16384]
        xdd = wt_t[:, 16384:32768]
        bct = xnt_t[:, 0:4096]
        fbf = xnt_t[:, 4096:12288]
        s_sb = xnt_t[:, 12288:20480]
        btok = xnt_t[:, 20480:22528]
        gmask = xnt_t[:, 22528:26624]
        ltb = xnt_t[:, 26624:27136]
        glb = xnt_t[:, 27136:27648]
        bdb = xnt_t[:, 27648:27904]
        xdtr = xnt_t[:, 27904:28160]
        ybuf = xnt_t[:, 28160:29184]

        n_chunks = NCH
        n_heads = NH
        ssm_parts = "ABCDE"
        if upto.startswith("ssm"):
            tag = upto[3:]
            if tag and tag[0] in "ABCD":
                ssm_parts = "ABCD"[: "ABCD".index(tag[0]) + 1]
                tag = tag[1:]
            if tag:
                parts = tag.split("h")
                n_chunks = int(parts[0]) if parts[0] else 1
                n_heads = int(parts[1]) if len(parts) > 1 else NH
            else:
                n_chunks, n_heads = 1, 0
        o_xbc_r = o_xbc.rearrange("(k p) t -> p k t", p=128)
        y_st = {}
        y_ev_g = 0
        prev_dst_ev = 0
        prev_fv = 0
        prev_bct_pe = 0
        fv = None
        SKIP = os.environ.get("SSM_SKIP", "")
        for c in range(n_chunks):
            cs = slice(c * 256, c * 256 + 256)
            pb.wait("tensor", [("ve", lm3 if c == 0 else prev_fv)])
            ldx2 = 0
            if "l" not in SKIP:
                ldx2 = pb.dma("sync", bct, o_xbc_r[:, 64:80, cs], sem="ld",
                              wait=[("pe", ffn0_pe if c == 0 else prev_bct_pe),
                                    ("st", xc_st), ("ve", prev_fv)])
            vna = 0
            if "v" not in SKIP:
                vna = pb.emit("vector", lambda e, cs=cs: e.tensor_scalar(
                    nacst[:, 0:256], acst[:, cs], -1.0, None, ALU.mult),
                    wait=[("ve", max(ac_tr, prev_dst_ev))], inc=("ve", 1))
            a_ds = 0
            if "a" not in SKIP:
                a_ds = pb.emit("scalar", lambda e, c=c, cs=cs: e.activation(
                    sqs[:, 1024:1280], acs[:, cs], AF.Exp,
                    bias=acs[:, c * 256 + 255:c * 256 + 256], scale=-1.0),
                    wait=[("ve", max(v_sc, prev_dst_ev))], inc=("ac", 1))
            dclamp = sqs[:, 1280:1792]
            tr_ev = 0
            for st_i in range(0 if "t" in SKIP else 2):
                t1 = pb.emit("tensor", lambda e, st_i=st_i: e.transpose(
                    pR[:, 0:128], sqs[:, 1024 + st_i * 128:1024 + st_i * 128 + 128], ident_f[:]),
                    wait=[("ac", a_ds), ("ve", tr_ev)], inc=("pe", 1))
                e1 = pb.emit("vector", lambda e, st_i=st_i: e.tensor_copy(
                    dsT[:, ts(st_i, 128)], pR[:, 0:128]), wait=[("pe", t1)], inc=("ve", 1))
                t2 = pb.emit("tensor", lambda e, c=c, st_i=st_i: e.transpose(
                    pR[:, 128:256], dtf[:, c * 256 + st_i * 128:c * 256 + st_i * 128 + 128],
                    ident_f[:]), wait=[("ve", v_dt), ("ve", e1)], inc=("pe", 1))
                e2 = pb.emit("vector", lambda e, st_i=st_i: e.tensor_copy(
                    dtT[:, ts(st_i, 128)], pR[:, 128:256]), wait=[("pe", t2)], inc=("ve", 1))
                tr_ev = e2
            prev_dst_ev = tr_ev
            xv = 0
            xl_prev = 0
            for st_i in range(2 if "B" in ssm_parts else 0):
                for kb in range(64):
                    bufb = kb % 2
                    xsem = "xa" if bufb == 0 else "xb"
                    xl = pb.dmat("sync", xdtr[:, ts(bufb, 128)],
                                 o_xbc[ts(kb, 128), c * 256 + st_i * 128:c * 256 + st_i * 128 + 128],
                                 sem=xsem,
                                 wait=([("st", xc_st), ("pe", ffn0_pe), ("ve", xv)]
                                       if (c == 0 and st_i == 0 and kb < 2) else
                                       ([("ve", xv)] if kb >= 2 or st_i or c else ())))
                    h2 = 2 * kb

                    def evx(e, kb=kb, st_i=st_i, h2=h2, bufb=bufb):
                        e.tensor_tensor(
                            xdt[:, st_i * 8192 + kb * 128:st_i * 8192 + kb * 128 + 128],
                            xdtr[:, ts(bufb, 128)],
                            dtT[:, st_i * 128 + h2:st_i * 128 + h2 + 2][:, :, None]
                            .broadcast_to([128, 2, PD]), ALU.mult)
                        return e.tensor_tensor(
                            xdd[:, st_i * 8192 + kb * 128:st_i * 8192 + kb * 128 + 128],
                            xdtr[:, ts(bufb, 128)],
                            dbc[:, h2:h2 + 2][:, :, None].broadcast_to([128, 2, PD]), ALU.mult)
                    xv = pb.emit("vector", evx,
                                 wait=[(xsem, xl), ("ve", tr_ev)] +
                                 ([("pe", ffn0_pe)] if (c == 0 and st_i == 0 and kb == 0) else []),
                                 inc=("ve", 1))
            xdt_rdy = xv
            if c == 0 and os.environ.get("DBG_XDT"):
                pb.dma("sync", o_dbg1[:, :], wt_t[:, 0:16384], sem="st",
                       wait=[("ve", xv)])
                pb.dma("sync", o_dbg3[:, :], dsT[:, :], sem="st", wait=[("ve", tr_ev)])
            gvv = {}
            btv = {}
            g_ev = 0
            for gi in range(8 if "C" in ssm_parts else 0):
                pb.wait("tensor", [("ve", g_ev)])
                for st_i in range(2):
                    pb.emit("tensor", lambda e, gi=gi, st_i=st_i: e.matmul(
                        pA[:, ts(st_i, 256)],
                        bct[:, gi * 256 + st_i * 128:gi * 256 + st_i * 128 + 128],
                        bct[:, (8 + gi) * 256:(8 + gi) * 256 + 256],
                        start=(st_i == 0), stop=(st_i == 1), skip_group_check=(st_i == 1)),
                        wait=([("ld", ldx2)] + ([("ve", lm3)] if c == 0 else []))
                        if (gi == 0 and st_i == 0) else (),
                        inc=("pe", 1) if st_i == 1 else None)
                gmm = pb.cnt["pe"]
                g_ev = pb.emit("vector", lambda e, gi=gi: (
                    e.tensor_tensor(gmask[:, gi * 512:gi * 512 + 256], pA[:, 0:256],
                                    tril[:, 0:256], ALU.mult),
                    e.tensor_tensor(gmask[:, gi * 512 + 256:gi * 512 + 512], pA[:, 256:512],
                                    tril[:, 256:512], ALU.mult))[-1],
                    wait=[("pe", gmm)], inc=("ve", 1))
                gvv[gi] = g_ev
                for st_i in range(2):
                    btv[(gi, st_i)] = pb.dmat(
                        "sync", btok[:, (gi * 2 + st_i) * 128:(gi * 2 + st_i) * 128 + 128],
                        o_xbc[8192 + gi * 128:8192 + gi * 128 + 128,
                              c * 256 + st_i * 128:c * 256 + st_i * 128 + 128],
                        sem="ts",
                        wait=[("st", xc_st), ("pe", ffn0_pe)] if (c == 0 and gi == 0 and st_i == 0)
                        else ())
                btv[gi] = btv[(gi, 1)]
            btok_rdy = btv.get(7, 0)
            if c == 0 and os.environ.get("DBG_XDT"):
                pb.dma("sync", o_dbg2[:, :], btok[:, :], sem="st", wait=[("ts", btok_rdy)])
            prev_bct_pe = pb.cnt["pe"]

            y_ev = y_ev_g
            prev_bc_pe = 0
            prev_lt_ac = 0
            prev_gl_ve = 0
            prev_dcl = 0
            prev_mm = 0
            for h in range(n_heads if "D" in ssm_parts else 0):
                gi = h // 16
                hm = h % 8
                fresh = hm == 0
                rx = pb.emit("tensor", lambda e, h=h, cs=cs: e.matmul(
                    pR[0:1, 256:512], ident_f[:, h:h + 1], acs[:, cs], start=True, stop=True,
                    skip_group_check=True),
                    wait=[("ve", prev_dcl), ("ve", tr_ev if h == 0 else 0)], inc=("pe", 1))
                re_ = pb.emit("vector", lambda e: e.tensor_copy(rowf[0:1, 0:256], pR[0:1, 256:512]),
                              wait=[("pe", rx)], inc=("ve", 1))
                bc = pb.emit("tensor", lambda e: e.matmul(
                    pR[:, 256:512], ones_f[0:1, :], rowf[0:1, 0:256], start=True, stop=True,
                    skip_group_check=True),
                    wait=[("ve", re_)], inc=("pe", 1))
                v_dcl = pb.emit("vector", lambda e, h=h: (
                    e.tensor_scalar(dclamp[:, 0:256], pR[:, 256:512],
                                    nacst[:, h:h + 1], 0.0, ALU.add, ALU.min),
                    e.tensor_scalar(dclamp[:, 256:512], pR[:, 256:512],
                                    nacst[:, 128 + h:128 + h + 1], 0.0, ALU.add, ALU.min))[-1],
                    wait=[("pe", bc), ("ve", vna), ("ac", prev_lt_ac)], inc=("ve", 1))
                a_lt = pb.emit("scalar", lambda e: (
                    e.activation(ltb[:, 0:256], dclamp[:, 0:256], AF.Exp),
                    e.activation(ltb[:, 256:512], dclamp[:, 256:512], AF.Exp))[-1],
                    wait=[("ve", v_dcl), ("ve", prev_gl_ve)], inc=("ac", 1))
                prev_lt_ac = a_lt
                prev_dcl = v_dcl
                v_gl = pb.emit("vector", lambda e, gi=gi: (
                    e.tensor_tensor(glb[:, 0:256], ltb[:, 0:256],
                                    gmask[:, gi * 512:gi * 512 + 256], ALU.mult),
                    e.tensor_tensor(glb[:, 256:512], ltb[:, 256:512],
                                    gmask[:, gi * 512 + 256:gi * 512 + 512], ALU.mult))[-1],
                    wait=[("ac", a_lt), ("ve", gvv[gi]), ("pe", prev_mm)], inc=("ve", 1))
                prev_gl_ve = v_gl
                v_bd = pb.emit("vector", lambda e, gi=gi, h=h: (
                    e.tensor_scalar(bdb[:, 0:128], btok[:, (gi * 2) * 128:(gi * 2) * 128 + 128],
                                    dsT[:, h:h + 1], None, ALU.mult),
                    e.tensor_scalar(bdb[:, 128:256],
                                    btok[:, (gi * 2 + 1) * 128:(gi * 2 + 1) * 128 + 128],
                                    dsT[:, 128 + h:128 + h + 1], None, ALU.mult))[-1],
                    wait=[("ts", btok_rdy), ("ve", xdt_rdy), ("pe", prev_mm)], inc=("ve", 1))
                if fresh:
                    pb.wait("tensor", [("ve", y_ev)])
                pb.wait("tensor", [("ve", max(v_gl, v_bd))])
                for lt_i in range(2):
                    pYx = pY0 if lt_i == 0 else pY1
                    for st_i in range(2):
                        pb.emit("tensor", lambda e, lt_i=lt_i, st_i=st_i, h=h, hm=hm, pYx=pYx,
                                fresh=fresh: e.matmul(
                            pYx[:, hm * 64:hm * 64 + 64],
                            glb[:, st_i * 256 + lt_i * 128:st_i * 256 + lt_i * 128 + 128],
                            xdt[:, st_i * 8192 + h * 64:st_i * 8192 + h * 64 + 64],
                            start=(fresh and st_i == 0), stop=False,
                            skip_group_check=not (fresh and st_i == 0)))
                for st_i in range(2):
                    pb.emit("tensor", lambda e, st_i=st_i, h=h, hm=hm, fresh=fresh: e.matmul(
                        pS[:, hm * 64:hm * 64 + 64], bdb[:, ts(st_i, 128)],
                        xdt[:, st_i * 8192 + h * 64:st_i * 8192 + h * 64 + 64],
                        start=(fresh and st_i == 0), stop=False,
                        skip_group_check=not (fresh and st_i == 0)),
                        inc=("pe", 1) if st_i == 1 else None)
                prev_mm = pb.cnt["pe"]
                if hm == 7:
                    hg = h // 8

                    def yev(e, hg=hg):
                        e.tensor_tensor(ybuf[:, 0:512], pY0[:],
                                        xdd[:, 0 * 8192 + hg * 512:0 * 8192 + hg * 512 + 512],
                                        ALU.add)
                        e.tensor_tensor(ybuf[:, 512:1024], pY1[:],
                                        xdd[:, 1 * 8192 + hg * 512:1 * 8192 + hg * 512 + 512],
                                        ALU.add)
                        return e.tensor_copy(s_sb[:, hg * 512:hg * 512 + 512], pS[:])
                    y_ev = pb.emit("vector", yev,
                                   wait=[("pe", prev_mm), ("st", y_st.get(0, 0))], inc=("ve", 1))
                    y_st[0] = pb.dma("sync", o_y1[c * 256:c * 256 + 128,
                                                  hg * 512:hg * 512 + 512],
                                     ybuf[:, 0:512], sem="st", wait=[("ve", y_ev)])
                    y_st[0] = pb.dma("sync", o_y1[c * 256 + 128:c * 256 + 256,
                                                  hg * 512:hg * 512 + 512],
                                     ybuf[:, 512:1024], sem="st", wait=[("ve", y_ev)])
            s_done = y_ev
            y_ev_g = y_ev
            prev_bct_pe = pb.cnt["pe"]
            sd = 0
            if "s" not in SKIP:
                sd = pb.dma("sync", o_s[ts(c, 128), :], s_sb[:, :], sem="st",
                            wait=[("ve", s_done)])
            if "f" in SKIP:
                fv = 0
            elif c == 0:
                fv = pb.emit("vector", lambda e: e.tensor_copy(fbf[:, :], s_sb[:, :]),
                             wait=[("ve", s_done)], inc=("ve", 1))
            else:
                a_ea = pb.emit("scalar", lambda e, c=c: e.activation(
                    SM(7), acs[:, c * 256 + 255:c * 256 + 256], AF.Exp),
                    wait=[("ve", v_sc)], inc=("ac", 1))
                t_ea = pb.emit("tensor", lambda e: e.transpose(pR[0:1, 0:128], SM(7),
                                                               ident_f[:]),
                               wait=[("ac", a_ea), ("ve", prev_dcl)], inc=("pe", 1))
                e_ea = pb.emit("vector", lambda e: e.tensor_copy(earow[0:1, 0:128],
                                                                 pR[0:1, 0:128]),
                               wait=[("pe", t_ea)], inc=("ve", 1))
                b_ea = pb.emit("tensor", lambda e: e.matmul(
                    pR[:, 128:256], ones_f[0:1, :], earow[0:1, 0:128], start=True, stop=True,
                    skip_group_check=True),
                    wait=[("ve", e_ea)], inc=("pe", 1))
                e2_ea = pb.emit("vector", lambda e: e.tensor_copy(eacol[:, :], pR[:, 128:256]),
                                wait=[("pe", b_ea)], inc=("ve", 1))
                fv = pb.emit("vector", lambda e: (
                    e.tensor_tensor(fbf[:, :], fbf[:, :],
                                    eacol[:, :, None].broadcast_to([128, 128, PD]), ALU.mult),
                    e.tensor_tensor(fbf[:, :], fbf[:, :], s_sb[:, :], ALU.add))[-1],
                    wait=[("ve", max(e2_ea, s_done))], inc=("ve", 1))
            pb.wait("vector", [("st", sd)])
            prev_fv = fv or 0
        pb.dma("gpsimd", o_f[:, :], fbf[:, :], sem="stg", wait=[("ve", fv)])

        pb.finish()
        pb.run(block)
    return nc



def build_launch2():
    nc = bass.Bass("TRN2", target_bir_lowering=False)
    dI = lambda n, s, d=F32: nc.dram_tensor(n, s, d, kind="ExternalInput").ap()
    dO = lambda n, s, d=F32: nc.dram_tensor(n, s, d, kind="ExternalOutput").ap()
    dT = lambda n, s, d=BF16: nc.dram_tensor(n, s, d, kind="Internal").ap()

    x_main = dI("x_main", [T, D])
    i_xn = dI("i_xn", [T, D], BF16)
    i_xbc = dI("i_xbc", [CONV_DIM, T], BF16)
    i_z = dI("i_z", [T, DI], BF16)
    i_acs = dI("i_acs", [NH, T])
    i_acst = dI("i_acst", [T, NH])
    i_y1 = dI("i_y1", [T, DI], BF16)
    i_s = dI("i_s", [NCH * NS, DI], BF16)
    i_h = dI("i_h", [T, D], BF16)
    i_hst = dI("i_hst", [8, 512])
    i_dst = dI("i_dst", [8, 512])
    i_zffn = dI("i_zffn", [32, T])
    g_f = dI("g_f", [W * NS, DI])
    g_a = dI("g_a", [W * NH, 1])
    g_max = dI("g_max", [W, 2])
    cmask = dI("cmask", [W])
    usuf = dI("usuf", [W, W])
    wo_w = dI("wo_w", [D, DI])
    bmat = dI("bmat", [32, D])
    limes = dI("limes", [8, 32])
    norm_l = dI("norm_l", [128, 64])
    cst = {n: dI(n, s, d) for n, s, d in CONST_SPECS}
    c_eps6 = dI("c_eps6", [1, 1])
    c_eps9 = dI("c_eps9", [1, 1])

    b_x1 = dO("b_x1", [T, D])
    b_dbgA = dO("b_dbgA", [128, 64])
    b_dbgB = dO("b_dbgB", [32, 1024])
    b_dbgC = dO("b_dbgC", [8, 512])
    b_dbgD = dO("b_dbgD", [128, 32])
    b_dbgE = dO("b_dbgE", [8, 64])
    b_dbgF = dO("b_dbgF", [128, 48])

    wo_bf = dT("wo_bf2", [D, DI])
    bmat_b = dT("bmat_b2", [32, D])
    g_dram = nc.dram_tensor("b_g", [T, D], BF16, kind="ExternalOutput").ap()
    rt_dram = dT("rt_dram2", [16, 512])
    wt_dram = dT("wt_dram2", [4, 128, 128])

    with contextlib.ExitStack() as xs:
        E = xs.enter_context
        b1 = E(nc.sbuf_tensor("L2b1", [128, 32 * 1024], BF16))
        b2 = E(nc.sbuf_tensor("L2b2", [128, 32 * 1024], BF16))
        m2 = E(nc.sbuf_tensor("L2m2", [128, 14 * 1024], BF16))
        fa = E(nc.sbuf_tensor("L2f32", [128, 12160], F32))
        pA = E(nc.psum_tensor("qA", [128, 512], F32))
        pB = E(nc.psum_tensor("qB", [128, 512], F32))
        pC = E(nc.psum_tensor("qC", [128, 512], F32))
        pD = E(nc.psum_tensor("qD", [128, 512], F32))
        pR = E(nc.psum_tensor("qR", [128, 512], F32))
        pT = E(nc.psum_tensor("qT", [128, 256], BF16))
        block = E(nc.Block())
        pb = PB(nc)
        for s in ["cast", "trd", "tr2", "ld", "ld2", "ld3", "pe", "ve", "ac", "gp",
                  "st", "st2", "stg"]:
            pb.new_sem(E(nc.semaphore("q_" + s)), s)

        b1v, b2v, m2v, fav = b1[:], b2[:], m2[:], fa[:]
        y_sl = b1v[:, 0:16384]
        z_sl = b1v[:, 16384:32768]
        fm_f = b1v[:, 0:16384].bitcast(F32)
        yT = b2v[:, 0:16384]
        sq_bf = b2v[:, 0:16384]
        h_sl = b2v[:, 0:8192]
        tmp_cmb = b2v[:, 0:16384].bitcast(F32)
        s_sl = b2v[:, 16384:24576]
        bm_sb = b2v[0:32, 16384:20480]
        r_bf = b2v[:, 24576:32768]
        cT = m2v[:, 0:2048]
        wob = m2v[:, 2048:3072]
        ident_bf = m2v[:, 3072:3200]
        limes_bf2 = m2v[0:8, 3200:3232]
        mskq = m2v[:, 3776:3808].bitcast(U8)
        zpb = m2v[0:32, 3264:4288]
        wfb2 = m2v[0:8, 4288:5312]
        t1tmp = m2v[:, 5312:6336].bitcast(F32)
        gsil = m2v[:, 6336:7360]
        slots = m2v[:, 7360:13536 - 1248]
        sl0 = 7360
        def slot(buf, what):
            base = sl0 + buf * 3072
            off = {"xn": 0, "g": 512, "xin": 1024, "x1": 2048}[what]
            seg = m2v[:, base + off: base + off + (512 if what in ("xn", "g") else 1024)]
            return seg if what in ("xn", "g") else seg.bitcast(F32)
        assert sl0 + 2 * 3072 <= 14 * 1024 - 256
        Rf = fav[:, 0:8192]
        acs_sb = fav[:, 8192:9216]
        odT = fav[:, 9216:9472]
        acst_sl = fav[:, 9472:9728]
        eacol = fav[:, 9728:9856]
        earow = fav[:, 9856:9984]
        stat = fav[:, 9984:10048]
        ident_f = fav[:, 10048:10176]
        hstb = fav[0:8, 10176:10688]
        dstb = fav[0:8, 10688:11200]
        lgA = fav[:, 11200:11232]
        prbs = fav[:, 11232:11264]
        knk = fav[:, 11264:11296]
        wn_f = fav[:, 11296:11328]
        gA8 = fav[0:8, 11328:11456]
        wnT = fav[0:8, 11456:11968]
        zeros8 = fav[:, 11968:12000]
        cm8 = fav[0:8, 12000:12004]
        us8 = fav[0:8, 12004:12012]
        whm = fav[:, 12016:12024]
        nw_l = fav[:, 12032:12096]
        gmx = fav[0:8, 12096:12098]
        zf_f = fav[32:64, 10176:11200]
        gt_f = fav[:, 12098 + 2:12160 - 48]

        def SC2(i, rows=128):
            return stat[0:rows, 4 * i:4 * i + 1]

        ld0 = [
            pb.dma("sync", ident_bf, cst["c_ident_bf"]),
            pb.dma("sync", ident_f, cst["c_ident_f32"]),
            pb.dma("sync", acs_sb, i_acs[:, :]),
            pb.dma("sync", SC2(15, 1), cst["c_eps"][0:1, 0:1]),
            pb.dma("sync", zeros8, cst["c_zeros"][:, 0:32].broadcast_to([128, 32])),
            pb.dma("sync", nw_l, norm_l[:, :]),
            pb.dma("sync", gmx, g_max[:, :]),
            pb.dma("sync", gA8, g_a.rearrange("(m h) one -> m (h one)", h=128)),
            pb.dma("sync", cm8[0:8, 0:1], cmask[:, None]),
            pb.dma("sync", us8[0:8, 0:8], usuf[:, :]),
            pb.dma("sync", SC2(14, 1), c_eps6[0:1, 0:1]),
            pb.dma("sync", SC2(13, 1), c_eps9[0:1, 0:1]),
            pb.dma("sync", hstb, i_hst[:, :]),
            pb.dma("sync", dstb, i_dst[:, :]),
            pb.dma("sync", zf_f[0:32, 0:1024], i_zffn[:, :]),
        ]
        ones_f = ident_f
        onesr = fav[0:1, 12104:12232 - 0] if False else None
        onesr = fav[:, 12100:12100]
        ones_f = fav[:, 12032 - 192:12032 - 64]
        ones_f = fav[:, 12128:12160]
        ones_f = m2v[:, 3264 - 256:3264].bitcast(F32)
        ones_f = m2v[:, 13536 - 1248:12288].bitcast(F32) if False else None
        ones_f = m2v[:, 14336 - 256:14336].bitcast(F32)
        ld0.append(pb.dma("sync", ones_f[0:1, :], cst["c_ones_f32"]))
        lds = max(ld0)

        cast_wo = [pb.dma("gpsimd", wo_bf[ts(b, 512), :], wo_w[ts(b, 512), :], sem="cast")
                   for b in range(8)]
        c_lm = pb.dma("gpsimd", limes_bf2, limes[:, :], sem="cast")
        c_bm = pb.dma("gpsimd", bmat_b[:, :], bmat[:, :], sem="cast")
        l_bm = pb.dma("sync", bm_sb, bmat_b[:, :], wait=[("cast", c_bm)])

        t_gm = pb.emit("tensor", lambda e: (
            e.transpose(pR[0:1, 0:8], gmx[0:8, 0:1], ident_f[0:8, 0:8]),
            e.transpose(pR[0:1, 8:16], gmx[0:8, 1:2], ident_f[0:8, 0:8]))[-1],
            wait=[("ld", lds)], inc=("pe", 1))
        v_g2 = pb.emit("vector", lambda e: (
            e.tensor_reduce(stat[0:1, 32:33], pR[0:1, 0:8], AX, ALU.max),
            e.tensor_reduce(stat[0:1, 33:34], pR[0:1, 8:16], AX, ALU.max))[-1],
            wait=[("pe", t_gm), ("ld", lds)], inc=("ve", 1))
        v_g4 = pb.emit("vector", lambda e: e.tensor_scalar(stat[0:1, 32:34], stat[0:1, 32:34],
                                                           SC2(14, 1), None, ALU.max),
                       wait=[("ve", v_g2)], inc=("ve", 1))
        a_g5 = pb.emit("scalar", lambda e: (
            e.activation(stat[0:1, 36:38], stat[0:1, 32:34], AF.Ln),
            e.activation(stat[0:1, 40:42], stat[0:1, 36:38], AF.Exp, scale=-1.0))[-1],
            wait=[("ve", v_g4)], inc=("ac", 1))
        v_g6 = pb.emit("vector", lambda e: e.tensor_scalar(stat[0:1, 44:46], stat[0:1, 40:42],
                                                           0.5, None, ALU.mult),
                       wait=[("ac", a_g5)], inc=("ve", 1))
        t_h = pb.emit("tensor", lambda e: e.matmul(pR[0:8, 16:17], ones_f[0:1, 0:8],
                                                   stat[0:1, 44:45], start=True, stop=True,
                                                   skip_group_check=True),
                      wait=[("ve", v_g6)], inc=("pe", 1))
        t_d = pb.emit("tensor", lambda e: e.matmul(pR[0:8, 20:21], ones_f[0:1, 0:8],
                                                   stat[0:1, 45:46], start=True, stop=True,
                                                   skip_group_check=True),
                      inc=("pe", 1))
        t_e9 = pb.emit("tensor", lambda e: (
            e.matmul(pR[0:128, 24:25], ones_f[0:1, 0:128], SC2(13, 1),
                     start=True, stop=True, skip_group_check=True),
            e.matmul(pR[0:128, 28:29], ones_f[0:1, 0:128], SC2(15, 1),
                     start=True, stop=True, skip_group_check=True))[-1],
            inc=("pe", 1))
        v_hd = pb.emit("vector", lambda e: (
            e.tensor_copy(SC2(1, 8), pR[0:8, 16:17]),
            e.tensor_copy(SC2(2, 8), pR[0:8, 20:21]),
            e.tensor_copy(SC2(13), pR[0:128, 24:25]),
            e.tensor_copy(SC2(15), pR[0:128, 28:29]))[-1],
            wait=[("pe", t_e9)], inc=("ve", 1))
        v_snap = pb.emit("vector", lambda e: (
            e.tensor_copy(stat[0:2, 34:42], pR[0:2, 0:8]),
            e.tensor_copy(stat[0:2, 42:43], stat[0:2, 0:1]),
            e.tensor_copy(stat[0:1, 16:18], pR[0:1, 8:10]),
            e.tensor_copy(stat[0:1, 18:20], stat[0:1, 32:34]),
            e.tensor_copy(stat[0:1, 20:22], stat[0:1, 36:38]),
            e.tensor_copy(stat[0:1, 22:24], stat[0:1, 40:42]),
            e.tensor_copy(stat[0:1, 24:26], stat[0:1, 44:46]))[-1],
            wait=[("ve", v_hd)], inc=("ve", 1))
        dmp = pb.dma("sync", b_dbgE[:, :], stat[0:8, :], sem="st", wait=[("ve", v_snap)])

        v_lg = pb.emit("vector", lambda e: (
            e.tensor_scalar(hstb, hstb, SC2(1, 8), None, ALU.mult),
            e.scalar_tensor_tensor(hstb, dstb, SC2(2, 8), hstb, ALU.mult, ALU.add))[-1],
            wait=[("ve", v_hd), ("st", dmp)], inc=("ve", 1))
        lgb = m2v[0:8, 3232:3744]
        v_lgb = pb.emit("vector", lambda e: e.tensor_copy(lgb, hstb),
                        wait=[("ve", v_lg)], inc=("ve", 1))
        s_lgb = pb.dma("sync", rt_dram[0:8, :], lgb, sem="st2", wait=[("ve", v_lgb)])
        tr_done = 0
        lgA_bf16x = m2v[:, 3744:3744 + 64]
        tdl = []
        for a4 in range(4):
            tdl.append(pb.dmat("sync", lgA_bf16x[:, a4 * 16:a4 * 16 + 16],
                               rt_dram[0:16, ts(a4, 128)],
                               sem="trd" if a4 % 2 == 0 else "tr2",
                               wait=[("st2", s_lgb)] if a4 == 0 else ()))
        tr_done = pb.emit("vector", lambda e: e.tensor_copy(
            lgA[:, 0:32].rearrange("p (a e) -> p a e", a=4),
            lgA_bf16x.rearrange("p (a e) -> p a e", a=4)[:, :, 0:8]),
            wait=[("trd", tdl[2]), ("tr2", tdl[3])], inc=("ve", 1))
        wn_done = 0
        for a4 in range(4):
            lg = lgA[:, ts(a4, 8)]
            v_m1 = pb.emit("vector", lambda e, lg=lg: e.tensor_reduce(SC2(3), lg, AX, ALU.max),
                           wait=[("ve", tr_done), ("ve", wn_done)], inc=("ve", 1))
            a_nm = pb.emit("scalar", lambda e: e.mul(SC2(4), SC2(3), -1.0),
                           wait=[("ve", v_m1)], inc=("ac", 1))
            p_sb = pb.emit("vector", lambda e, lg=lg: e.tensor_scalar(
                knk[:, 0:8], lg, SC2(4), None, ALU.add),
                wait=[("ac", a_nm)], inc=("ve", 1))
            a_ex = pb.emit("scalar", lambda e: e.activation(prbs[:, 0:8], knk[:, 0:8], AF.Exp),
                           wait=[("ve", p_sb)], inc=("ac", 1))
            v_sm = pb.emit("vector", lambda e: e.tensor_reduce(SC2(5), prbs[:, 0:8], AX,
                                                               ALU.add),
                           wait=[("ac", a_ex)], inc=("ve", 1))
            a_ri = pb.emit("scalar", lambda e: (
                e.activation(SC2(6), SC2(5), AF.Ln),
                e.activation(SC2(7), SC2(6), AF.Exp, scale=-1.0))[-1],
                wait=[("ve", v_sm)], inc=("ac", 1))
            v_pr = pb.emit("vector", lambda e: e.tensor_scalar(prbs[:, 0:8], prbs[:, 0:8],
                                                               SC2(7), None, ALU.mult),
                           wait=[("ac", a_ri)], inc=("ve", 1))
            prb2 = knk[:, 8:16]
            knk2 = knk[:, 16:24]
            a_cp = pb.emit("scalar", lambda e: e.copy(prb2, prbs[:, 0:8]),
                           wait=[("ve", v_pr)], inc=("ac", 1))
            v_t1 = pb.emit("vector", lambda e: e.tensor_reduce(SC2(8), prb2, AX, ALU.max),
                           wait=[("ac", a_cp)], inc=("ve", 1))
            a_t1 = pb.emit("scalar", lambda e: e.copy(SC2(9), SC2(8)),
                           wait=[("ve", v_t1)], inc=("ac", 1))
            v_kk = pb.emit("vector", lambda e: (
                e.tensor_scalar(mskq[:, 0:8], prb2, SC2(9), None, ALU.is_equal),
                e.select(knk[:, 0:8], mskq[:, 0:8], zeros8[:, 0:8], prb2))[-1],
                wait=[("ac", a_t1)], inc=("ve", 1))
            a_cp2 = pb.emit("scalar", lambda e: e.copy(knk2, knk[:, 0:8]),
                            wait=[("ve", v_kk)], inc=("ac", 1))
            v_t2 = pb.emit("vector", lambda e: e.tensor_reduce(SC2(10), knk2, AX, ALU.max),
                           wait=[("ac", a_cp2)], inc=("ve", 1))
            a_ng = pb.emit("scalar", lambda e: e.mul(SC2(11), SC2(10), -2.0),
                           wait=[("ve", v_t2)], inc=("ac", 1))
            a_sg = pb.emit("scalar", lambda e: e.activation(knk[:, 0:8], prb2,
                                                            AF.Sigmoid, bias=SC2(11),
                                                            scale=2.0), inc=("ac", 1))
            v_w = pb.emit("vector", lambda e: e.tensor_tensor(wn_f[:, 0:8], prb2,
                                                              knk[:, 0:8], ALU.mult),
                          wait=[("ac", a_sg)], inc=("ve", 1))
            w2b = knk[:, 24:32]
            a_ws = pb.emit("scalar", lambda e: e.activation(w2b, wn_f[:, 0:8], AF.Identity),
                           wait=[("ve", v_w)], inc=("ac", 1))
            v_ws2 = pb.emit("vector", lambda e: e.tensor_reduce(SC2(0), w2b, AX, ALU.add),
                            wait=[("ac", a_ws)], inc=("ve", 1))
            a_wi = pb.emit("scalar", lambda e: (
                e.activation(SC2(6), SC2(0), AF.Ln, bias=SC2(13)),
                e.activation(SC2(7), SC2(6), AF.Exp, scale=-1.0))[-1],
                wait=[("ve", v_ws2)], inc=("ac", 1))
            v_wn = pb.emit("vector", lambda e: e.tensor_scalar(wn_f[:, 0:8], w2b,
                                                               SC2(7), None, ALU.mult),
                           wait=[("ac", a_wi)], inc=("ve", 1))
            if a4 == 3:
                v_snp2 = pb.emit("vector", lambda e: (
                    e.tensor_copy(fav[:, 12104:12112], prbs[:, 0:8]),
                    e.tensor_copy(fav[:, 12112:12120], knk[:, 0:8]),
                    e.tensor_copy(fav[:, 12120:12128], w2b),
                    e.tensor_copy(fav[:, 12128:12136], knk2),
                    e.tensor_copy(fav[:, 12136:12137], SC2(5)),
                    e.tensor_copy(fav[:, 12137:12138], SC2(7)),
                    e.tensor_copy(fav[:, 12138:12139], SC2(8)),
                    e.tensor_copy(fav[:, 12139:12140], SC2(10)),
                    e.tensor_copy(fav[:, 12140:12141], SC2(12)),
                    e.tensor_copy(fav[:, 12141:12149], lgA[:, 24:32]),
                    e.tensor_copy(fav[:, 12149:12150], SC2(13)),
                    e.tensor_copy(fav[:, 12150:12151], SC2(0)),
                    e.tensor_copy(fav[:, 12151:12152], SC2(6)))[-1],
                    wait=[("ve", v_wn)], inc=("ve", 1))
                pb.dma("sync", b_dbgF[:, :], fav[:, 12104:12152], sem="st2",
                       wait=[("ve", v_snp2)])
            wnb = m2v[:, 3808:3816]
            v_wb = pb.emit("vector", lambda e: e.tensor_copy(wnb, wn_f[:, 0:8]),
                           wait=[("ve", v_wn)], inc=("ve", 1))
            s_wb = pb.dma("sync", wt_dram[a4, :, 0:8], wnb, sem="st2", wait=[("ve", v_wb)])
            wnT_bf = m2v[0:128, 3840 + a4 * 128:3840 + a4 * 128 + 128]
            t_wb = pb.dmat("sync", wnT_bf, wt_dram[a4, :, :],
                           sem="trd" if a4 % 2 == 0 else "tr2",
                           wait=[("st2", s_wb)])
            wn_done = pb.emit("vector", lambda e, a4=a4, wnT_bf=wnT_bf: e.tensor_copy(
                wnT[0:8, ts(a4, 128)], wnT_bf[0:8, :]),
                wait=[("trd" if a4 % 2 == 0 else "tr2", t_wb)], inc=("ve", 1))
        v_wf = pb.emit("vector", lambda e: e.tensor_copy(
            wfb2, wnT[0:8, :, None].broadcast_to([8, 512, 2])),
            wait=[("ve", wn_done)], inc=("ve", 1))
        pm_ev = 0
        for half in range(2):
            t_p = pb.emit("tensor", lambda e, half=half: e.matmul(
                pA[0:32, :], limes_bf2[0:8, 0:32], wfb2[0:8, 512 * half:512 * half + 512],
                start=True, stop=True),
                wait=[("ve", v_wf), ("cast", c_lm), ("ve", pm_ev)], inc=("pe", 1))
            pm_ev = pb.emit("vector", lambda e, half=half: e.tensor_tensor(
                zpb[0:32, ts(half, 512)], zf_f[0:32, ts(half, 512)], pA[0:32, 0:512],
                ALU.mult),
                wait=[("pe", t_p)], inc=("ve", 1))
        v_zp = pm_ev

        pb.dma("sync", b_dbgA[:, :], stat[:, :], sem="st", wait=[("ve", v_zp)])
        pb.dma("sync", b_dbgB[:, :], zf_f[0:32, 0:1024], sem="st", wait=[("ve", v_zp)])
        pb.dma("sync", b_dbgC[:, :], wnT[0:8, :], sem="st", wait=[("ve", v_zp)])
        pb.dma("sync", b_dbgD[:, :], lgA[:, :], sem="st", wait=[("ve", v_zp)])
        gst = {0: 0, 1: 0, "a": 0}
        hld = {0: 0, 1: 0}
        for t_i in range(8):
            buf = t_i % 2
            lh = pb.dma("sync", h_sl[:, ts(buf, 4096)], i_h[ts(t_i, 128), :],
                        sem="ld2" if buf == 0 else "ld3",
                        wait=[("ve", hld[buf])] if hld[buf] else ())
            last_gt = 0
            for oc in range(8):
                ps = pC if oc % 2 == 0 else pD
                t_g = pb.emit("tensor", lambda e, t_i=t_i, oc=oc, ps=ps: e.matmul(
                    ps[:], zpb[0:32, ts(t_i, 128)], bm_sb[0:32, ts(oc, 512)],
                    start=True, stop=True),
                    wait=[("ve", v_zp), ("ld", l_bm),
                          ("ld2" if buf == 0 else "ld3", lh),
                          ("ac", gst["a"])], inc=("pe", 1))
                v_gt = pb.emit("vector", lambda e, oc=oc, buf=buf, ps=ps: e.scalar_tensor_tensor(
                    t1tmp[:, 0:512], ps[:], 0.5,
                    h_sl[:, buf * 4096 + oc * 512:buf * 4096 + oc * 512 + 512],
                    ALU.mult, ALU.add),
                    wait=[("pe", t_g), ("ac", gst["a"])], inc=("ve", 1))
                a_gs = pb.emit("scalar", lambda e: e.activation(gsil[:, 0:512],
                                                                t1tmp[:, 0:512], AF.Silu),
                               wait=[("ve", v_gt), ("st", gst[buf])], inc=("ac", 1))
                gst["a"] = a_gs
                gst[buf] = pb.dma("sync", g_dram[ts(t_i, 128), ts(oc, 512)], gsil[:, 0:512],
                                  sem="st", wait=[("ac", a_gs)])
                last_gt = v_gt
            hld[buf] = last_gt
        g_pe = pb.cnt["pe"]
        g_done_st = pb.cnt["st"]
        g_done_ve = pb.cnt["ve"]
        g_done_ac = pb.cnt["ac"]

        v_am = pb.emit("vector", lambda e: e.tensor_scalar(gA8, gA8, cm8[0:8, 0:1], None,
                                                           ALU.mult),
                       wait=[("ld", lds), ("ve", g_done_ve)], inc=("ve", 1))
        t_sf = pb.emit("tensor", lambda e: e.matmul(pR[0:8, 384:512], us8[0:8, 0:8], gA8,
                                                    start=True, stop=True,
                                                    skip_group_check=True),
                       wait=[("ve", v_am)], inc=("pe", 1))
        a_we = pb.emit("scalar", lambda e: e.activation(gA8, pR[0:8, 384:512], AF.Exp),
                       wait=[("pe", t_sf), ("ac", g_done_ac)], inc=("ac", 1))
        v_wm = pb.emit("vector", lambda e: e.tensor_scalar(gA8, gA8, cm8[0:8, 0:1], None,
                                                           ALU.mult),
                       wait=[("ac", a_we)], inc=("ve", 1))
        rc_ev = 0
        for m in range(W):
            l_fm = pb.dma("sync", fm_f, g_f[ts(m, 128), :], sem="ld2",
                          wait=[("ve", rc_ev if rc_ev else g_done_ve)])
            tc = pb.emit("tensor", lambda e, m=m: e.matmul(
                pR[0:1, 384:512], ident_f[0:8, m:m + 1], gA8, start=True, stop=True,
                skip_group_check=True),
                wait=[("ve", v_wm), ("ve", rc_ev)], inc=("pe", 1))
            ec = pb.emit("vector", lambda e: e.tensor_copy(earow[0:1, 0:128],
                                                           pR[0:1, 384:512]),
                         wait=[("pe", tc)], inc=("ve", 1))
            bc = pb.emit("tensor", lambda e: e.matmul(pR[:, 256:384], ones_f[0:1, :],
                                                      earow[0:1, 0:128], start=True,
                                                      stop=True, skip_group_check=True),
                         wait=[("ve", ec)], inc=("pe", 1))
            e2 = pb.emit("vector", lambda e: e.tensor_copy(eacol[:, :], pR[:, 256:384]),
                         wait=[("pe", bc)], inc=("ve", 1))
            if m == 0:
                rc_ev = pb.emit("vector", lambda e: e.tensor_tensor(
                    Rf[:, :], fm_f[:, :], eacol[:, :, None].broadcast_to([128, 128, PD]),
                    ALU.mult), wait=[("ve", e2), ("ld2", l_fm)], inc=("ve", 1))
            else:
                rc_ev = pb.emit("vector", lambda e: (
                    e.tensor_tensor(tmp_cmb[:, 0:8192], fm_f[:, :],
                                    eacol[:, :, None].broadcast_to([128, 128, PD]), ALU.mult),
                    e.tensor_tensor(Rf[:, :], Rf[:, :], tmp_cmb[:, 0:8192], ALU.add))[-1],
                    wait=[("ve", e2), ("ld2", l_fm), ("pe", g_pe)], inc=("ve", 1))

        y_pe_prev = 0
        evslot_ev = {0: 0, 1: 0}
        x1_st = {0: 0, 1: 0}
        t1_ev = 0
        yt_ev = 0
        for c in range(NCH):
            cs = slice(c * 256, c * 256 + 256)
            l_y = pb.dma("sync", y_sl.rearrange("p (a d) -> p a d", a=2),
                         i_y1[cs, :].rearrange("(a p) d -> p a d", p=128), sem="ld",
                         wait=[("pe", y_pe_prev), ("ve", rc_ev)])
            l_c = pb.dma("sync", cT.rearrange("p (k t) -> p k t", k=8),
                         i_xbc.rearrange("(k p) t -> p k t", p=128)[:, 72:80, cs], sem="ld",
                         wait=[("pe", y_pe_prev)])
            l_s = pb.dma("sync", s_sl, i_s[ts(c, 128), :], sem="ld",
                         wait=[("ve", rc_ev), ("pe", g_pe)])
            l_a = pb.dma("sync", acst_sl.rearrange("p (a h) -> p a h", a=2),
                         i_acst[cs, :].rearrange("(a p) h -> p a h", p=128), sem="ld",
                         wait=[("ac", g_done_ac if c == 0 else 0)])
            l_z2 = pb.dma("sync", z_sl.rearrange("p (a d) -> p a d", a=2),
                          i_z[cs, :].rearrange("(a p) d -> p a d", p=128), sem="ld",
                          wait=[("ve", rc_ev)])
            ld_all = l_z2
            a_od = pb.emit("scalar", lambda e: e.activation(odT[:, 0:256], acst_sl[:, 0:256],
                                                            AF.Exp),
                           wait=[("ld", ld_all), ("ve", t1_ev)], inc=("ac", 1))
            v_rb = pb.emit("vector", lambda e: e.tensor_copy(r_bf[:, :], Rf[:, :]),
                           wait=[("ve", rc_ev), ("pe", y_pe_prev)], inc=("ve", 1))
            for hg in range(16):
                gi = hg // 2
                for hm in range(8):
                    h = hg * 8 + hm
                    for lt in range(2):
                        psx = pA if lt == 0 else pB
                        pb.emit("tensor", lambda e, gi=gi, lt=lt, h=h, hm=hm, psx=psx: e.matmul(
                            psx[:, hm * 64:hm * 64 + 64],
                            cT[:, gi * 256 + lt * 128:gi * 256 + lt * 128 + 128],
                            r_bf[:, h * 64:h * 64 + 64],
                            start=(hm == 0), stop=False, skip_group_check=(hm != 0)),
                            wait=([("ld", ld_all), ("ve", v_rb), ("ve", t1_ev)]
                                  if hm == 0 else ()),
                            inc=("pe", 1) if (hm == 7 and lt == 1) else None)
                mmd = pb.cnt["pe"]

                def t1add(e, hg=hg):
                    e.tensor_tensor(t1tmp[:, 0:512], pA[:],
                                    odT[:, hg * 8:hg * 8 + 8][:, :, None]
                                    .broadcast_to([128, 8, PD]), ALU.mult)
                    e.tensor_tensor(y_sl[:, hg * 512:hg * 512 + 512], t1tmp[:, 0:512],
                                    y_sl[:, hg * 512:hg * 512 + 512], ALU.add)
                    e.tensor_tensor(t1tmp[:, 0:512], pB[:],
                                    odT[:, 128 + hg * 8:128 + hg * 8 + 8][:, :, None]
                                    .broadcast_to([128, 8, PD]), ALU.mult)
                    return e.tensor_tensor(y_sl[:, 8192 + hg * 512:8192 + hg * 512 + 512],
                                           t1tmp[:, 0:512],
                                           y_sl[:, 8192 + hg * 512:8192 + hg * 512 + 512],
                                           ALU.add)
                t1_ev = pb.emit("vector", t1add,
                                wait=[("pe", mmd), ("ld", ld_all), ("ac", a_od)],
                                inc=("ve", 1))
            g_ev2 = t1_ev
            for st_i in range(2):
                ysl = y_sl[:, st_i * 8192:st_i * 8192 + 8192]
                zsl = z_sl[:, st_i * 8192:st_i * 8192 + 8192]
                a_sz = pb.emit("scalar", lambda e, zsl=zsl: e.activation(zsl, zsl, AF.Silu),
                               wait=[("ld", ld_all), ("ve", g_ev2)], inc=("ac", 1))
                v_g = pb.emit("vector", lambda e, ysl=ysl, zsl=zsl: e.tensor_tensor(
                    ysl, ysl, zsl, ALU.mult), wait=[("ac", a_sz)], inc=("ve", 1))
                a_sq = pb.emit("scalar", lambda e, ysl=ysl, st_i=st_i: e.activation(
                    sq_bf[:, st_i * 8192:st_i * 8192 + 8192], ysl, AF.Square),
                    wait=[("ve", v_g), ("pe", y_pe_prev), ("ve", yt_ev)], inc=("ac", 1))
                v_v = pb.emit("vector", lambda e, st_i=st_i: e.tensor_reduce(
                    SC2(5 + st_i), sq_bf[:, st_i * 8192:st_i * 8192 + 8192], AX, ALU.add),
                    wait=[("ac", a_sq)], inc=("ve", 1))
                a_rs = pb.emit("scalar", lambda e, st_i=st_i: (
                    e.activation(SC2(7 + st_i), SC2(5 + st_i), AF.Ln, bias=SC2(15),
                                 scale=1.0 / DI),
                    e.activation(SC2(9 + st_i), SC2(7 + st_i), AF.Exp, scale=-0.5))[-1],
                    wait=[("ve", v_v)], inc=("ac", 1))
                g_ev2 = pb.emit("vector", lambda e, ysl=ysl, st_i=st_i: e.tensor_scalar(
                    ysl, ysl, SC2(9 + st_i), None, ALU.mult),
                    wait=[("ac", a_rs)], inc=("ve", 1))
            yt_ev = 0
            for kb in range(64):
                t_y = pb.emit("tensor", lambda e, kb=kb: (
                    e.transpose(pT[:, 0:128], y_sl[:, kb * 128:kb * 128 + 128], ident_bf[:]),
                    e.transpose(pT[:, 128:256],
                                y_sl[:, 8192 + kb * 128:8192 + kb * 128 + 128],
                                ident_bf[:]))[-1],
                    wait=[("ve", g_ev2), ("ve", yt_ev)], inc=("pe", 1))
                yt_ev = pb.emit("vector", lambda e, kb=kb: e.tensor_scalar(
                    yT[:, kb * 256:kb * 256 + 256], pT[:, 0:256], nw_l[:, kb:kb + 1],
                    None, ALU.mult),
                    wait=[("pe", t_y)], inc=("ve", 1))
            wo_pe = {0: 0, 1: 0}
            for oc in range(8):
                for k in range(64):
                    kp = k % 2
                    tsem = "trd" if kp == 0 else "tr2"
                    wbuf = wob[:, ts(kp, 512)]
                    wo_tr = pb.dmat("sync", wbuf, wo_bf[ts(oc, 512), ts(k, 128)], sem=tsem,
                                    wait=([("pe", wo_pe[kp])] if wo_pe[kp] else
                                          [("cast", cast_wo[oc]), ("ve", yt_ev)]))
                    for tt2 in range(2):
                        psx = pC if tt2 == 0 else pD
                        pb.emit("tensor", lambda e, k=k, tt2=tt2, psx=psx, wbuf=wbuf: e.matmul(
                            psx[:], yT[:, k * 256 + tt2 * 128:k * 256 + tt2 * 128 + 128],
                            wbuf, start=(k == 0), stop=(k == 63)),
                            wait=([(tsem, wo_tr)] + ([("ve", evslot_ev[0]),
                                                      ("ve", evslot_ev[1])]
                                                     if k == 0 else [])) if tt2 == 0 else (),
                            inc=("pe", 1) if (tt2 == 1) else None)
                    wo_pe[kp] = pb.cnt["pe"]
                for tt2 in range(2):
                    psx = pC if tt2 == 0 else pD
                    buf = tt2
                    row = slice(c * 256 + tt2 * 128, c * 256 + tt2 * 128 + 128)
                    lsem = "ld2" if buf == 0 else "ld3"
                    l_xn = pb.dma("sync", slot(buf, "xn"), i_xn[row, ts(oc, 512)], sem=lsem,
                                  wait=[("ve", evslot_ev[buf])] if evslot_ev[buf] else ())
                    l_g = pb.dma("sync", slot(buf, "g"), g_dram[row, ts(oc, 512)], sem=lsem,
                                 wait=[("st", g_done_st)] if (c == 0 and oc == 0) else ())
                    l_x = pb.dma("sync", slot(buf, "xin"), x_main[row, ts(oc, 512)],
                                 sem=lsem)
                    mm_d = pb.cnt["pe"]

                    def sevac(e, psx=psx, buf=buf):
                        e.tensor_tensor(psx[:], psx[:], slot(buf, "xn"), ALU.add)
                        e.tensor_tensor(psx[:], psx[:], slot(buf, "g"), ALU.mult)
                        return e.tensor_tensor(slot(buf, "x1"), psx[:], slot(buf, "xin"),
                                               ALU.add)
                    ev = pb.emit("vector", sevac,
                                 wait=[("pe", mm_d), (lsem, l_x),
                                       ("st", x1_st[buf])], inc=("ve", 1))
                    evslot_ev[buf] = ev
                    x1_st[buf] = pb.dma("sync", b_x1[row, ts(oc, 512)], slot(buf, "x1"),
                                        sem="st", wait=[("ve", ev)])
            if c < NCH - 1:
                a_ea = pb.emit("scalar", lambda e, c=c: e.activation(
                    SC2(11), acs_sb[:, c * 256 + 255:c * 256 + 256], AF.Exp),
                    wait=[("ld", lds)], inc=("ac", 1))
                t_e = pb.emit("tensor", lambda e: e.transpose(pR[0:1, 384:512], SC2(11),
                                                              ident_f[:]),
                              wait=[("ac", a_ea)], inc=("pe", 1))
                e_e = pb.emit("vector", lambda e: e.tensor_copy(earow[0:1, 0:128],
                                                                pR[0:1, 384:512]),
                              wait=[("pe", t_e)], inc=("ve", 1))
                b_e = pb.emit("tensor", lambda e: e.matmul(pR[:, 256:384], ones_f[0:1, :],
                                                           earow[0:1, 0:128], start=True,
                                                           stop=True, skip_group_check=True),
                              wait=[("ve", e_e)], inc=("pe", 1))
                e2_e = pb.emit("vector", lambda e: e.tensor_copy(eacol[:, :], pR[:, 256:384]),
                               wait=[("pe", b_e)], inc=("ve", 1))
                rc_ev = pb.emit("vector", lambda e: (
                    e.tensor_tensor(Rf[:, :], Rf[:, :],
                                    eacol[:, :, None].broadcast_to([128, 128, PD]), ALU.mult),
                    e.tensor_tensor(Rf[:, :], Rf[:, :], s_sl[:, :], ALU.add))[-1],
                    wait=[("ve", e2_e), ("ld", ld_all), ("ve", v_rb)], inc=("ve", 1))
            y_pe_prev = pb.cnt["pe"]

        pb.finish()
        pb.run(block)
    return nc

def build_launch3():
    nc = bass.Bass("TRN2", target_bir_lowering=False)
    dI = lambda n, s, d=F32: nc.dram_tensor(n, s, d, kind="ExternalInput").ap()
    dO = lambda n, s, d=F32: nc.dram_tensor(n, s, d, kind="ExternalOutput").ap()
    dT = lambda n, s, d=BF16: nc.dram_tensor(n, s, d, kind="Internal").ap()

    i_x1 = dI("i_x1", [T, D])
    x1_halo2 = dI("x1_halo2", [2, D])
    cw3 = dI("cw3", [3, D, 3072])
    cb3 = dI("cb3", [3072])
    wq3 = dI("wq3", [1024, 1024])
    bq3c = dI("bq3c", [128, 8])
    wk3 = dI("wk3", [1024, 1024])
    bk3c = dI("bk3c", [128, 8])
    wv3 = dI("wv3", [1024, 1024])
    bv3 = dI("bv3", [1024])
    wo3 = dI("wo3", [1024, 1024])
    bo3c = dI("bo3c", [128, 8])
    pw3 = dI("pw3", [1024, D])
    pb3 = dI("pb3", [D])
    aw3 = dI("aw3", [512, D])
    bw3 = dI("bw3", [D, 512])
    wqr3 = dI("wqr3", [16, D])
    keys3 = dI("keys3", [64, 16])
    cst = {n: dI(n, s, d) for n, s, d in CONST_SPECS}

    o_x2 = dO("o_x2", [T, D])
    d3C = dO("d3C", [128, 192])

    cw_bf = [dT(f"cw_bf{w}", [D, 3072]) for w in range(3)]
    wq_bfd = dT("wq_bfd3", [1024, 1024])
    wk_bfd = dT("wk_bfd3", [1024, 1024])
    wv_bfd = dT("wv_bfd3", [1024, 1024])
    wo_bfd = dT("wo_bfd3", [1024, 1024])
    pw_bfd = dT("pw_bfd3", [1024, D])
    aw_bfd = dT("aw_bfd3", [512, D])
    bw_bfd = dT("bw_bfd3", [D, 512])
    wqr_bfd = dT("wqr_bfd3", [16, D])
    keys_bfd = dT("keys_bfd3", [64, 128])
    cb_bfd = dT("cb_bfd3", [1, 3072])
    bv_bfd = dT("bv_bfd3", [1, 1024])
    pb_bfd = dT("pb_bfd3", [1, D])
    hd_dram = dT("hd_dram3", [16, D])
    g1_dram = dT("g1_dram3", [T, D])

    with contextlib.ExitStack() as xs:
        E = xs.enter_context
        a3 = E(nc.sbuf_tensor("L3a", [128, 32832], BF16))
        b3 = E(nc.sbuf_tensor("L3b", [128, 32768], BF16))
        c3 = E(nc.sbuf_tensor("L3c", [128, 12288], BF16))
        d3 = E(nc.sbuf_tensor("L3d", [128, 22784], BF16))
        f3 = E(nc.sbuf_tensor("L3f", [128, 1536], F32))
        pA = E(nc.psum_tensor("r3A", [128, 512], F32))
        pB = E(nc.psum_tensor("r3B", [128, 512], F32))
        pC = E(nc.psum_tensor("r3C", [128, 512], F32))
        pD = E(nc.psum_tensor("r3D", [128, 512], F32))
        pE2 = E(nc.psum_tensor("r3E", [128, 512], F32))
        pF2 = E(nc.psum_tensor("r3F", [128, 512], F32))
        pG = E(nc.psum_tensor("r3G", [128, 512], F32))
        pTb = E(nc.psum_tensor("r3T", [128, 256], BF16))
        block = E(nc.Block())
        pb = PB(nc)
        for s in ["cast", "trd", "tr2", "ld", "ld2", "ld3", "pe", "ve", "ac", "gp",
                  "st", "st2", "stg"]:
            pb.new_sem(E(nc.semaphore("w_" + s)), s)

        av, bv_, cv_, dv, fv = a3[:], b3[:], c3[:], d3[:], f3[:]
        xn1T = av[:, 0:32832]
        wbuf = [bv_[:, 0:4096], bv_[:, 4096:8192]]
        wq_sb = bv_[:, 0:8192]
        wk_sb = bv_[:, 8192:16384]
        wv_sb = bv_[:, 16384:24576]
        wo_sb = bv_[:, 24576:32768]
        awT_sb = bv_[:, 0:16384]
        bwT_sb = bv_[:, 16384:32768]
        pw_sb = bv_[:, 0:32768]
        xld3 = cv_[:, 0:4096].bitcast(F32)
        sqs3 = cv_[:, 4096:8192].bitcast(F32)
        xn_tok = cv_[:, 8192:12288]
        cvt = [cv_[:, i * 3072:(i + 1) * 3072] for i in range(4)]
        zT_f = cv_[:, 0:1024].bitcast(F32)
        x1c_f = [cv_[:, 1024 + j * 1024:1024 + j * 1024 + 1024].bitcast(F32)
                 for j in range(2)]
        x2_f = [cv_[:, 3072 + j * 1024:3072 + j * 1024 + 1024].bitcast(F32)
                for j in range(2)]
        g1c = [cv_[:, 5120 + j * 512:5120 + j * 512 + 512] for j in range(2)]
        ident_bf = dv[:, 0:128]
        ones_bf = dv[0:1, 128:256]
        o2T_all = dv[:, 256:8448]
        qhT_sb = dv[:, 8448:9472]
        khT_sb = dv[:, 9472:10496]
        vh_sb = dv[:, 10496:11520]
        p_sb = dv[:, 11520:11648]
        pT_sb = dv[:, 11648:11776]
        brows = dv
        cbrow = dv[0:1, 11776:14848]
        bvrow = dv[0:1, 14848:15872]
        pbrow = dv[0:1, 18688:22784]
        keysT_sb = dv[:, 15872:15936]
        wqT_sb = dv[:, 15936:16448]
        gateT_bf = dv[:, 16448:16576]
        zgT_sb = dv[:, 16576:17088]
        oT_tmp = dv[:, 17088:18112]
        qT_bf = dv[:, 18112:18240]
        hview = dv[:, 18240:18256]
        msk3 = dv[:, 18256:18288].bitcast(U8)
        eexp_sb = dv[:, 18288:18800 + 0] if False else dv[:, 18176:18176]
        eexp_sb = dv[:, 18176:18112] if False else None
        eexp_sb = dv[:, 18112:18624]
        qT_bf = dv[:, 18624:18688 - 64 + 64] if False else None
        qT_bf = dv[:, 18640:18688]
        oT_tmp = dv[:, 17088:18112]
        eexp_sb = dv[:, 18112:18624]
        qT_bf = dv[:, 18624:18688 - 48 + 88] if False else None
        qT_bf = dv[:, 15936 - 192:15936 - 64] if False else None
        qT_bf = dv[:, 18624:18688]
        qT_bf = dv[:, 11520:11648]
        mask_f = fv[:, 0:128]
        att_sc = fv[:, 128:256]
        ex_f = fv[:, 256:384]
        stat3 = fv[:, 384:448]
        ffn_sc = fv[:, 448:512]
        gate_f = fv[:, 512:576]
        ninf_f = fv[:, 576:640]
        zro64 = fv[:, 640:704]
        bq_sc = fv[:, 704:712]
        bk_cc = fv[:, 712:720]
        bo_cc = fv[:, 720:728]
        wk_f = fv[:, 728:792]
        ident_f = fv[:, 768 + 32:1024 - 64] if False else None
        ident_f = fv[:, 832:960]
        ones_f = fv[:, 960:1088]
        bq_raw = fv[:, 1088:1096]
        wk2_f = fv[:, 1096:1160]
        spare = fv[:, 1160:1536]

        def SC3(i, rows=128):
            return stat3[0:rows, 4 * i:4 * i + 1]

        ld0 = [
            pb.dma("sync", ident_bf, cst["c_ident_bf"]),
            pb.dma("sync", ones_bf, cst["c_ones_bf"]),
            pb.dma("sync", mask_f, cst["c_mask128"]),
            pb.dma("sync", ninf_f, cst["c_ninf"][0:1, :].broadcast_to([128, 64])),
            pb.dma("sync", zro64, cst["c_zeros"][:, 0:64].broadcast_to([128, 64])),
            pb.dma("sync", eexp_sb[0:64, :], cst["c_eexp"]),
            pb.dma("sync", ident_f, cst["c_ident_f32"]),
            pb.dma("sync", ones_f[0:1, :], cst["c_ones_f32"]),
            pb.dma("sync", SC3(15, 1), cst["c_eps"][0:1, 0:1]),
            pb.dma("sync", bq_raw, bq3c[:, :]),
            pb.dma("sync", bk_cc, bk3c[:, :]),
            pb.dma("sync", bo_cc, bo3c[:, :]),
        ]
        lds = max(ld0)
        t_eps = pb.emit("tensor", lambda e: e.matmul(
            pG[0:128, 24:25], ones_f[0:1, 0:128], SC3(15, 1),
            start=True, stop=True, skip_group_check=True),
            wait=[("ld", lds)], inc=("pe", 1))
        v_eps = pb.emit("vector", lambda e: e.tensor_copy(SC3(14), pG[0:128, 24:25]),
                        wait=[("pe", t_eps), ("ld", lds)], inc=("ve", 1))
        v_bq = pb.emit("vector", lambda e: e.tensor_scalar(bq_sc, bq_raw, 0.125, None,
                                                           ALU.mult),
                       wait=[("ve", v_eps)], inc=("ve", 1))

        ln_prev = {0: 0}

        def emit_ln3(rows, src, dst_fn):
            a1 = v1 = None
            for hh in range(2):
                ldx = pb.dma("sync", xld3[0:rows, :], src[:, ts(hh, 2048)],
                             wait=[("ac", a1)] if a1 else
                             ([("ve", ln_prev[0])] if ln_prev[0] else [("ve", v_bq)]))
                v1 = pb.emit("vector", lambda e, rows=rows, hh=hh: e.tensor_reduce(
                    SC3(8 + hh, rows), xld3[0:rows, :], AX, ALU.add),
                    wait=[("ld", ldx)], inc=("ve", 1))
                a1 = pb.emit("scalar", lambda e, rows=rows: e.activation(
                    sqs3[0:rows, :], xld3[0:rows, :], AF.Square),
                    wait=[("ld", ldx), ("ve", v1)], inc=("ac", 1))
                v1 = pb.emit("vector", lambda e, rows=rows, hh=hh: e.tensor_reduce(
                    SC3(10 + hh, rows), sqs3[0:rows, :], AX, ALU.add),
                    wait=[("ac", a1)], inc=("ve", 1))
            s_sum = pb.emit("scalar", lambda e, rows=rows: e.add(
                SC3(13, rows), SC3(8, rows), SC3(9, rows)),
                wait=[("ve", v1)], inc=("ac", 1))
            p_s2 = pb.emit("gpsimd", lambda e, rows=rows: e.tensor_tensor(
                SC3(12, rows), SC3(10, rows), SC3(11, rows), ALU.add),
                wait=[("ve", v1)], inc=("gp", 1))
            v_mean = pb.emit("vector", lambda e, rows=rows: e.tensor_scalar(
                SC3(2, rows), SC3(13, rows), 1.0 / D, None, ALU.mult),
                wait=[("ac", s_sum)], inc=("ve", 1))
            a_m2 = pb.emit("scalar", lambda e, rows=rows: e.activation(
                SC3(3, rows), SC3(2, rows), AF.Square),
                wait=[("ve", v_mean)], inc=("ac", 1))
            v2 = pb.emit("vector", lambda e, rows=rows: e.tensor_scalar(
                SC3(5, rows), SC3(12, rows), 1.0 / D, SC3(3, rows),
                ALU.mult, ALU.subtract),
                wait=[("gp", p_s2), ("ac", a_m2)], inc=("ve", 1))
            a2 = pb.emit("scalar", lambda e, rows=rows: (
                e.activation(SC3(6, rows), SC3(5, rows), AF.Ln, bias=SC3(14, rows)),
                e.activation(SC3(7, rows), SC3(6, rows), AF.Exp, scale=-0.5))[-1],
                wait=[("ve", v2)], inc=("ac", 1))
            v3 = pb.emit("vector", lambda e, rows=rows: e.tensor_scalar(
                xn_tok[0:rows, 2048:4096], xld3[0:rows, :], SC3(2, rows), SC3(7, rows),
                ALU.subtract, ALU.mult),
                wait=[("ac", a2)], inc=("ve", 1))
            ldx0 = pb.dma("sync", xld3[0:rows, :], src[:, 0:2048], wait=[("ve", v3)])
            v4 = pb.emit("vector", lambda e, rows=rows: e.tensor_scalar(
                xn_tok[0:rows, 0:2048], xld3[0:rows, :], SC3(2, rows), SC3(7, rows),
                ALU.subtract, ALU.mult),
                wait=[("ld", ldx0)], inc=("ve", 1))
            ln_prev[0] = dst_fn(v4)

        def main_dst(t_i):
            def fn(v4):
                last = v4
                for c2 in range(16):
                    pv = pb.emit("tensor", lambda e, c2=c2: (
                        e.transpose(pTb[:, 0:128], xn_tok[:, ts(2 * c2, 128)], ident_bf),
                        e.transpose(pTb[:, 128:256], xn_tok[:, ts(2 * c2 + 1, 128)],
                                    ident_bf))[-1],
                        wait=[("ve", last)], inc=("pe", 1))
                    last = pb.emit("vector", lambda e, c2=c2, t_i=t_i: (
                        e.tensor_copy(xn1T[:, (2 * c2) * 1026 + 1 + t_i * 128:
                                           (2 * c2) * 1026 + 1 + t_i * 128 + 128],
                                      pTb[:, 0:128]),
                        e.tensor_copy(xn1T[:, (2 * c2 + 1) * 1026 + 1 + t_i * 128:
                                           (2 * c2 + 1) * 1026 + 1 + t_i * 128 + 128],
                                      pTb[:, 128:256]))[-1],
                        wait=[("pe", pv)], inc=("ve", 1))
                    pb.wait("tensor", [("ve", last)])
                return last
            return fn

        def halo_dst(v4):
            sh = pb.dma("sync", hd_dram[0:2, :], xn_tok[0:2, :], sem="st2",
                        wait=[("ve", v4)])
            last = 0
            for kt in range(32):
                tdh = pb.dmat("sync", hview, hd_dram[:, ts(kt, 128)],
                              sem="trd" if kt % 2 == 0 else "tr2",
                              wait=([("st2", sh)] if kt == 0 else []) +
                                   ([("ve", last)] if last else []))
                last = pb.emit("vector", lambda e, kt=kt: (
                    e.tensor_copy(xn1T[:, kt * 1026:kt * 1026 + 1], hview[:, 0:1]),
                    e.tensor_copy(xn1T[:, kt * 1026 + 1025:kt * 1026 + 1026],
                                  hview[:, 1:2]))[-1],
                    wait=[("trd" if kt % 2 == 0 else "tr2", tdh)], inc=("ve", 1))
            return last

        for t_i in range(8):
            emit_ln3(128, i_x1[ts(t_i, 128), :], main_dst(t_i))
        emit_ln3(2, x1_halo2[:, :], halo_dst)
        xn1T_rdy = pb.cnt["ve"]

        cast_cw = [pb.dma("gpsimd", cw_bf[w][:, :], cw3[w, :, :], sem="cast")
                   for w in range(3)]
        c_small = [
            pb.dma("gpsimd", wq_bfd[:, :], wq3[:, :], sem="cast"),
            pb.dma("gpsimd", wk_bfd[:, :], wk3[:, :], sem="cast"),
            pb.dma("gpsimd", wv_bfd[:, :], wv3[:, :], sem="cast"),
            pb.dma("gpsimd", wo_bfd[:, :], wo3[:, :], sem="cast"),
            pb.dma("gpsimd", pw_bfd[:, :], pw3[:, :], sem="cast"),
            pb.dma("gpsimd", aw_bfd[:, :], aw3[:, :], sem="cast"),
            pb.dma("gpsimd", bw_bfd[:, :], bw3[:, :], sem="cast"),
            pb.dma("gpsimd", wqr_bfd[:, :], wqr3[:, :], sem="cast"),
            pb.dma("gpsimd", keys_bfd[0:64, 0:16], keys3[:, :], sem="cast"),
            pb.dma("gpsimd", cb_bfd[:, :], cb3[None, :], sem="cast"),
            pb.dma("gpsimd", bv_bfd[:, :], bv3[None, :], sem="cast"),
            pb.dma("gpsimd", pb_bfd[:, :], pb3[None, :], sem="cast"),
        ]
        cast_all = max(cast_cw + c_small)
        ld1 = [
            pb.dma("sync", cbrow, cb_bfd[:, :], wait=[("cast", cast_all)]),
            pb.dma("sync", bvrow, bv_bfd[:, :]),
            pb.dma("sync", pbrow, pb_bfd[:, :]),
        ]

        tkq = pb.dmat("sync", keysT_sb, keys_bfd[:, :], sem="trd",
                      wait=[("cast", cast_all), ("ve", xn1T_rdy)])
        last_t = tkq
        for kt in range(32):
            last_t = pb.dmat("sync", wqT_sb[:, ts(kt, 16)], wqr_bfd[:, ts(kt, 128)],
                             sem="tr2" if kt % 2 == 0 else "trd",
                             wait=[("trd", tkq)] if kt == 0 else ())
        wqT_done_trd = pb.cnt["trd"]
        wqT_done_tr2 = pb.cnt["tr2"]

        banks = [pA, pB, pC, pD]
        bank_ev = [0, 0, 0, 0]
        wb_pe = [0, 0]
        qkv_loaded = {0: 0}
        att_done = {0: 0}
        o2_done = [0] * 8

        def conv_group(G):
            cvdone = [0, 0, 0, 0]
            for c in range(6):
                first_mm = {}
                for w in range(3):
                    for kg in range(4):
                        idx = c * 12 + w * 4 + kg
                        buf = idx % 2
                        sem = "ld2" if buf == 0 else "ld3"
                        wld = pb.dma("sync", wbuf[buf].rearrange("p (a n) -> p a n", a=8),
                                     cw_bf[w].rearrange("(a p) n -> p a n", p=128)
                                     [:, ts2(kg * 8, 8), ts(c, 512)],
                                     sem=sem,
                                     wait=([("pe", wb_pe[buf])] if wb_pe[buf] else
                                           [("cast", cast_all)]))
                        mmlast = 0
                        for i4 in range(4):
                            i = G * 4 + i4
                            def mms(e, i4=i4, i=i, kg=kg, w=w, c=c, buf=buf):
                                r = None
                                for k8 in range(8):
                                    kt = kg * 8 + k8
                                    st = (w == 0 and kg == 0 and k8 == 0)
                                    r = e.matmul(
                                        banks[i4][:, 0:512],
                                        xn1T[:, kt * 1026 + i * 128 + w:
                                             kt * 1026 + i * 128 + w + 128],
                                        wbuf[buf][:, ts(k8, 512)],
                                        start=st, stop=False,
                                        skip_group_check=True)
                                return r
                            wl = [(sem, wld)]
                            if w == 0 and kg == 0:
                                wl.append(("ve", bank_ev[i4]))
                                if G == 0 and c == 0:
                                    wl.append(("ve", xn1T_rdy))
                                if G == 1 and c == 0:
                                    wl.append(("ve", att_done[0]))
                            mmlast = pb.emit("tensor", mms, wait=wl, inc=("pe", 1))
                        wb_pe[buf] = mmlast
                for i4 in range(4):
                    bmm = pb.emit("tensor", lambda e, i4=i4, c=c: e.matmul(
                        banks[i4][:, 0:512], ones_bf[0:1, 0:128],
                        cbrow[0:1, ts(c, 512)], start=False, stop=True,
                        skip_group_check=True),
                        wait=[("ld", max(ld1))], inc=("pe", 1))
                    bank_ev[i4] = pb.emit("vector", lambda e, i4=i4, c=c: e.tensor_copy(
                        cvt[i4][:, ts(c, 512)], banks[i4][:, 0:512]),
                        wait=[("pe", bmm)], inc=("ve", 1))
                    cvdone[i4] = bank_ev[i4]
            return cvdone

        def ts2(a, n):
            return slice(a, a + n)

        def attn_tile(G, i4, cvdone):
            i = G * 4 + i4
            cv_i = cvt[i4]
            last = cvdone
            for j in range(24):
                pv = pb.emit("tensor", lambda e, j=j: e.transpose(
                    pTb[:, 0:128], cv_i[:, ts(j, 128)], ident_bf),
                    wait=[("ve", last)], inc=("pe", 1))
                last = pb.emit("vector", lambda e, j=j: e.tensor_copy(
                    cv_i[:, ts(j, 128)], pTb[:, 0:128]),
                    wait=[("pe", pv)], inc=("ve", 1))
                pb.wait("tensor", [("ve", last)])
            cvT_done = last
            qk_ev = {0: 0}
            for pr in range(8):
                for (wsb, dst, bias_ap, scl, ps) in (
                        (wq_sb, qhT_sb, bq_sc, 0.125, pE2),
                        (wk_sb, khT_sb, bk_cc, 1.0, pF2)):
                    def qmm(e, pr=pr, wsb=wsb, ps=ps):
                        r = None
                        for kt in range(8):
                            r = e.matmul(ps[:, 0:128],
                                         wsb[:, kt * 1024 + pr * 128:
                                             kt * 1024 + pr * 128 + 128],
                                         cv_i[:, ts(kt, 128)],
                                         start=(kt == 0), stop=(kt == 7),
                                         skip_group_check=True)
                        return r
                    tq = pb.emit("tensor", qmm,
                                 wait=[("ve", cvT_done), ("ld", qkv_loaded[0]),
                                       ("ac", qk_ev[0])], inc=("pe", 1))
                    qk_ev[0] = pb.emit("scalar",
                                       lambda e, pr=pr, dst=dst, bias_ap=bias_ap,
                                       scl=scl, ps=ps: e.activation(
                                           dst[:, ts(pr, 128)], ps[:, 0:128],
                                           AF.Identity,
                                           bias=bias_ap[:, pr:pr + 1], scale=scl),
                                       wait=[("pe", tq)], inc=("ac", 1))
            for nh in range(2):
                def vmm(e, nh=nh):
                    for kt in range(8):
                        e.matmul(pG[:, 0:512],
                                 cv_i[:, ts(kt, 128)],
                                 wv_sb[:, kt * 1024 + nh * 512:
                                       kt * 1024 + nh * 512 + 512],
                                 start=(kt == 0), stop=False,
                                 skip_group_check=True)
                    return e.matmul(pG[:, 0:512], ones_bf[0:1, 0:128],
                                    bvrow[0:1, ts(nh, 512)], start=False, stop=True,
                                    skip_group_check=True)
                tv = pb.emit("tensor", vmm,
                             wait=[("ve", cvT_done), ("ld", qkv_loaded[0]),
                                   ("ve", qk_ev.get(1, 0))], inc=("pe", 1))
                qk_ev[1] = pb.emit("vector", lambda e, nh=nh: e.tensor_copy(
                    vh_sb[:, ts(nh, 512)], pG[:, 0:512]),
                    wait=[("pe", tv)], inc=("ve", 1))
            qkv_ready_ac = qk_ev[0]
            qkv_ready_ve = qk_ev[1]
            h_ev = {0: 0, 1: 0, 2: 0}
            for h in range(16):
                pr, half = h // 2, h % 2
                t_sc = pb.emit("tensor", lambda e, pr=pr, half=half: e.matmul(
                    pE2[:, 128:256],
                    qhT_sb[half * 64:half * 64 + 64, ts(pr, 128)],
                    khT_sb[half * 64:half * 64 + 64, ts(pr, 128)],
                    start=True, stop=True, skip_group_check=True),
                    wait=[("ac", qkv_ready_ac), ("ve", h_ev[0])], inc=("pe", 1))
                v_msk = pb.emit("vector", lambda e: e.tensor_tensor(
                    att_sc, pE2[:, 128:256], mask_f, ALU.add),
                    wait=[("pe", t_sc)], inc=("ve", 1))
                v_mx = pb.emit("vector", lambda e: e.tensor_reduce(
                    SC3(0), att_sc, AX, ALU.max),
                    wait=[("ve", v_msk)], inc=("ve", 1))
                a_ng = pb.emit("scalar", lambda e: e.mul(SC3(1), SC3(0), -1.0),
                               wait=[("ve", v_mx)], inc=("ac", 1))
                a_ex = pb.emit("scalar", lambda e: e.activation(
                    ex_f, att_sc, AF.Exp, bias=SC3(1)),
                    wait=[("ac", a_ng)], inc=("ac", 1))
                v_sm = pb.emit("vector", lambda e: e.tensor_reduce(
                    SC3(2), ex_f, AX, ALU.add),
                    wait=[("ac", a_ex)], inc=("ve", 1))
                a_rc = pb.emit("scalar", lambda e: (
                    e.activation(SC3(3), SC3(2), AF.Ln),
                    e.activation(SC3(4), SC3(3), AF.Exp, scale=-1.0))[-1],
                    wait=[("ve", v_sm)], inc=("ac", 1))
                v_p = pb.emit("vector", lambda e: e.tensor_scalar(
                    p_sb, ex_f, SC3(4), None, ALU.mult),
                    wait=[("ac", a_rc)], inc=("ve", 1))
                t_pT = pb.emit("tensor", lambda e: e.transpose(
                    pTb[:, 128:256], p_sb, ident_bf),
                    wait=[("ve", v_p), ("ve", h_ev[1])], inc=("pe", 1))
                h_ev[1] = pb.emit("vector", lambda e: e.tensor_copy(pT_sb, pTb[:, 128:256]),
                                  wait=[("pe", t_pT)], inc=("ve", 1))
                t_oT = pb.emit("tensor", lambda e, h=h: e.matmul(
                    pF2[:, 128:256][0:64, :],
                    vh_sb[:, h * 64:h * 64 + 64], pT_sb,
                    start=True, stop=True, skip_group_check=True),
                    wait=[("ve", h_ev[1]), ("ve", qkv_ready_ve), ("ve", h_ev[2])],
                    inc=("pe", 1))
                h_ev[2] = pb.emit("vector", lambda e, pr=pr, half=half: e.tensor_copy(
                    oT_tmp[half * 64:half * 64 + 64, ts(pr, 128)],
                    pF2[0:64, 128:256]),
                    wait=[("pe", t_oT)], inc=("ve", 1))
                h_ev[0] = h_ev[2]
            o2_ev = 0
            for mt in range(8):
                def omm(e, mt=mt):
                    r = None
                    for kt in range(8):
                        r = e.matmul(pE2[:, 384:512],
                                     wo_sb[:, kt * 1024 + mt * 128:
                                           kt * 1024 + mt * 128 + 128],
                                     oT_tmp[:, ts(kt, 128)],
                                     start=(kt == 0), stop=(kt == 7),
                                     skip_group_check=True)
                    return r
                to2 = pb.emit("tensor", omm,
                              wait=[("ve", h_ev[2]), ("ac", o2_ev)], inc=("pe", 1))
                o2_ev = pb.emit("scalar", lambda e, mt=mt, i=i: e.activation(
                    o2T_all[:, i * 1024 + mt * 128:i * 1024 + mt * 128 + 128],
                    pE2[:, 384:512], AF.Identity, bias=bo_cc[:, mt:mt + 1]),
                    wait=[("pe", to2)], inc=("ac", 1))
            o2_done[i] = o2_ev
            return o2_ev

        for G in range(2):
            cvdone = conv_group(G)
            pe_after_conv = pb.cnt["pe"]
            if G == 0:
                lq = [pb.dma("sync", wq_sb.rearrange("p (a n) -> p a n", a=8),
                             wq_bfd.rearrange("(a p) n -> p a n", p=128),
                             wait=[("pe", pe_after_conv)]),
                      pb.dma("sync", wk_sb.rearrange("p (a n) -> p a n", a=8),
                             wk_bfd.rearrange("(a p) n -> p a n", p=128)),
                      pb.dma("sync", wv_sb.rearrange("p (a n) -> p a n", a=8),
                             wv_bfd.rearrange("(a p) n -> p a n", p=128)),
                      pb.dma("sync", wo_sb.rearrange("p (a n) -> p a n", a=8),
                             wo_bfd.rearrange("(a p) n -> p a n", p=128))]
                qkv_loaded[0] = max(lq)
            else:
                lq = [pb.dma("sync", wq_sb.rearrange("p (a n) -> p a n", a=8),
                             wq_bfd.rearrange("(a p) n -> p a n", p=128),
                             wait=[("pe", pe_after_conv)])]
                qkv_loaded[0] = max(lq)
            for i4 in range(4):
                attn_tile(G, i4, cvdone[i4])
            att_done[0] = pb.cnt["ve"]
            wb_pe[0] = wb_pe[1] = pb.cnt["pe"]

        pe_after_att = pb.cnt["pe"]
        lawt = []
        for kt in range(32):
            lawt.append(pb.dmat("sync", awT_sb[:, ts(kt, 512)],
                                aw_bfd[0:512, ts(kt, 128)],
                                sem="trd" if kt % 2 == 0 else "tr2",
                                wait=[("pe", pe_after_att)] if kt == 0 else ()))
        for j in range(4):
            for oc in range(8):
                lawt.append(pb.dmat("sync",
                                    bwT_sb[:, j * 4096 + oc * 512:
                                           j * 4096 + oc * 512 + 512],
                                    bw_bfd[ts(oc, 512), ts(j, 128)],
                                    sem="trd" if (j * 8 + oc) % 2 == 0 else "tr2"))
        awt_trd = pb.cnt["trd"]
        awt_tr2 = pb.cnt["tr2"]
        g1_prev = {0: 0, 1: 0}
        ffn_prev = {"pe": 0, "ve": 0, "ac": 0}
        for i in range(8):
            t_qT = pb.emit("tensor", lambda e, i=i: [
                e.matmul(pG[0:16, 0:128], wqT_sb[:, ts(kt, 16)],
                         xn1T[:, kt * 1026 + 1 + i * 128:kt * 1026 + 1 + i * 128 + 128],
                         start=(kt == 0), stop=(kt == 31), skip_group_check=True)
                for kt in range(32)][-1],
                wait=[("trd", wqT_done_trd), ("tr2", wqT_done_tr2),
                      ("ve", ffn_prev["ve"]), ("ac", ffn_prev["ac"])],
                inc=("pe", 1))
            v_qT = pb.emit("vector", lambda e: e.tensor_copy(qT_bf[0:16, :],
                                                             pG[0:16, 0:128]),
                           wait=[("pe", t_qT)], inc=("ve", 1))
            t_sc2 = pb.emit("tensor", lambda e: e.matmul(
                pG[:, 128:192], qT_bf[0:16, :], keysT_sb[0:16, 0:64],
                start=True, stop=True, skip_group_check=True),
                wait=[("ve", v_qT)], inc=("pe", 1))
            v_sc2 = pb.emit("vector", lambda e: (
                e.tensor_copy(ffn_sc, pG[:, 128:192]),
                e.tensor_copy(wk_f, pG[:, 128:192]))[-1],
                wait=[("pe", t_sc2)], inc=("ve", 1))
            vprev = v_sc2
            for it in range(15):
                srcb = wk_f if it % 2 == 0 else wk2_f
                dstb = wk2_f if it % 2 == 0 else wk_f
                v_m = pb.emit("vector", lambda e, srcb=srcb: e.tensor_reduce(
                    SC3(5), srcb, AX, ALU.max),
                    wait=[("ve", vprev)], inc=("ve", 1))
                if it == 0:
                    a_c = pb.emit("scalar", lambda e: (
                        e.copy(SC3(6), SC3(5)), e.copy(SC3(8), SC3(5)))[-1],
                        wait=[("ve", v_m)], inc=("ac", 1))
                else:
                    a_c = pb.emit("scalar", lambda e: e.copy(SC3(6), SC3(5)),
                                  wait=[("ve", v_m)], inc=("ac", 1))
                vprev = pb.emit("vector", lambda e, srcb=srcb, dstb=dstb: (
                    e.tensor_scalar(msk3, srcb, SC3(6), None, ALU.is_equal),
                    e.select(dstb, msk3, ninf_f, srcb))[-1],
                    wait=[("ac", a_c)], inc=("ve", 1))
            v_thr = pb.emit("vector", lambda e: e.tensor_reduce(
                SC3(5), wk2_f, AX, ALU.max),
                wait=[("ve", vprev)], inc=("ve", 1))
            a_thr = pb.emit("scalar", lambda e: (
                e.copy(SC3(7), SC3(5)), e.mul(SC3(9), SC3(8), -1.0))[-1],
                wait=[("ve", v_thr)], inc=("ac", 1))
            v_ge = pb.emit("vector", lambda e: e.tensor_scalar(
                msk3, ffn_sc, SC3(7), None, ALU.is_ge),
                wait=[("ac", a_thr)], inc=("ve", 1))
            a_ex2 = pb.emit("scalar", lambda e: e.activation(
                ex_f[:, 0:64], ffn_sc, AF.Exp, bias=SC3(9)),
                wait=[("ve", v_ge)], inc=("ac", 1))
            v_sel = pb.emit("vector", lambda e: (
                e.select(gate_f, msk3, ex_f[:, 0:64], zro64),
                e.tensor_reduce(SC3(10), gate_f, AX, ALU.add))[-1],
                wait=[("ac", a_ex2)], inc=("ve", 1))
            a_rc2 = pb.emit("scalar", lambda e: (
                e.activation(SC3(11), SC3(10), AF.Ln),
                e.activation(SC3(12), SC3(11), AF.Exp, scale=-1.0))[-1],
                wait=[("ve", v_sel)], inc=("ac", 1))
            v_gt = pb.emit("vector", lambda e: e.tensor_scalar(
                gate_f, gate_f, SC3(12), None, ALU.mult),
                wait=[("ac", a_rc2)], inc=("ve", 1))
            if i == 0:
                pb.dma("sync", d3C[:, 0:64], ffn_sc, sem="st2", wait=[("ve", v_gt)])
                pb.dma("sync", d3C[:, 64:128], gate_f, sem="st2")
                pb.dma("sync", d3C[:, 128:192], stat3, sem="st2")
            t_gT = pb.emit("tensor", lambda e: e.transpose(
                pG[0:64, 192:320], gate_f, ident_f),
                wait=[("ve", v_gt)], inc=("pe", 1))
            v_gTe = pb.emit("vector", lambda e: e.tensor_copy(
                gateT_bf[0:64, :], pG[0:64, 192:320]),
                wait=[("pe", t_gT)], inc=("ve", 1))
            t_ge = pb.emit("tensor", lambda e: [
                e.matmul(pE2[:, ts(j, 128)], eexp_sb[0:64, ts(j, 128)],
                         gateT_bf[0:64, :], start=True, stop=True,
                         skip_group_check=True) for j in range(4)][-1],
                wait=[("ve", v_gTe), ("ve", ffn_prev["ve"])], inc=("pe", 1))
            t_zT = pb.emit("tensor", lambda e, i=i: [
                e.matmul(pF2[:, ts(j, 128)], awT_sb[:, kt * 512 + j * 128:
                                                    kt * 512 + j * 128 + 128],
                         xn1T[:, kt * 1026 + 1 + i * 128:
                              kt * 1026 + 1 + i * 128 + 128],
                         start=(kt == 0), stop=(kt == 31), skip_group_check=True)
                for j in range(4) for kt in range(32)][-1],
                wait=[("trd", awt_trd), ("tr2", awt_tr2), ("ve", ffn_prev["ve"]),
                      ("ve", att_done[0])], inc=("pe", 1))
            v_ze = pb.emit("vector", lambda e: e.tensor_copy(zT_f, pF2[:, 0:512]),
                           wait=[("pe", t_zT), ("st", g1_prev[1])], inc=("ve", 1))
            v_zg = pb.emit("vector", lambda e: e.tensor_tensor(
                zgT_sb, zT_f, pE2[:, 0:512], ALU.mult),
                wait=[("ve", v_ze), ("pe", t_ge)], inc=("ve", 1))
            ffn_prev["ve"] = v_zg
            for oc in range(8):
                buf = oc % 2
                t_fo = pb.emit("tensor", lambda e, oc=oc: [
                    e.matmul(pG[:, 0:512],
                             zgT_sb[:, ts(j, 128)],
                             bwT_sb[:, j * 4096 + oc * 512:j * 4096 + oc * 512 + 512],
                             start=(j == 0), stop=(j == 3), skip_group_check=True)
                    for j in range(4)][-1],
                    wait=[("ve", v_zg), ("ac", ffn_prev["ac"])], inc=("pe", 1))
                a_g1 = pb.emit("scalar", lambda e, buf=buf: e.activation(
                    g1c[buf], pG[:, 0:512], AF.Silu, scale=2.0),
                    wait=[("pe", t_fo), ("st", g1_prev[buf])], inc=("ac", 1))
                ffn_prev["ac"] = a_g1
                g1_prev[buf] = pb.dma("sync", g1_dram[ts(i, 128), ts(oc, 512)],
                                      g1c[buf], sem="st", wait=[("ac", a_g1)])

        pe_after_ffn = pb.cnt["pe"]
        st_after_ffn = pb.cnt["st"]
        lpw = pb.dma("sync", pw_sb.rearrange("p (a n) -> p a n", a=8),
                     pw_bfd.rearrange("(a p) n -> p a n", p=128),
                     wait=[("pe", pe_after_ffn)])
        cmb_prev = {0: 0, 1: 0, "st": 0}
        for i in range(8):
            for oc in range(8):
                buf = oc % 2
                lg1 = pb.dma("sync", g1c[buf], g1_dram[ts(i, 128), ts(oc, 512)],
                             sem="ld2" if buf == 0 else "ld3",
                             wait=[("ve", cmb_prev[buf]), ("st", st_after_ffn)])
                lx1 = pb.dma("sync", x1c_f[buf], i_x1[ts(i, 128), ts(oc, 512)],
                             sem="trd" if buf == 0 else "tr2")
                ps = pE2 if buf == 0 else pF2
                t_pr = pb.emit("tensor", lambda e, i=i, oc=oc, ps=ps: [
                    e.matmul(ps[:, 0:512],
                             o2T_all[:, i * 1024 + kt * 128:i * 1024 + kt * 128 + 128],
                             pw_sb[:, kt * 4096 + oc * 512:kt * 4096 + oc * 512 + 512],
                             start=(kt == 0), stop=False, skip_group_check=True)
                    for kt in range(8)][-1],
                    wait=[("ld", lpw), ("ve", cmb_prev[buf]),
                          ("ac", o2_done[i])], inc=("pe", 1))
                t_prb = pb.emit("tensor", lambda e, oc=oc, ps=ps: e.matmul(
                    ps[:, 0:512], ones_bf[0:1, 0:128], pbrow[0:1, ts(oc, 512)],
                    start=False, stop=True, skip_group_check=True), inc=("pe", 1))
                v_c1 = pb.emit("vector", lambda e, buf=buf, ps=ps: e.tensor_tensor(
                    ps[:, 0:512], ps[:, 0:512], g1c[buf], ALU.mult),
                    wait=[("pe", t_prb), ("ld2" if buf == 0 else "ld3", lg1)],
                    inc=("ve", 1))
                v_c2 = pb.emit("vector", lambda e, buf=buf, ps=ps: e.tensor_tensor(
                    x2_f[buf], ps[:, 0:512], x1c_f[buf], ALU.add),
                    wait=[("trd" if buf == 0 else "tr2", lx1), ("ve", v_c1)],
                    inc=("ve", 1))
                stx = pb.dma("sync", o_x2[ts(i, 128), ts(oc, 512)], x2_f[buf],
                             sem="st", wait=[("ve", v_c2)])
                cmb_prev[buf] = v_c2
                cmb_prev["st"] = stx
        pb.finish()
        pb.run(block)
    return nc

def kernel(**inputs):
    x = np.asarray(inputs["L0_attn_hidden_states"], np.float32)[0]
    csts = make_consts()

    def rl(a):
        return np.ascontiguousarray(a)

    conv_b0 = rl(np.asarray(inputs["L0_attn_conv_b"], np.float32).reshape(80, 128).T)
    norm_l = rl(np.asarray(inputs["L0_attn_norm_w"], np.float32).reshape(64, 128).T)
    usuf = (np.arange(8)[:, None] > np.arange(8)[None, :]).astype(np.float32)

    im1s = []
    for k in range(8):
        x_main = rl(x[k * T:(k + 1) * T])
        halo3 = np.zeros((3, D), np.float32) if k == 0 else rl(x[k * T - 3:k * T])
        im = {
            "x_main": x_main, "x_halo3": halo3,
            "w_in": np.asarray(inputs["L0_attn_in_proj_w"], np.float32),
            "conv_w0": np.asarray(inputs["L0_attn_conv_w"], np.float32),
            "conv_b0": conv_b0,
            "dt_bias": np.asarray(inputs["L0_attn_dt_bias"], np.float32),
            "a_log": np.asarray(inputs["L0_attn_A_log"], np.float32),
            "d_vec": np.asarray(inputs["L0_attn_D"], np.float32),
            "ffn_w": np.asarray(inputs["L0_ffn_W"], np.float32),
            "ffn_b": np.asarray(inputs["L0_ffn_b"], np.float32),
            "ffn_a": np.asarray(inputs["L0_ffn_A"], np.float32),
            "limes": np.asarray(inputs["L0_ffn_LiMEs"], np.float32),
        }
        im.update(csts)
        im1s.append(im)
    nc1 = build_launch1(upto="all")
    rs1 = run_bass_kernel_spmd(nc1, im1s, core_ids=list(range(8))).results

    gF = np.concatenate([np.asarray(rs1[k]["o_f"], np.float32) for k in range(8)], 0)
    gA = np.concatenate([np.asarray(rs1[k]["o_asum"], np.float32) for k in range(8)], 0)
    gmax = np.concatenate([np.asarray(rs1[k]["o_locmax"], np.float32)
                           for k in range(8)], 0)
    im2s = []
    for k in range(8):
        cmask = (np.arange(8) < k).astype(np.float32)
        im = {
            "x_main": rl(x[k * T:(k + 1) * T]),
            "i_xn": np.asarray(rs1[k]["o_xn"]),
            "i_xbc": np.asarray(rs1[k]["o_xbc"]),
            "i_z": np.asarray(rs1[k]["o_z"]),
            "i_acs": np.asarray(rs1[k]["o_acs"]),
            "i_acst": np.asarray(rs1[k]["o_acst"]),
            "i_y1": np.asarray(rs1[k]["o_y1"]),
            "i_s": np.asarray(rs1[k]["o_s"]),
            "i_h": np.asarray(rs1[k]["o_h"]),
            "i_hst": np.asarray(rs1[k]["o_hst"]),
            "i_dst": np.asarray(rs1[k]["o_dst"]),
            "i_zffn": np.asarray(rs1[k]["o_zffn"]),
            "g_f": gF, "g_a": gA, "g_max": gmax, "cmask": cmask, "usuf": usuf,
            "wo_w": np.asarray(inputs["L0_attn_out_proj_w"], np.float32),
            "bmat": np.asarray(inputs["L0_ffn_Bmat"], np.float32),
            "limes": np.asarray(inputs["L0_ffn_LiMEs"], np.float32),
            "norm_l": norm_l,
            "c_eps6": np.full((1, 1), 1e-6, np.float32),
            "c_eps9": np.full((1, 1), 1e-9, np.float32),
        }
        im.update(csts)
        im2s.append(im)
    nc2 = build_launch2()
    rs2 = run_bass_kernel_spmd(nc2, im2s, core_ids=list(range(8))).results
    x1_full = np.concatenate([np.asarray(rs2[k]["b_x1"], np.float32)
                              for k in range(8)], 0)

    wq3 = rl(np.asarray(inputs["L1_attn_wq"], np.float32).reshape(1024, 1024))
    wk3 = rl(np.asarray(inputs["L1_attn_wk"], np.float32).reshape(1024, 1024))
    wv3 = rl(np.asarray(inputs["L1_attn_wv"], np.float32).reshape(1024, 1024))
    wo3 = rl(np.asarray(inputs["L1_attn_wo"], np.float32).reshape(1024, 1024))
    bq3c = rl(np.asarray(inputs["L1_attn_bq"], np.float32).reshape(1024)
              .reshape(8, 128).T)
    bk3c = rl(np.asarray(inputs["L1_attn_bk"], np.float32).reshape(1024)
              .reshape(8, 128).T)
    bv3 = rl(np.asarray(inputs["L1_attn_bv"], np.float32).reshape(1024))
    bo3c = rl(np.asarray(inputs["L1_attn_bo"], np.float32).reshape(8, 128).T)
    im3s = []
    for k in range(8):
        halo2 = np.zeros((2, D), np.float32)
        if k > 0:
            halo2[0] = x1_full[k * T - 1]
        if k < 7:
            halo2[1] = x1_full[(k + 1) * T]
        im = {
            "i_x1": rl(x1_full[k * T:(k + 1) * T]),
            "x1_halo2": halo2,
            "cw3": np.asarray(inputs["L1_attn_conv_w"], np.float32),
            "cb3": np.asarray(inputs["L1_attn_conv_b"], np.float32),
            "wq3": wq3, "bq3c": bq3c, "wk3": wk3, "bk3c": bk3c,
            "wv3": wv3, "bv3": bv3, "wo3": wo3, "bo3c": bo3c,
            "pw3": np.asarray(inputs["L1_attn_proj_w"], np.float32),
            "pb3": np.asarray(inputs["L1_attn_proj_b"], np.float32),
            "aw3": np.asarray(inputs["L1_ffn_A_w"], np.float32),
            "bw3": np.asarray(inputs["L1_ffn_B_w"], np.float32),
            "wqr3": np.asarray(inputs["L1_ffn_Wq_w"], np.float32),
            "keys3": np.asarray(inputs["L1_ffn_keys"], np.float32),
        }
        im.update(csts)
        im3s.append(im)
    nc3 = build_launch3()
    rs3 = run_bass_kernel_spmd(nc3, im3s, core_ids=list(range(8))).results
    out = np.concatenate([np.asarray(rs3[k]["o_x2"], np.float32) for k in range(8)], 0)
    return out[None, :, :]
